# version 27
# baseline (speedup 1.0000x reference)
import numpy as np
import ml_dtypes
import concourse.bass as bass
import concourse.mybir as mybir
from concourse.bass_utils import run_bass_kernel_spmd

F32 = mybir.dt.float32
BF16 = mybir.dt.bfloat16
I32 = mybir.dt.int32
U8 = mybir.dt.uint8
I8 = mybir.dt.int8
AF = mybir.ActivationFunctionType
ALU = mybir.AluOpType
AX = mybir.AxisListType

P = 128
TOK = 1024
HALO = 4
COLS = TOK + HALO
DM = 2048
NDB = DM // P
DIN = 4096
NH = 64
HD = 64
DST = 128
DCONV = 4
CH = 128
NCH = TOK // CH
DIP = 8512
DIPP = 67 * 128
CONVD = 4352
NCB = 34
FI = 8192
NIB = 64
AH = 16
SELF_SYNC = True
DBG = False

COMPUTE = ("te", "ve", "se", "ge")
STREAMS = ("te", "ve", "se", "ge", "sy")


class Sched:

    def __init__(self):
        self.ops = []
        self.wr = {}
        self.rd = {}
        self.PAGE = 4096

    def _pages(self, lo, hi):
        return range(lo // self.PAGE, (max(hi - 1, lo)) // self.PAGE + 1)

    PSUM_BANK = {"PSA": 512, "PSB": 512, "PSC": 512, "PTA": 1024, "PTB": 1024}

    def _bankify(self, ranges):
        out = []
        for (b, lo, hi) in ranges:
            bk = self.PSUM_BANK.get(b)
            if bk:
                lo = (lo // bk) * bk
                hi = ((hi + bk - 1) // bk) * bk
            out.append((b, lo, hi))
        return out

    def add(self, stream, fn, reads=(), writes=(), dma=False, tag="", dkey=None):
        reads = self._bankify(reads)
        writes = self._bankify(writes)
        opi = len(self.ops)
        deps = set()
        for (b, lo, hi) in reads:
            m = self.wr.get(b)
            if m:
                for pg in self._pages(lo, hi):
                    for (l2, h2, oi) in m.get(pg, ()):
                        if l2 < hi and lo < h2:
                            deps.add(oi)
        for (b, lo, hi) in writes:
            m = self.wr.get(b)
            if m:
                for pg in self._pages(lo, hi):
                    for (l2, h2, oi) in m.get(pg, ()):
                        if l2 < hi and lo < h2:
                            deps.add(oi)
            m = self.rd.get(b)
            if m:
                for pg in self._pages(lo, hi):
                    for (l2, h2, oi) in m.get(pg, ()):
                        if l2 < hi and lo < h2:
                            deps.add(oi)
        for (b, lo, hi) in writes:
            wm = self.wr.setdefault(b, {})
            rm = self.rd.setdefault(b, {})
            for pg in self._pages(lo, hi):
                lst = wm.setdefault(pg, [])
                lst[:] = [e for e in lst if not (lo <= e[0] and e[1] <= hi)]
                lst.append((lo, hi, opi))
                rl = rm.get(pg)
                if rl:
                    rl[:] = [e for e in rl if not (lo <= e[0] and e[1] <= hi)]
        for (b, lo, hi) in reads:
            rm = self.rd.setdefault(b, {})
            for pg in self._pages(lo, hi):
                rm.setdefault(pg, []).append((lo, hi, opi))
        if dma and dkey is None:
            for (b, lo, hi) in list(writes) + list(reads):
                if b == "sbuf" or isinstance(b, str) and b.startswith("PS") or b in ("PTA", "PTB"):
                    dkey = ("a", b, lo)
                    break
            if dkey is None:
                dkey = ("a", "none", opi)
        elif dma:
            dkey = ("b", dkey)
        self.ops.append(dict(stream=stream, fn=fn, deps=deps, dma=dma,
                             tag=tag, inc=False, cnt=0, dkey=dkey))
        return opi

    def emit(self, nc, block):
        ops = self.ops
        for op in ops:
            kept = set()
            for d in op["deps"]:
                dop = ops[d]
                if dop["dma"]:
                    kept.add(d)
                elif dop["stream"] != op["stream"]:
                    kept.add(d)
                elif SELF_SYNC and dop["stream"] != "te":
                    kept.add(d)
            op["deps"] = kept
            for d in kept:
                ops[d]["inc"] = True
        dk_cnt = {}
        dk_positions = {}
        cs_cnt = {s: 0 for s in STREAMS}
        for opi, op in enumerate(ops):
            if op["dma"]:
                k = op["dkey"]
                dk_cnt[k] = dk_cnt.get(k, 0) + 1
                op["cnt"] = dk_cnt[k] * 16
                dk_positions.setdefault(k, []).append(opi)
            elif op["inc"]:
                cs_cnt[op["stream"]] += 1
                op["cnt"] = cs_cnt[op["stream"]]
        sems = {}
        for s in STREAMS:
            sems[("c", s)] = nc.alloc_semaphore(f"sem_c_{s}")
        for ki, k in enumerate(dk_cnt):
            sems[k] = nc.alloc_semaphore(f"sem_d{ki}")
        n_waits = [0]

        import bisect

        def body_for(name):
            def body(e):
                water = {}
                for opi, op in enumerate(ops):
                    if op["stream"] != name:
                        continue
                    needed = {}
                    for d in op["deps"]:
                        dop = ops[d]
                        if dop["dma"]:
                            key = dop["dkey"]
                            if key[0] == "b":
                                v = 16 * bisect.bisect_right(dk_positions[key], opi)
                            else:
                                v = dop["cnt"]
                        else:
                            key = ("c", dop["stream"])
                            v = dop["cnt"]
                        if v > needed.get(key, 0):
                            needed[key] = v
                    for key, v in sorted(needed.items(), key=str):
                        if v > water.get(key, 0):
                            e.wait_ge(sems[key], v)
                            water[key] = v
                            n_waits[0] += 1
                    inst = op["fn"](e)
                    if op["dma"]:
                        inst.then_inc(sems[op["dkey"]], 16)
                    elif op["inc"]:
                        inst.then_inc(sems[("c", op["stream"])], 1)
                if name == "sy":
                    for k, c in dk_cnt.items():
                        e.wait_ge(sems[k], 16 * c)
            return body

        block.sync(body_for("sy"))
        block.vector(body_for("ve"))
        block.scalar(body_for("se"))
        block.tensor(body_for("te"))
        block.gpsimd(body_for("ge"))
        return n_waits[0]


class Regions:

    def __init__(self, nbytes):
        self.nbytes = nbytes
        self.off = 0
        self.peak = 0
        self.marks = []

    def alloc(self, n, align=512):
        o = (self.off + align - 1) // align * align
        self.off = o + n
        self.peak = max(self.peak, self.off)
        assert self.off <= self.nbytes, f"SBUF arena overflow: {self.off} > {self.nbytes}"
        return o

    def push(self):
        self.marks.append(self.off)

    def pop(self):
        self.off = self.marks.pop()


def build_program():
    nc = bass.Bass("TRN2", target_bir_lowering=False)
    sch = Sched()

    def din(name, shape, dtype):
        return nc.dram_tensor(name, shape, dtype, kind="ExternalInput").ap()

    xin = din("xin", [DM, COLS], F32)
    win = din("win", [67, 16, P, P], F32)
    convw = din("convw", [P, NCB * DCONV], F32)
    convb = din("convb", [P, NCB], F32)
    avec = din("avec", [NH, 1], F32)
    dtbv = din("dtbv", [NH, 1], F32)
    dcol = din("dcol", [P, 32], F32)
    drow = din("drow", [1, NH], F32)
    nwc = din("nwc", [P, 32], F32)
    wout = din("wout", [16, 32, P, P], F32)
    qw = din("qw", [16, 16, P, P], F32)
    kw = din("kw", [16, 16, P, P], F32)
    vw = din("vw", [16, 16, P, P], F32)
    ow = din("ow", [16, 16, P, P], F32)
    qb = din("qb", [P, 16], F32)
    kb = din("kb", [P, 16], F32)
    vb = din("vb", [P, 16], F32)
    ob = din("ob", [P, 16], F32)
    gw = din("gw", [NIB, 16, P, P], I8)
    uw = din("uw", [NIB, 16, P, P], I8)
    dw = din("dw", [4, 16, 16, P, P], I8)
    gsc = din("gsc", [P, NIB], F32)
    usc = din("usc", [P, NIB], F32)
    dsc = din("dsc", [P, 16], F32)
    mmask = din("mmask", [4, 4], F32)
    cmask = din("cmask", [4, 1], F32)

    out = nc.dram_tensor("out", [DM, TOK], F32, kind="ExternalOutput").ap()

    kind_s = "ExternalOutput" if DBG else "Internal"
    xbc_s = nc.dram_tensor("xbc_s", [NCB, P, TOK], BF16, kind=kind_s).ap()
    z_s = nc.dram_tensor("z_s", [32, P, TOK], BF16, kind=kind_s).ap()
    x1_s = nc.dram_tensor("x1_s", [16, P, TOK], F32, kind=kind_s).ap()
    xt_s = nc.dram_tensor("xt_s", [NCH, P, DIN], BF16, kind="Internal").ap()
    x2_s = nc.dram_tensor("x2_s", [16, P, TOK], F32, kind=kind_s).ap()
    ag_in = nc.dram_tensor("ag_in", [P, 4224], BF16, kind="Internal").ap()
    ag_out = nc.dram_tensor("ag_out", [4 * P, 4224], BF16, kind="Internal").ap()
    co_d = nc.dram_tensor("co_d", [4, NH], F32, kind="Internal").ap()
    ar_in = nc.dram_tensor("ar_in", [P, 2064], F32, kind="Internal").ap()
    ar_out = nc.dram_tensor("ar_out", [P, 2064], F32, kind="Internal").ap()
    if DBG:
        dbg_dt = nc.dram_tensor("dbg_dt", [NH, TOK], F32, kind="ExternalOutput").ap()
        dbg_y = nc.dram_tensor("dbg_y", [P, 4096 * NCH], BF16, kind="ExternalOutput").ap()
        dbg_xt = nc.dram_tensor("dbg_xt", [P, 4096], BF16, kind="ExternalOutput").ap()
        dbg_m = nc.dram_tensor("dbg_m", [P, 8192], BF16, kind="ExternalOutput").ap()
        dbg_sin = nc.dram_tensor("dbg_sin", [P, 4096 * NCH], BF16, kind="ExternalOutput").ap()
        dbg_slc = nc.dram_tensor("dbg_slc", [P, 4096 * 2], F32, kind="ExternalOutput").ap()
        dbg_g = nc.dram_tensor("dbg_g", [P, 4096], BF16, kind="ExternalOutput").ap()
        dbg_yn = nc.dram_tensor("dbg_yn", [P, 32 * 512], BF16, kind="ExternalOutput").ap()
        dbg_csc = nc.dram_tensor("dbg_csc", [P, 8192], BF16, kind="ExternalOutput").ap()
        dbg_exc = nc.dram_tensor("dbg_exc", [NH, TOK], BF16, kind="ExternalOutput").ap()
        dbg_xc8 = nc.dram_tensor("dbg_xc8", [NH, TOK], F32, kind="ExternalOutput").ap()

    RG = [[0, 1, 2, 3], [4, 5, 6, 7]]

    SB = "sbuf"

    ARENA_BYTES = 212480
    rg = Regions(ARENA_BYTES)

    with (
        nc.sbuf_tensor("AR", [P, ARENA_BYTES // 4], F32) as AR,
        nc.psum_tensor("PSA", [P, 1536], F32) as PSA,
        nc.psum_tensor("PSB", [P, 1024], F32) as PSB,
        nc.psum_tensor("PSC", [P, 512], F32) as PSC,
        nc.psum_tensor("PTA", [P, 1024], BF16) as PTA,
        nc.psum_tensor("PTB", [P, 1024], BF16) as PTB,
        nc.Block() as block,
    ):
        arf = AR[:].bitcast(BF16)

        def f32(off, n, p0=0, pn=P):
            assert off % 4 == 0
            return AR[p0:p0 + pn, off // 4: off // 4 + n]

        def b16(off, n, p0=0, pn=P):
            assert off % 2 == 0
            return arf[p0:p0 + pn, off // 2: off // 2 + n]

        def R(off, nbytes):
            return (SB, off, off + nbytes)

        o_id128b = rg.alloc(256)
        o_idf = rg.alloc(512)
        o_id64b = rg.alloc(128)
        o_mask = rg.alloc(1024)
        o_ones1 = rg.alloc(64)
        o_onesr = rg.alloc(256)
        o_onesrf = rg.alloc(512)
        o_onesf = rg.alloc(512)
        o_m128 = rg.alloc(256)
        o_convw = rg.alloc(NCB * DCONV * 4)
        o_convb = rg.alloc(NCB * 4)
        o_avec = rg.alloc(4)
        o_dtb = rg.alloc(4)
        o_dcol = rg.alloc(128)
        o_nwc = rg.alloc(128)
        o_qb = rg.alloc(64); o_kb = rg.alloc(64); o_vb = rg.alloc(64); o_obi = rg.alloc(64)
        o_gsc = rg.alloc(256); o_usc = rg.alloc(256); o_dsc = rg.alloc(64)
        o_dt = rg.alloc(TOK * 4)
        o_mmask = rg.alloc(16)
        o_cmask = rg.alloc(4)
        o_c1 = rg.alloc(4)
        o_ceps = rg.alloc(4)
        o_zcol = rg.alloc(64)
        o_drow = rg.alloc(256)
        o_dt64 = rg.alloc(128)

        ID128b = b16(o_id128b, 128)
        IDF = f32(o_idf, 128)
        ID64b = b16(o_id64b, 64, 0, 64)
        MASKR = b16(o_mask, 512)
        ONESC = b16(o_ones1, 1)
        ONESR = b16(o_onesr, 128, 0, 1)
        ONESRF = f32(o_onesrf, 128, 0, 1)
        ONESF = f32(o_onesf, 128, 0, 64)
        M128R = b16(o_m128, 128, 0, 1)
        CVW = f32(o_convw, NCB * DCONV)
        CVB = f32(o_convb, NCB)
        AC = f32(o_avec, 1, 0, NH)
        DTB = f32(o_dtb, 1, 0, NH)
        DCOL = f32(o_dcol, 32)
        NWC = f32(o_nwc, 32)
        QBc = f32(o_qb, 16); KBc = f32(o_kb, 16); VBc = f32(o_vb, 16); OBc = f32(o_obi, 16)
        GSC = f32(o_gsc, NIB); USC = f32(o_usc, NIB); DSC = f32(o_dsc, 16)
        DTS = f32(o_dt, TOK, 0, NH)
        MMK = f32(o_mmask, 4, 0, 4)
        CMK = f32(o_cmask, 1, 0, 4)
        C1 = f32(o_c1, 1)
        CEPS = f32(o_ceps, 1)
        ZCOL = b16(o_zcol, 1)
        DRW = f32(o_drow, NH, 0, 1)
        DT64B = b16(o_dt64, NH)
        RC1 = R(o_c1, 4)
        RCEPS = R(o_ceps, 4)

        PS = {"PSA": PSA, "PSB": PSB, "PSC": PSC, "PTA": PTA, "PTB": PTB}

        def pr(name, lo, hi):
            return (name, lo, hi)

        def load_const(dst_off, src_ap, nelem, dtype_bytes=4, p0=0, pn=P, name="c"):
            dst = f32(dst_off, nelem, p0, pn) if dtype_bytes == 4 else b16(dst_off, nelem, p0, pn)
            sch.add("se", lambda e, d=dst, s=src_ap: e.dma_start(out=d, in_=s),
                    reads=[], writes=[R(dst_off, nelem * dtype_bytes)], dma=True, tag=name,
                    dkey="const")

        load_const(o_convw, convw, NCB * DCONV)
        load_const(o_convb, convb, NCB)
        load_const(o_avec, avec, 1, p0=0, pn=NH)
        load_const(o_dtb, dtbv, 1, p0=0, pn=NH)
        load_const(o_dcol, dcol, 32)
        load_const(o_drow, drow, NH, p0=0, pn=1)
        load_const(o_nwc, nwc, 32)
        load_const(o_qb, qb, 16); load_const(o_kb, kb, 16)
        load_const(o_vb, vb, 16); load_const(o_obi, ob, 16)
        load_const(o_gsc, gsc, NIB); load_const(o_usc, usc, NIB)
        load_const(o_dsc, dsc, 16)
        load_const(o_mmask, mmask, 4, p0=0, pn=4)
        load_const(o_cmask, cmask, 1, p0=0, pn=4)

        def ge_init(fn, wr_rng, rd_rng=(), tag="init"):
            sch.add("ge", fn, reads=list(rd_rng), writes=[wr_rng], tag=tag)

        ge_init(lambda e: e.memset(ID128b, 1.0), R(o_id128b, 256))
        ge_init(lambda e: e.affine_select(out=ID128b, in_=ID128b, compare_op=ALU.is_equal,
                                          fill=0.0, base=0, pattern=[[-1, 128]], channel_multiplier=1),
                R(o_id128b, 256), [R(o_id128b, 256)])
        ge_init(lambda e: e.memset(IDF, 1.0), R(o_idf, 512))
        ge_init(lambda e: e.affine_select(out=IDF, in_=IDF, compare_op=ALU.is_equal,
                                          fill=0.0, base=0, pattern=[[-1, 128]], channel_multiplier=1),
                R(o_idf, 512), [R(o_idf, 512)])
        ge_init(lambda e: e.memset(ID64b, 1.0), R(o_id64b, 128))
        ge_init(lambda e: e.affine_select(out=ID64b, in_=ID64b, compare_op=ALU.is_equal,
                                          fill=0.0, base=0, pattern=[[-1, 64]], channel_multiplier=1),
                R(o_id64b, 128), [R(o_id64b, 128)])
        ge_init(lambda e: e.memset(MASKR, 0.0), R(o_mask, 1024))
        ge_init(lambda e: e.affine_select(
            out=MASKR.rearrange("p (r l) -> p r l", r=4), in_=MASKR.rearrange("p (r l) -> p r l", r=4),
            compare_op=ALU.is_ge, fill=-1e30, base=0, pattern=[[0, 4], [1, 128]],
            channel_multiplier=-1), R(o_mask, 1024), [R(o_mask, 1024)])
        ge_init(lambda e: e.memset(ONESC, 1.0), R(o_ones1, 64))
        ge_init(lambda e: e.memset(ONESR, 1.0), R(o_onesr, 256))
        ge_init(lambda e: e.memset(ONESRF, 1.0), R(o_onesrf, 512))
        ge_init(lambda e: e.memset(ONESF, 1.0), R(o_onesf, 512))
        ge_init(lambda e: e.memset(M128R, -128.0), R(o_m128, 256))
        ge_init(lambda e: e.memset(C1, 1.0), RC1)
        ge_init(lambda e: e.memset(ZCOL, 0.0), R(o_zcol, 64))
        sch.add("te", lambda e: e.matmul(PSC[:, 0:NH], ONESRF, DRW, start=True, stop=True),
                reads=[R(o_drow, 256), R(o_onesrf, 512)], writes=[pr("PSC", 0, NH)], tag="dt64m")
        sch.add("ve", lambda e: e.tensor_copy(DT64B, PSC[:, 0:NH]),
                reads=[pr("PSC", 0, NH)], writes=[R(o_dt64, 128)], tag="dt64e")
        ge_init(lambda e: e.memset(CEPS, 1e-5), RCEPS)

        def layernorm(xsrc_dram, o_xb, cols, xdram_key, dram_colspan=None):
            rg.push()
            o_tmp = rg.alloc(cols * 4)
            o_tmp2 = rg.alloc(cols * 4)
            o_sq = rg.alloc(cols * 2)
            o_rows = rg.alloc(6 * ((cols * 4 + 511) // 512) * 512)
            row_sz = (cols * 4 + 511) // 512 * 512
            o_mb = rg.alloc(cols * 4)
            o_rb = rg.alloc(cols * 4)
            o_tl = rg.alloc(cols * 4)

            ntile = [(t, min(512, cols - t)) for t in range(0, cols, 512)]
            for b in range(NDB):
                o_x = o_tmp if b % 2 == 0 else o_tmp2
                xt = f32(o_x, cols)
                sch.add("sy", lambda e, d=xt, s=xsrc_dram(b): e.dma_start(out=d, in_=s),
                        writes=[R(o_x, cols * 4)], reads=[(xdram_key, b, b + 1)], dma=True, tag="lnld")
                xbb = b16(o_xb + b * cols * 2, cols)
                sch.add("ve", lambda e, d=xbb, s=xt: e.tensor_copy(d, s),
                        reads=[R(o_x, cols * 4)], writes=[R(o_xb + b * cols * 2, cols * 2)], tag="lncp")
                sqb = b16(o_sq, cols)
                sch.add("se", lambda e, d=sqb, s=xbb: e.activation(d, s, AF.Square),
                        reads=[R(o_xb + b * cols * 2, cols * 2)], writes=[R(o_sq, cols * 2)], tag="lnsq")
                for (t0, tn) in ntile:
                    sch.add("te", lambda e, t0=t0, tn=tn, b=b, s=xbb: e.matmul(
                        PSA[0:1, t0:t0 + tn], ONESC, s[:, t0:t0 + tn],
                        start=(b == 0), stop=(b == NDB - 1)),
                        reads=[R(o_xb + b * cols * 2, cols * 2), R(o_ones1, 64)],
                        writes=[pr("PSA", t0, t0 + tn)], tag="lnsum")
                    sch.add("te", lambda e, t0=t0, tn=tn, b=b, s=sqb: e.matmul(
                        PSB[0:1, t0:t0 + tn] if t0 < 1024 else PSC[0:1, t0 - 1024:t0 - 1024 + tn],
                        ONESC, s[:, t0:t0 + tn],
                        start=(b == 0), stop=(b == NDB - 1)),
                        reads=[R(o_sq, cols * 2), R(o_ones1, 64)],
                        writes=[pr("PSB", t0, t0 + tn) if t0 < 1024 else pr("PSC", t0 - 1024, t0 - 1024 + tn)],
                        tag="lnsq2")
            MROW = f32(o_rows, cols, 0, 1)
            VROW = f32(o_rows + row_sz, cols, 0, 1)
            RROW = f32(o_rows + 2 * row_sz, cols, 0, 1)
            rows_rng = R(o_rows, 6 * row_sz)
            sch.add("ve", lambda e: e.tensor_scalar_mul(MROW, PSA[0:1, 0:cols], 1.0 / DM),
                    reads=[pr("PSA", 0, cols)], writes=[rows_rng], tag="lnmean")
            if cols <= 1024:
                sch.add("ve", lambda e: e.tensor_scalar_mul(VROW, PSB[0:1, 0:cols], 1.0 / DM),
                        reads=[pr("PSB", 0, cols)], writes=[rows_rng], tag="lnvar")
            else:
                sch.add("ve", lambda e: e.tensor_scalar_mul(VROW[0:1, 0:1024], PSB[0:1, 0:1024], 1.0 / DM),
                        reads=[pr("PSB", 0, 1024)], writes=[rows_rng], tag="lnvar")
                sch.add("ve", lambda e: e.tensor_scalar_mul(VROW[0:1, 1024:cols], PSC[0:1, 0:cols - 1024], 1.0 / DM),
                        reads=[pr("PSC", 0, cols - 1024)], writes=[rows_rng], tag="lnvar2")
            sch.add("ve", lambda e: e.scalar_tensor_tensor(RROW, MROW, -1.0, MROW, ALU.mult, ALU.mult),
                    reads=[rows_rng], writes=[rows_rng], tag="lnm2")
            sch.add("ve", lambda e: e.tensor_add(VROW, VROW, RROW),
                    reads=[rows_rng], writes=[rows_rng], tag="lnv2")
            sch.add("se", lambda e: e.activation(RROW, VROW, AF.Ln, bias=CEPS[0:1, :]),
                    reads=[rows_rng, RCEPS], writes=[rows_rng], tag="lnln")
            sch.add("se", lambda e: e.activation(RROW, RROW, AF.Exp, scale=-0.5),
                    reads=[rows_rng], writes=[rows_rng], tag="lnexp")
            for (t0, tn) in ntile:
                sch.add("te", lambda e, t0=t0, tn=tn: e.matmul(
                    PSA[:, t0:t0 + tn], ONESRF, MROW[0:1, t0:t0 + tn], start=True, stop=True),
                    reads=[rows_rng, R(o_onesrf, 512)], writes=[pr("PSA", t0, t0 + tn)], tag="lnbc1")
                sch.add("te", lambda e, t0=t0, tn=tn: e.matmul(
                    PSB[:, t0:t0 + tn] if t0 < 1024 else PSC[:, t0 - 1024:t0 - 1024 + tn],
                    ONESRF, RROW[0:1, t0:t0 + tn], start=True, stop=True),
                    reads=[rows_rng, R(o_onesrf, 512)],
                    writes=[pr("PSB", t0, t0 + tn) if t0 < 1024 else pr("PSC", t0 - 1024, t0 - 1024 + tn)],
                    tag="lnbc2")
            MB = f32(o_mb, cols)
            RB = f32(o_rb, cols)
            sch.add("ve", lambda e: e.tensor_copy(MB, PSA[:, 0:cols]),
                    reads=[pr("PSA", 0, cols)], writes=[R(o_mb, cols * 4)], tag="lnmbe")
            if cols <= 1024:
                sch.add("ve", lambda e: e.tensor_copy(RB, PSB[:, 0:cols]),
                        reads=[pr("PSB", 0, cols)], writes=[R(o_rb, cols * 4)], tag="lnrbe")
            else:
                sch.add("ve", lambda e: e.tensor_copy(RB[:, 0:1024], PSB[:, 0:1024]),
                        reads=[pr("PSB", 0, 1024)], writes=[R(o_rb, cols * 4)], tag="lnrbe")
                sch.add("ve", lambda e: e.tensor_copy(RB[:, 1024:cols], PSC[:, 0:cols - 1024]),
                        reads=[pr("PSC", 0, cols - 1024)], writes=[R(o_rb, cols * 4)], tag="lnrbe2")
            TL = f32(o_tl, cols)
            for b in range(NDB):
                o_x = o_tmp if b % 2 == 0 else o_tmp2
                xt = f32(o_x, cols)
                sch.add("sy", lambda e, d=xt, s=xsrc_dram(b): e.dma_start(out=d, in_=s),
                        writes=[R(o_x, cols * 4)], reads=[(xdram_key, b, b + 1)], dma=True, tag="lnld2")
                xbb = b16(o_xb + b * cols * 2, cols)
                sch.add("ve", lambda e, s=xt: e.tensor_sub(TL, s, MB),
                        reads=[R(o_x, cols * 4), R(o_mb, cols * 4)],
                        writes=[R(o_tl, cols * 4)], tag="lnn1")
                sch.add("ve", lambda e, d=xbb: e.tensor_mul(d, TL, RB),
                        reads=[R(o_tl, cols * 4), R(o_rb, cols * 4)],
                        writes=[R(o_xb + b * cols * 2, cols * 2)], tag="lnn2")
            rg.pop()

        rg.push()
        o_xn = rg.alloc(NDB * COLS * 2)
        layernorm(lambda b: xin[b * P:(b + 1) * P, :], o_xn, COLS, "xin")

        rg.push()
        o_wib = [rg.alloc(16 * P * 2) for _ in range(2)]
        o_xcb = [rg.alloc(COLS * 2) for _ in range(2)]
        o_cvo = [rg.alloc(TOK * 2) for _ in range(2)]
        o_dg = [rg.alloc(DCONV * P * 2) for _ in range(2)]
        o_zev = [rg.alloc(TOK * 2) for _ in range(2)]
        o_dtr = rg.alloc(TOK * 4)
        o_dtt = rg.alloc(TOK * 4)
        o_xts = [rg.alloc(1024 * 2) for _ in range(2)]

        for mb in range(67):
            slot = mb % 2
            WIB = b16(o_wib[slot], 16 * P)
            sch.add("ge", lambda e, d=WIB, mb=mb: e.dma_start(
                    out=d.rearrange("p (kb m) -> p kb m", kb=16), in_=win[mb].transpose([1, 0, 2])),
                    writes=[R(o_wib[slot], 16 * P * 2)], reads=[("win", mb, mb + 1)],
                    dma=True, tag="wld")
            is_z = mb < 32
            is_xbc = 32 <= mb < 66
            is_dt = mb == 66
            if is_xbc:
                tiles = [(0, 512), (512, 512), (1024, 4)]
            else:
                tiles = [(4, 512), (516, 512)]
            for kbi in range(16):
                xnb = b16(o_xn + kbi * COLS * 2, COLS)
                for ti, (t0, tn) in enumerate(tiles):
                    po = [0, 512, 1024][ti]
                    sch.add("te", lambda e, po=po, tn=tn, t0=t0, kbi=kbi, W=WIB, s=xnb: e.matmul(
                        PSA[:, po:po + tn], W[:, kbi * P:(kbi + 1) * P], s[:, t0:t0 + tn],
                        start=(kbi == 0), stop=(kbi == 15)),
                        reads=[R(o_wib[slot], 16 * P * 2), R(o_xn + kbi * COLS * 2, COLS * 2)],
                        writes=[pr("PSA", po, po + tn)], tag="ipmm")
            if is_z:
                ZEV = b16(o_zev[slot], TOK)
                sch.add("ve", lambda e, d=ZEV: e.tensor_copy(d, PSA[:, 0:TOK]),
                        reads=[pr("PSA", 0, 1024)], writes=[R(o_zev[slot], TOK * 2)], tag="zev")
                sch.add("sy", lambda e, s=ZEV, mb=mb: e.dma_start(out=z_s[mb], in_=s),
                        reads=[R(o_zev[slot], TOK * 2)], writes=[("z_s", mb, mb + 1)],
                        dma=True, tag="zst")
            elif is_dt:
                DTR = f32(o_dtr, TOK, 0, NH)
                sch.add("ve", lambda e: e.tensor_copy(DTR, PSA[0:NH, 0:TOK]),
                        reads=[pr("PSA", 0, 1024)], writes=[R(o_dtr, TOK * 4)], tag="dtev")
                DTT = f32(o_dtt, TOK, 0, NH)
                sch.add("ve", lambda e: e.tensor_scalar_add(DTT, DTR, DTB),
                        reads=[R(o_dtr, TOK * 4), R(o_dtb, 4)], writes=[R(o_dtt, TOK * 4)], tag="sp1")
                sch.add("ve", lambda e: e.scalar_tensor_tensor(DTT, DTT, -1.0, DTT, ALU.mult, ALU.min),
                        reads=[R(o_dtt, TOK * 4)], writes=[R(o_dtt, TOK * 4)], tag="sp1b")
                sch.add("se", lambda e: e.activation(DTT, DTT, AF.Exp),
                        reads=[R(o_dtt, TOK * 4)], writes=[R(o_dtt, TOK * 4)], tag="sp2")
                sch.add("se", lambda e: e.activation(DTT, DTT, AF.Ln, bias=f32(o_c1, 1, 0, NH)),
                        reads=[R(o_dtt, TOK * 4), RC1], writes=[R(o_dtt, TOK * 4)], tag="sp3")
                sch.add("ve", lambda e: e.tensor_scalar(DTS, DTR, DTB, 0.0, ALU.add, ALU.max),
                        reads=[R(o_dtr, TOK * 4), R(o_dtb, 4)], writes=[R(o_dt, TOK * 4)], tag="sp4")
                sch.add("ve", lambda e: e.tensor_add(DTS, DTS, DTT),
                        reads=[R(o_dt, TOK * 4), R(o_dtt, TOK * 4)], writes=[R(o_dt, TOK * 4)], tag="sp5")
                if DBG:
                    sch.add("sy", lambda e: e.dma_start(out=dbg_dt, in_=DTS),
                            reads=[R(o_dt, TOK * 4)], writes=[("dbg", 0, 1)], dma=True, tag="dbg")
            else:
                cb = mb - 32
                XCB = b16(o_xcb[slot], COLS)
                sch.add("ve", lambda e, d=XCB: e.tensor_copy(d, PSA[:, 0:COLS]),
                        reads=[pr("PSA", 0, COLS)], writes=[R(o_xcb[slot], COLS * 2)], tag="xcev")
                DG = b16(o_dg[slot], DCONV * P)
                for k in range(DCONV):
                    dgk = b16(o_dg[slot] + k * P * 2, P)
                    sch.add("ge", lambda e, d=dgk, cb=cb, k=k: e.tensor_copy(
                        d, CVW[:, cb * DCONV + k:cb * DCONV + k + 1].broadcast_to([P, P])),
                        reads=[R(o_convw, NCB * DCONV * 4)],
                        writes=[R(o_dg[slot] + k * P * 2, P * 2)], tag="dgc")
                    sch.add("ge", lambda e, d=dgk: e.affine_select(
                        out=d, in_=d, compare_op=ALU.is_equal, fill=0.0, base=0,
                        pattern=[[-1, P]], channel_multiplier=1),
                        reads=[R(o_dg[slot] + k * P * 2, P * 2)],
                        writes=[R(o_dg[slot] + k * P * 2, P * 2)], tag="dgs")
                for k in range(DCONV):
                    dgk = b16(o_dg[slot] + k * P * 2, P)
                    for a in (0, 512):
                        sch.add("te", lambda e, a=a, k=k, d=dgk, s=XCB: e.matmul(
                            PSB[:, a:a + 512], d, s[:, a + 1 + k:a + 513 + k],
                            start=(k == 0), stop=(k == DCONV - 1)),
                            reads=[R(o_dg[slot] + k * P * 2, P * 2), R(o_xcb[slot], COLS * 2)],
                            writes=[pr("PSB", a, a + 512)], tag="cvmm")
                CVO = b16(o_cvo[slot], TOK)
                sch.add("se", lambda e, d=CVO, cb=cb: e.activation(
                    d, PSB[:, 0:TOK], AF.Silu, bias=CVB[:, cb:cb + 1]),
                    reads=[pr("PSB", 0, 1024), R(o_convb, NCB * 4)],
                    writes=[R(o_cvo[slot], TOK * 2)], tag="cvsl")
                sch.add("sy", lambda e, s=CVO, cb=cb: e.dma_start(out=xbc_s[cb], in_=s),
                        reads=[R(o_cvo[slot], TOK * 2)], writes=[("xbc_s", cb, cb + 1)],
                        dma=True, tag="cvst")
                if cb < 32:
                    for j in range(NCH):
                        sch.add("te", lambda e, j=j, s=CVO: e.transpose(
                            PTA[:, j * P:(j + 1) * P], s[:, j * P:(j + 1) * P], ID128b),
                            reads=[R(o_cvo[slot], TOK * 2), R(o_id128b, 256)],
                            writes=[pr("PTA", j * P, (j + 1) * P)], tag="bxt")
                    XTS = b16(o_xts[slot], 1024)
                    sch.add("ve", lambda e, d=XTS: e.tensor_copy(d, PTA[:, 0:1024]),
                            reads=[pr("PTA", 0, 1024)], writes=[R(o_xts[slot], 2048)], tag="bxe")
                    sch.add("sy", lambda e, s=XTS, cb=cb: e.dma_start(
                        out=xt_s[0:NCH, :, cb * P:(cb + 1) * P].transpose([1, 0, 2]),
                        in_=s.rearrange("p (c t) -> p c t", c=NCH)),
                        reads=[R(o_xts[slot], 2048)], writes=[("xt_s", cb, cb + 1)],
                        dma=True, tag="bxst")
        rg.pop()
        rg.pop()

        o_xscs = rg.alloc(2 * CH * 2)
        o_xscs2 = rg.alloc(2 * CH * 2)
        o_xt = rg.alloc(DIN * 2)
        o_xt2 = rg.alloc(DIN * 2)
        o_xtd = rg.alloc(DIN * 2)
        o_bw = rg.alloc(NH * CH * 2)
        o_m0 = rg.alloc(3 * 512 * 2)
        o_csc = rg.alloc(3 * 512 * 2)
        o_sin = rg.alloc(DIN * 4)
        o_slp = rg.alloc(DIN * 2)
        o_slc = rg.alloc(DIN * 4)
        o_sinb = rg.alloc(DIN * 2)
        o_wob2 = [rg.alloc(32 * P * 2) for _ in range(2)]
        o_pre = rg.alloc(24 * 1024)
        op_xc8 = o_pre
        op_vfm8 = o_pre + 4096
        op_wdc8 = o_pre + 8192
        op_xcb8 = o_pre + 12288
        op_xcl8 = o_pre + 14336
        op_exc8 = o_pre + 16384
        op_vfh8 = o_pre + 18432
        op_vfl8 = o_pre + 20480
        op_gct8 = o_pre + 22528
        o_sinb2 = o_pre
        o_xtP = [o_xt, o_xt2]
        o_xscsP = [o_xscs, o_xscs2]
        o_sinbP = [o_sinb, o_sinb2]
        o_pct = o_sinb
        o_suf = o_sinb + 2048
        PRE = R(o_pre, 24 * 1024)
        o_small = rg.alloc(4 * 1024)
        o_dat = rg.alloc(256)
        os_bt = o_small
        os_wt = o_small + 256
        os_gtb = o_small + 512
        os_row2 = o_small + 1024
        os_dat4 = o_small + 1536
        os_coef = o_small + 2048
        os_dacol = o_small + 2560

        SM = R(o_small, 4 * 1024)

        def ssd_preamble(sweep):
            XC8 = f32(op_xc8, TOK, 0, NH)
            XC8G = f32(op_vfm8, TOK, 0, NH)
            DA8 = f32(op_wdc8, TOK, 0, NH)
            WDC8 = f32(op_wdc8, TOK, 0, NH)
            sch.add("ve", lambda e: e.tensor_scalar_mul(DA8, DTS, AC),
                    reads=[R(o_dt, TOK * 4), R(o_avec, 4)], writes=[R(op_wdc8, 4096)], tag="da")
            sch.add("ve", lambda e: e.tensor_tensor_scan(
                XC8G, DA8, ZCOL[0:NH, 0:1].broadcast_to([NH, TOK]), 0.0,
                ALU.add, ALU.add),
                reads=[R(op_wdc8, 4096), R(o_zcol, 64)],
                writes=[R(op_vfm8, 4096)], tag="xc")
            sch.add("ve", lambda e: e.tensor_copy(XC8[:, 0:CH], XC8G[:, 0:CH]),
                    reads=[R(op_vfm8, 4096)], writes=[R(op_xc8, 4096)], tag="xcs0")
            for c in range(1, NCH):
                sch.add("ve", lambda e, c=c: e.tensor_scalar(
                    XC8[:, c * CH:(c + 1) * CH], XC8G[:, c * CH:(c + 1) * CH],
                    XC8G[:, c * CH - 1:c * CH], 0.0, ALU.subtract, ALU.add),
                    reads=[R(op_vfm8, 4096)], writes=[R(op_xc8, 4096)], tag="xcs")
            DACOL = f32(os_dacol, 8, 0, NH)
            sch.add("ve", lambda e: e.tensor_copy(
                DACOL, XC8.rearrange("p (c t) -> p c t", c=NCH)[:, :, 127]),
                reads=[R(op_xc8, 4096)], writes=[SM], tag="dacol")
            if sweep == 1:
                DCOLS = f32(os_row2, 1, 0, NH)
                sch.add("ve", lambda e: e.reduce_sum(DCOLS, DACOL, axis=AX.X),
                        reads=[SM], writes=[SM], tag="datred")
                sch.add("te", lambda e: e.transpose(PSC[0:1, 0:NH], DCOLS, IDF[0:NH, 0:NH]),
                        reads=[SM, R(o_idf, 512)], writes=[pr("PSC", 0, NH)], tag="datt")
                sch.add("ve", lambda e: e.tensor_copy(f32(o_dat, NH, 0, 1), PSC[0:1, 0:NH]),
                        reads=[pr("PSC", 0, NH)], writes=[R(o_dat, 256)], tag="datev")
            if sweep == 2:
                for c in range(NCH):
                    sch.add("te", lambda e, c=c: e.transpose(
                        PSC[0:1, 0:NH], DACOL[:, c:c + 1], IDF[0:NH, 0:NH]),
                        reads=[SM, R(o_idf, 512)], writes=[pr("PSC", 0, NH)], tag="gdt")
                    GR = f32(os_row2 + 512, NH, 0, 1)
                    sch.add("se", lambda e: e.activation(GR, PSC[0:1, 0:NH], AF.Exp),
                            reads=[pr("PSC", 0, NH)], writes=[SM], tag="gexp")
                    sch.add("te", lambda e: e.matmul(PSC[:, 64:64 + NH], ONESRF, GR,
                                                     start=True, stop=True),
                            reads=[SM, R(o_onesrf, 512)], writes=[pr("PSC", 64, 64 + NH)], tag="gbc")
                    sch.add("ve", lambda e, c=c: e.tensor_copy(
                        f32(op_gct8 + c * 256, NH), PSC[:, 64:64 + NH]),
                        reads=[pr("PSC", 64, 64 + NH)], writes=[R(op_gct8 + c * 256, 256)], tag="gce")
            else:
                SUF = f32(o_suf, 16, 0, NH)
                sch.add("ve", lambda e: e.memset(SUF, 0.0),
                        writes=[R(o_suf, 64)], tag="sufz")
                sch.add("ve", lambda e: e.tensor_copy(SUF[:, 0:8], DACOL),
                        reads=[SM], writes=[R(o_suf, 64)], tag="sufc")
                for sh in (1, 2, 4):
                    sch.add("ve", lambda e, sh=sh: e.tensor_add(
                        SUF[:, 0:8], SUF[:, 0:8], SUF[:, sh:sh + 8]),
                        reads=[R(o_suf, 64)], writes=[R(o_suf, 64)], tag="sufa")
                for c in range(NCH):
                    sch.add("te", lambda e, c=c: e.transpose(
                        PSC[0:1, 0:NH], SUF[:, c + 1:c + 2], IDF[0:NH, 0:NH]),
                        reads=[R(o_suf, 64), R(o_idf, 512)], writes=[pr("PSC", 0, NH)], tag="pdt")
                    GR = f32(os_row2 + 512, NH, 0, 1)
                    sch.add("se", lambda e: e.activation(GR, PSC[0:1, 0:NH], AF.Exp),
                            reads=[pr("PSC", 0, NH)], writes=[SM], tag="pexp")
                    sch.add("te", lambda e: e.matmul(PSC[:, 64:64 + NH], ONESRF, GR,
                                                     start=True, stop=True),
                            reads=[SM, R(o_onesrf, 512)], writes=[pr("PSC", 64, 64 + NH)], tag="pbc")
                    sch.add("ve", lambda e, c=c: e.tensor_copy(
                        f32(o_pct + c * 256, NH), PSC[:, 64:64 + NH]),
                        reads=[pr("PSC", 64, 64 + NH)], writes=[R(o_pct + c * 256, 256)], tag="pce")
            for c in range(NCH):
                sch.add("se", lambda e, c=c: e.activation(
                    WDC8[:, c * CH:(c + 1) * CH], XC8[:, c * CH:(c + 1) * CH],
                    AF.Exp, bias=XC8[:, c * CH + 127:c * CH + 128], scale=-1.0),
                    reads=[R(op_xc8, 4096)], writes=[R(op_wdc8, 4096)], tag="wdc")
            sch.add("ve", lambda e: e.tensor_mul(WDC8, WDC8, DTS),
                    reads=[R(op_wdc8, 4096), R(o_dt, TOK * 4)],
                    writes=[R(op_wdc8, 4096)], tag="wdt")
            if sweep == 2:
                XCB8 = b16(op_xcb8, TOK, 0, NH)
                XCL8 = b16(op_xcl8, TOK, 0, NH)
                EXC8 = b16(op_exc8, TOK, 0, NH)
                VFM8 = f32(op_vfm8, TOK, 0, NH)
                VFH8 = b16(op_vfh8, TOK, 0, NH)
                VFL8 = b16(op_vfl8, TOK, 0, NH)
                sch.add("ve", lambda e: e.tensor_copy(XCB8, XC8),
                        reads=[R(op_xc8, 4096)], writes=[R(op_xcb8, 2048)], tag="xcb")
                sch.add("ve", lambda e: e.tensor_sub(XCL8, XC8, XCB8),
                        reads=[R(op_xc8, 4096), R(op_xcb8, 2048)],
                        writes=[R(op_xcl8, 2048)], tag="xcl")
                sch.add("se", lambda e: e.activation(EXC8, XC8, AF.Exp),
                        reads=[R(op_xc8, 4096)], writes=[R(op_exc8, 2048)], tag="exc")
                sch.add("se", lambda e: e.activation(VFM8, DTS, AF.Ln),
                        reads=[R(o_dt, TOK * 4)], writes=[R(op_vfm8, 4096)], tag="lnd")
                sch.add("ve", lambda e: e.tensor_sub(VFM8, VFM8, XC8),
                        reads=[R(op_vfm8, 4096), R(op_xc8, 4096)],
                        writes=[R(op_vfm8, 4096)], tag="vfm")
                sch.add("ve", lambda e: e.tensor_copy(VFH8, VFM8),
                        reads=[R(op_vfm8, 4096)], writes=[R(op_vfh8, 2048)], tag="vfh")
                sch.add("ve", lambda e: e.tensor_sub(VFL8, VFM8, VFH8),
                        reads=[R(op_vfm8, 4096), R(op_vfh8, 2048)],
                        writes=[R(op_vfl8, 2048)], tag="vfl")

        def chunk_load(c, sweep):
            par = c % 2
            XTp = b16(o_xtP[par], DIN)
            sch.add("sy", lambda e, c=c, d=XTp: e.dma_start(out=d, in_=xt_s[c]),
                    reads=[("xt_s", 0, 32)], writes=[R(o_xtP[par], DIN * 2)],
                    dma=True, tag="chld")
            nbc = 2 if sweep == 2 else 1
            sch.add("sy", lambda e, c=c, nbc=nbc, off=o_xscsP[par]: e.dma_start(
                out=arf[:, off // 2:off // 2 + nbc * CH].rearrange(
                    "p (b t) -> p b t", b=nbc),
                in_=xbc_s[32:32 + nbc, :, c * CH:(c + 1) * CH].transpose([1, 0, 2])),
                reads=[("xbc_s", 32, 34)], writes=[R(o_xscsP[par], 512)],
                dma=True, tag="bcld")

        def bwprep_te(c):
            par = c % 2
            XSC = b16(o_xscsP[par], 256)
            sch.add("te", lambda e, XSC=XSC: e.transpose(PTB[:, 0:P], XSC[:, 0:CH], ID128b),
                    reads=[R(o_xscsP[par], 512), R(o_id128b, 256)],
                    writes=[pr("PTB", 0, P)], tag="btt")
            sch.add("te", lambda e, c=c: e.transpose(
                PSC[:, 128:192], f32(op_wdc8, TOK, 0, NH)[:, c * CH:(c + 1) * CH],
                IDF[0:64, 0:64]),
                reads=[R(op_wdc8, 4096), R(o_idf, 512)], writes=[pr("PSC", 128, 192)], tag="wtt")

        def bwprep_ve(c, sweep):
            par = c % 2
            XT = b16(o_xtP[par], DIN)
            BT = b16(os_bt, P)
            sch.add("ve", lambda e: e.tensor_copy(BT, PTB[:, 0:P]),
                    reads=[pr("PTB", 0, P)], writes=[R(os_bt, 256)], tag="bte")
            WT = f32(os_wt, 64)
            sch.add("ve", lambda e: e.tensor_copy(WT, PSC[:, 128:192]),
                    reads=[pr("PSC", 128, 192)], writes=[R(os_wt, 256)], tag="wte")
            BW = b16(o_bw, NH * CH)
            for eng, h0, h1 in (("ve", 0, 46), ("ge", 46, 64)):
                sch.add(eng, lambda e, h0=h0, h1=h1: e.tensor_tensor(
                    BW[0:P, h0 * P:h1 * P].rearrange("p (h n) -> p h n", h=h1 - h0),
                    BT.unsqueeze(1).broadcast_to([P, h1 - h0, P]),
                    WT[0:P, h0:h1].unsqueeze(2).broadcast_to([P, h1 - h0, P]), ALU.mult),
                    reads=[R(os_bt, 256), R(os_wt, 256)],
                    writes=[R(o_bw + h0 * P * 2, (h1 - h0) * P * 2)], tag="bw")
            if sweep == 2:
                sch.add("ve", lambda e, XT=XT: e.tensor_tensor(
                    b16(o_xtd, DIN).rearrange("p (h q) -> p h q", h=NH),
                    XT.rearrange("p (h q) -> p h q", h=NH),
                    DT64B.unsqueeze(2).broadcast_to([P, NH, HD]), ALU.mult),
                    reads=[R(o_xtP[par], DIN * 2), R(o_dt64, 128)],
                    writes=[R(o_xtd, DIN * 2)], tag="xtd")

        def scan_slice(c, hg):
            off = hg * 2048
            SRUNs = f32(o_sin + off, 512)
            GCT = f32(op_gct8 + (c - 1) * 256, NH)
            sch.add("ge", lambda e, s=SRUNs, g=GCT, hg=hg: e.tensor_tensor(
                s.rearrange("p (h q) -> p h q", h=8), s.rearrange("p (h q) -> p h q", h=8),
                g[:, hg * 8:(hg + 1) * 8].unsqueeze(2).broadcast_to([P, 8, HD]), ALU.mult),
                reads=[R(o_sin + off, 2048), R(op_gct8 + (c - 1) * 256, 256)],
                writes=[R(o_sin + off, 2048)], tag="sc1")
            sch.add("ve", lambda e, s=SRUNs, off=off: e.tensor_add(
                s, s, f32(o_slc + off, 512)),
                reads=[R(o_sin + off, 2048), R(o_slc + off, 2048)],
                writes=[R(o_sin + off, 2048)], tag="sc2")

        def slocmm(c, mid=None, scan_c=None):
            par = c % 2
            XT = b16(o_xtP[par], DIN)
            BW = b16(o_bw, NH * CH)
            for hg in range(8):
                if hg == 4 and mid is not None:
                    mid()
                if scan_c is not None:
                    scan_slice(scan_c, hg)
                po = (hg % 3) * 512
                for hh in range(8):
                    h = hg * 8 + hh
                    sch.add("te", lambda e, po=po, hh=hh, h=h, BW=BW, XT=XT: e.matmul(
                        PSA[:, po + hh * 64:po + (hh + 1) * 64],
                        BW[:, h * P:(h + 1) * P], XT[:, h * HD:(h + 1) * HD],
                        start=True, stop=True),
                        reads=[R(o_bw, NH * CH * 2), R(o_xtP[par], DIN * 2)],
                        writes=[pr("PSA", po + hh * 64, po + (hh + 1) * 64)], tag="slmm")
                sch.add("se", lambda e, hg=hg, po=po: e.copy(
                    f32(o_slc + hg * 512 * 4, 512), PSA[:, po:po + 512]),
                    reads=[pr("PSA", po, po + 512)],
                    writes=[R(o_slc + hg * 2048, 2048)], tag="slev")

        def scan_update(c):
            SRUN = f32(o_sin, DIN)
            GCT = f32(op_gct8 + (c - 1) * 256, NH)
            SLP = f32(o_slc, DIN)
            sch.add("ge", lambda e: e.tensor_tensor(
                SRUN.rearrange("p (h q) -> p h q", h=NH), SRUN.rearrange("p (h q) -> p h q", h=NH),
                GCT.unsqueeze(2).broadcast_to([P, NH, HD]), ALU.mult),
                reads=[R(o_sin, DIN * 4), R(op_gct8 + (c - 1) * 256, 256)],
                writes=[R(o_sin, DIN * 4)], tag="sc1")
            sch.add("ve", lambda e: e.tensor_add(SRUN, SRUN, SLP),
                    reads=[R(o_sin, DIN * 4), R(o_slc, DIN * 4)], writes=[R(o_sin, DIN * 4)], tag="sc2")

        ssd_preamble(1)
        sch.add("ve", lambda e: e.memset(f32(o_sin, DIN), 0.0),
                writes=[R(o_sin, DIN * 4)], tag="s1z")
        OSWDB = os_row2
        O_BWP = [(o_bw, o_bw + 32 * P * 2), (o_slc, o_sin)]

        def s1_prep(c):
            par = c % 2
            blo, bhi = O_BWP[par]
            XSC = b16(o_xscsP[par], 256)
            WDB = b16(OSWDB, CH)
            sch.add("ve", lambda e, c=c, d=WDB: e.tensor_copy(
                d[0:NH, :], f32(op_wdc8, TOK, 0, NH)[:, c * CH:(c + 1) * CH]),
                reads=[R(op_wdc8, 4096)], writes=[R(OSWDB, 256)], tag="wdb")
            sch.add("te", lambda e, XSC=XSC: e.transpose(PTB[:, 0:P], XSC[:, 0:CH], ID128b),
                    reads=[R(o_xscsP[par], 512), R(o_id128b, 256)],
                    writes=[pr("PTB", 0, P)], tag="btt")
            sch.add("te", lambda e, s=WDB: e.transpose(
                PTB[:, 128:192], s[0:NH, :], ID128b[0:NH, 0:NH]),
                reads=[R(OSWDB, 256), R(o_id128b, 256)], writes=[pr("PTB", 128, 192)], tag="wtt1")
            BT = b16(os_bt, P)
            sch.add("ve", lambda e: e.tensor_copy(BT, PTB[:, 0:P]),
                    reads=[pr("PTB", 0, P)], writes=[R(os_bt, 256)], tag="bte")
            WT = f32(os_wt, 64)
            sch.add("ve", lambda e, c=c: e.tensor_tensor(
                WT, PTB[:, 128:192], f32(o_pct + c * 256, NH), ALU.mult),
                reads=[pr("PTB", 128, 192), R(o_pct + c * 256, 256)],
                writes=[R(os_wt, 256)], tag="wtp")
            for eng, h0, h1 in (("ve", 0, 32), ("ve", 32, 46), ("ge", 46, 64)):
                base = blo if h0 < 32 else bhi + (h0 - 32) * P * 2
                BWh = b16(base, (h1 - h0) * P)
                sch.add(eng, lambda e, h0=h0, h1=h1, BT=BT, WT=WT, BWh=BWh: e.tensor_tensor(
                    BWh.rearrange("p (h n) -> p h n", h=h1 - h0),
                    BT.unsqueeze(1).broadcast_to([P, h1 - h0, P]),
                    WT[0:P, h0:h1].unsqueeze(2).broadcast_to([P, h1 - h0, P]), ALU.mult),
                    reads=[R(os_bt, 256), R(os_wt, 256)],
                    writes=[R(base, (h1 - h0) * P * 2)], tag="bw")

        PERS = [("PSA", 0), ("PSA", 512), ("PSA", 1024), ("PSB", 0), ("PSB", 512)]
        PERS_T = [PSA, PSA, PSA, PSB, PSB]
        chunk_load(0, 1)
        s1_prep(0)
        for c in range(NCH):
            par = c % 2
            XT = b16(o_xtP[par], DIN)
            blo, bhi = O_BWP[par]
            if c < NCH - 1:
                chunk_load(c + 1, 1)
            for hg in range(5):
                pbn, po = PERS[hg]
                PT = PERS_T[hg]
                for hh in range(8):
                    h = hg * 8 + hh
                    base = blo if h < 32 else bhi
                    BWh = b16(base, 32 * P)
                    hr = h % 32
                    sch.add("te", lambda e, PT=PT, po=po, hh=hh, h=h, hr=hr, BWh=BWh, XT=XT, c=c: e.matmul(
                        PT[:, po + hh * 64:po + (hh + 1) * 64],
                        BWh[:, hr * P:(hr + 1) * P], XT[:, h * HD:(h + 1) * HD],
                        start=(c == 0), stop=(c == NCH - 1)),
                        reads=[R(base, 32 * P * 2), R(o_xtP[par], DIN * 2)],
                        writes=[pr(pbn, po + hh * 64, po + (hh + 1) * 64)], tag="spmm")
                if hg == 2 and c < NCH - 1:
                    s1_prep(c + 1)
            for hg in (5, 6, 7):
                for hh in range(8):
                    h = hg * 8 + hh
                    hr = h % 32
                    BWh = b16(bhi, 32 * P)
                    sch.add("te", lambda e, hh=hh, h=h, hr=hr, BWh=BWh, XT=XT: e.matmul(
                        PSC[:, hh * 64:(hh + 1) * 64],
                        BWh[:, hr * P:(hr + 1) * P], XT[:, h * HD:(h + 1) * HD],
                        start=True, stop=True),
                        reads=[R(bhi, 32 * P * 2), R(o_xtP[par], DIN * 2)],
                        writes=[pr("PSC", hh * 64, (hh + 1) * 64)], tag="slmm")
                sch.add("se", lambda e, hg=hg: e.copy(
                    f32(o_slc + hg * 2048, 512), PSC[:, 0:512]),
                    reads=[pr("PSC", 0, 512)],
                    writes=[R(o_slc + hg * 2048, 2048)], tag="slev")
                sch.add("ve", lambda e, hg=hg: e.tensor_add(
                    f32(o_sin + hg * 2048, 512), f32(o_sin + hg * 2048, 512),
                    f32(o_slc + hg * 2048, 512)),
                    reads=[R(o_sin + hg * 2048, 2048), R(o_slc + hg * 2048, 2048)],
                    writes=[R(o_sin + hg * 2048, 2048)], tag="s1a")
        AGP = b16(o_bw, 4224)
        for hg in range(8):
            if hg < 5:
                pbn, po = PERS[hg]
                PT = PERS_T[hg]
                sch.add("se", lambda e, PT=PT, po=po, hg=hg: e.copy(
                    b16(o_bw + hg * 1024, 512), PT[:, po:po + 512]),
                    reads=[pr(pbn, po, po + 512)],
                    writes=[R(o_bw + hg * 1024, 1024)], tag="agev")
            else:
                sch.add("ve", lambda e, hg=hg: e.tensor_copy(
                    b16(o_bw + hg * 1024, 512), f32(o_sin + hg * 2048, 512)),
                    reads=[R(o_sin + hg * 2048, 2048)],
                    writes=[R(o_bw + hg * 1024, 1024)], tag="agp1")
            if hg == 3:
                sch.add("sy", lambda e: e.dma_start(out=ag_in[:, 0:2048], in_=AGP[:, 0:2048]),
                        reads=[R(o_bw, 4096)], writes=[("ag_in", 0, 1)], dma=True, tag="agw")
        sch.add("ve", lambda e: e.tensor_copy(
            b16(o_bw + DIN * 2, 128, 0, 1), f32(o_dat, NH, 0, 1).bitcast(BF16)),
            reads=[R(o_dat, 256)], writes=[R(o_bw + DIN * 2, 256)], tag="agp2")
        sch.add("sy", lambda e: e.dma_start(out=ag_in[:, 2048:4224], in_=AGP[:, 2048:4224]),
                reads=[R(o_bw + 4096, 4352)], writes=[("ag_in", 1, 2)], dma=True, tag="agw")
        cc_op = sch.add("ge", lambda e: e.collective_compute(
            "AllGather", ALU.bypass, RG, [ag_in.opt()], [ag_out.opt()]),
            reads=[("ag_in", 0, 2)], writes=[("ag_out", 0, 4)], tag="ag")
        sch.ops[cc_op]["dma"] = False

        o_zc = rg.alloc(DIN * 2)
        o_yb = o_slp

        def s0_combine():
            sch.add("sy", lambda e: e.dma_start(
                out=f32(os_dat4, NH, 0, 4),
                in_=ag_out.rearrange("(i p) c -> i p c", p=P)[:, 0, DIN:DIN + 128].bitcast(F32)),
                reads=[("ag_out", 0, 4)], writes=[SM], dma=True, tag="datld")
            sch.add("te", lambda e: e.matmul(PSC[0:4, 0:NH], MMK, f32(os_dat4, NH, 0, 4),
                                             start=True, stop=True),
                    reads=[SM, R(o_mmask, 16)], writes=[pr("PSC", 0, NH)], tag="exmm")
            sch.add("se", lambda e: e.activation(f32(os_coef, NH, 0, 4), PSC[0:4, 0:NH], AF.Exp),
                    reads=[pr("PSC", 0, NH)], writes=[SM], tag="exe")
            sch.add("ve", lambda e: e.tensor_scalar_mul(f32(os_coef, NH, 0, 4), f32(os_coef, NH, 0, 4), CMK),
                    reads=[SM, R(o_cmask, 4)], writes=[SM], tag="exm")
            sch.add("sy", lambda e: e.dma_start(out=co_d, in_=f32(os_coef, NH, 0, 4)),
                    reads=[SM], writes=[("co_d", 0, 1)], dma=True, tag="cost")
            sch.add("ve", lambda e: e.memset(f32(o_sin, DIN), 0.0),
                    reads=[], writes=[R(o_sin, DIN * 4)], tag="s0z")
            o_cb = o_wob2[0]
            o_t0 = o_wob2[0] + 1024
            for i in range(4):
                o_sfb = o_wob2[1] if i % 2 == 0 else o_zc
                SFB = b16(o_sfb, DIN)
                sch.add("sy", lambda e, i=i, d=SFB: e.dma_start(
                    out=d, in_=ag_out[i * P:(i + 1) * P, 0:DIN]),
                    reads=[("ag_out", i, i + 1)], writes=[R(o_sfb, DIN * 2)], dma=True, tag="sfld")
                CB = f32(o_cb, NH)
                sch.add("sy", lambda e, i=i, d=CB: e.dma_start(
                    out=d, in_=co_d[i:i + 1, :].broadcast_to([P, NH])),
                    reads=[("co_d", 0, 1)], writes=[R(o_cb, NH * 4)], dma=True, tag="cbld")
                for q in range(4):
                    sch.add("ve", lambda e, s=SFB, cb=CB, q=q, o_t0=o_t0: e.tensor_tensor(
                        f32(o_t0, 1024).rearrange("p (h q) -> p h q", h=16),
                        s[0:P, q * 1024:(q + 1) * 1024].rearrange("p (h q) -> p h q", h=16),
                        cb[0:P, q * 16:(q + 1) * 16].unsqueeze(2).broadcast_to([P, 16, HD]),
                        ALU.mult),
                        reads=[R(o_sfb, DIN * 2), R(o_cb, NH * 4)],
                        writes=[R(o_t0, 4096)], tag="s0m")
                    sch.add("ve", lambda e, q=q, o_t0=o_t0: e.tensor_add(
                        f32(o_sin + q * 4096, 1024), f32(o_sin + q * 4096, 1024), f32(o_t0, 1024)),
                        reads=[R(o_sin, DIN * 4), R(o_t0, 4096)], writes=[R(o_sin, DIN * 4)], tag="s0a")
        o_g2 = o_zc
        o_yn = rg.alloc(32 * 512 * 2)
        o_xr1 = [rg.alloc(512 * 4) for _ in range(2)]
        o_x1o = [rg.alloc(512 * 4) for _ in range(2)]

        def emit_wold(mb):
            slot = mb % 2
            WOB = b16(o_wob2[slot], 32 * P)
            sch.add("ge", lambda e, d=WOB, mb=mb: e.dma_start(
                    out=d.rearrange("p (kb m) -> p kb m", kb=32), in_=wout[mb].transpose([1, 0, 2])),
                    reads=[("wout", mb, mb + 1)], writes=[R(o_wob2[slot], 32 * P * 2)],
                    dma=True, tag="wold")

        def emit_sinb(c):
            SINB = b16(o_sinbP[c % 2], DIN)
            sch.add("se", lambda e, d=SINB: e.copy(d, f32(o_sin, DIN)),
                    reads=[R(o_sin, DIN * 4)], writes=[R(o_sinbP[c % 2], DIN * 2)], tag="sinb")

        ssd_preamble(2)
        if DBG:
            sch.add("sy", lambda e: e.dma_start(out=dbg_exc, in_=b16(op_exc8, TOK, 0, NH)),
                    reads=[PRE], writes=[("dbg", 9, 10)], dma=True, tag="dbgexc")
        chunk_load(0, 2)
        bwprep_te(0)
        bwprep_ve(0, 2)
        for c in range(NCH):
            par = c % 2
            if c < NCH - 1:
                chunk_load(c + 1, 2)
            slocmm(c)
            XSC = b16(o_xscsP[par], 256)
            sch.add("te", lambda e, XSC=XSC: e.matmul(
                PSC[:, 0:128], XSC[:, 0:CH], XSC[:, CH:2 * CH],
                start=True, stop=True),
                reads=[R(o_xscsP[par], 512)], writes=[pr("PSC", 0, 128)], tag="gtmm")
            GTB = b16(os_gtb, P)
            sch.add("ve", lambda e: e.tensor_copy(GTB, PSC[:, 0:128]),
                    reads=[pr("PSC", 0, 128)], writes=[SM], tag="gtev")
            if DBG and c < 2:
                sch.add("sy", lambda e, c=c: e.dma_start(
                    out=dbg_slc[:, c * 4096:(c + 1) * 4096], in_=f32(o_slc, DIN)),
                    reads=[R(o_slc, DIN * 4)], writes=[("dbg", 3, 4)], dma=True, tag="dbgslc")
            ZC = b16(o_zc, DIN)
            zdst = arf[:, o_zc // 2: o_zc // 2 + DIN].rearrange("p (b t) -> p b t", b=32)
            def emit_zld(c=c, d=zdst):
                sch.add("sy", lambda e, c=c, d=d: e.dma_start(
                    out=d, in_=z_s[0:32, :, c * CH:(c + 1) * CH].transpose([1, 0, 2])),
                    reads=[("z_s", 0, 32)], writes=[R(o_zc, DIN * 2)], dma=True, tag="zld")
            XCB16 = b16(op_xcb8, TOK, 0, NH)[0:NH, c * CH:(c + 1) * CH]
            XCL = b16(op_xcl8, TOK, 0, NH)[0:NH, c * CH:(c + 1) * CH]
            EXCB = b16(op_exc8, TOK, 0, NH)[0:NH, c * CH:(c + 1) * CH]
            VFH = b16(op_vfh8, TOK, 0, NH)[0:NH, c * CH:(c + 1) * CH]
            VFL = b16(op_vfl8, TOK, 0, NH)[0:NH, c * CH:(c + 1) * CH]
            if c == 0:
                s0_combine()
                emit_zld()
                emit_sinb(0)
            else:
                emit_zld()
            SINB = b16(o_sinbP[par], DIN)
            if DBG:
                sch.add("sy", lambda e, c=c, s=SINB: e.dma_start(
                    out=dbg_sin[:, c * 4096:(c + 1) * 4096], in_=s),
                    reads=[R(o_sinbP[par], DIN * 2)], writes=[("dbg", 4, 5)], dma=True, tag="dbgsin")
            XT = b16(o_xtP[par], DIN)

            CSE_SLOT = [(PSC, 0, pr("PSC", 0, 512)), (PSB, 512, pr("PSB", 512, 1024))]

            def emit_cse(tg):
                PT, cpo, cpb = CSE_SLOT[tg % 2]
                for r4 in range(4):
                    h = tg * 4 + r4
                    sch.add("te", lambda e, PT=PT, cpo=cpo, r4=r4, h=h, xs=EXCB: e.matmul(
                        PT[:, cpo + r4 * 128:cpo + (r4 + 1) * 128],
                        ID64b[:, h:h + 1].broadcast_to([NH, P]), xs,
                        start=True, stop=True),
                        reads=[R(op_exc8, 2048), R(o_id64b, 128)],
                        writes=[cpb], tag="cse")

            def emit_T(tg):
                po = (tg % 3) * 512
                sch.add("te", lambda e, po=po: e.matmul(
                    PSA[:, po:po + 512], ID128b, MASKR, start=True, stop=False),
                    reads=[R(o_id128b, 256), R(o_mask, 1024)],
                    writes=[pr("PSA", po, po + 512)], tag="tbm")
                for hh in range(4):
                    h = tg * 4 + hh
                    for xi, xsrc in enumerate((XCB16, XCL)):
                        ro = op_xcb8 if xi == 0 else op_xcl8
                        sch.add("te", lambda e, po=po, hh=hh, h=h, xs=xsrc: e.matmul(
                            PSA[:, po + hh * 128:po + (hh + 1) * 128],
                            ID64b[:, h:h + 1].broadcast_to([NH, P]), xs,
                            start=False, stop=False),
                            reads=[R(ro, 2048), R(o_id64b, 128)],
                            writes=[pr("PSA", po + hh * 128, po + (hh + 1) * 128)], tag="tbe")
                for vi, vsrc in enumerate((VFH, VFL)):
                    ro = op_vfh8 if vi == 0 else op_vfl8
                    sch.add("te", lambda e, po=po, tg=tg, vs=vsrc, vi=vi: e.matmul(
                        PSA[:, po:po + 512], vs,
                        ID64b[:, tg * 4:(tg + 1) * 4].unsqueeze(2).broadcast_to([NH, 4, P]),
                        start=False, stop=(vi == 1)),
                        reads=[R(ro, 2048), R(o_id64b, 128)], writes=[pr("PSA", po, po + 512)], tag="tbv")
                sch.add("se", lambda e, po=po, tg=tg: e.activation(
                    b16(o_m0 + (tg % 3) * 1024, 512), PSA[:, po:po + 512], AF.Exp),
                    reads=[pr("PSA", po, po + 512)],
                    writes=[R(o_m0 + (tg % 3) * 1024, 1024)], tag="tbx")

            def emit_CY(tg):
                po = (tg % 3) * 512
                mo = o_m0 + (tg % 3) * 1024
                co = o_csc + (tg % 3) * 1024
                M0S = b16(mo, 512)
                CSCS = b16(co, 512)
                sch.add("ve", lambda e, M0S=M0S: e.tensor_tensor(
                    M0S.rearrange("p (r l) -> p r l", r=4),
                    M0S.rearrange("p (r l) -> p r l", r=4),
                    GTB.unsqueeze(1).broadcast_to([P, 4, P]), ALU.mult),
                    reads=[R(mo, 1024), R(os_gtb, 256)],
                    writes=[R(mo, 1024)], tag="mg")
                cpt, cpo, cpb = CSE_SLOT[tg % 2]
                sch.add("ve", lambda e, PT=cpt, cpo=cpo, XSC=XSC, CSCS=CSCS: e.tensor_tensor(
                    CSCS.rearrange("p (r l) -> p r l", r=4),
                    PT[:, cpo:cpo + 512].rearrange("p (r l) -> p r l", r=4),
                    XSC[:, CH:2 * CH].unsqueeze(1).broadcast_to([P, 4, P]), ALU.mult),
                    reads=[cpb], writes=[R(co, 1024)], tag="csc")
                for bp in (tg * 2, tg * 2 + 1):
                    yo = (bp % 4) * 128
                    sch.add("te", lambda e, yo=yo, bp=bp: e.matmul(
                        PSB[:, yo:yo + 128], b16(o_xtd, DIN)[:, bp * 128:(bp + 1) * 128],
                        ID128b, start=True, stop=False),
                        reads=[R(o_xtd, DIN * 2), R(o_id128b, 256)],
                        writes=[pr("PSB", yo, yo + 128)], tag="yds")
                    h0b, h1b = 2 * bp - tg * 4, 2 * bp + 1 - tg * 4
                    sch.add("te", lambda e, yo=yo, hh=h0b, XT=XT, M0S=M0S, bp=bp: e.matmul(
                        PSB[0:64, yo:yo + 128], XT[:, bp * 128:bp * 128 + HD],
                        M0S[:, hh * P:(hh + 1) * P], start=False, stop=False,
                        tile_position=(0, 0)),
                        reads=[R(o_xtP[par], DIN * 2), R(mo, 1024)],
                        writes=[pr("PSB", yo, yo + 128)], tag="ydg")
                    sch.add("te", lambda e, yo=yo, hh=h0b, SINB=SINB, CSCS=CSCS, bp=bp: e.matmul(
                        PSB[0:64, yo:yo + 128], SINB[:, bp * 128:bp * 128 + HD],
                        CSCS[:, hh * P:(hh + 1) * P], start=False, stop=False,
                        tile_position=(0, 0)),
                        reads=[R(o_sinbP[par], DIN * 2), R(co, 1024)],
                        writes=[pr("PSB", yo, yo + 128)], tag="yof")
                    sch.add("te", lambda e, yo=yo, hh=h1b, XT=XT, M0S=M0S, bp=bp: e.matmul(
                        PSB[64:128, yo:yo + 128], XT[:, bp * 128 + HD:(bp + 1) * 128],
                        M0S[:, hh * P:(hh + 1) * P], start=False, stop=False,
                        tile_position=(0, 64)),
                        reads=[R(o_xtP[par], DIN * 2), R(mo, 1024)],
                        writes=[pr("PSB", yo, yo + 128)], tag="ydg2")
                    sch.add("te", lambda e, yo=yo, hh=h1b, SINB=SINB, CSCS=CSCS, bp=bp: e.matmul(
                        PSB[64:128, yo:yo + 128], SINB[:, bp * 128 + HD:(bp + 1) * 128],
                        CSCS[:, hh * P:(hh + 1) * P], start=False, stop=True,
                        tile_position=(0, 64)),
                        reads=[R(o_sinbP[par], DIN * 2), R(co, 1024)],
                        writes=[pr("PSB", yo, yo + 128)], tag="yof2")
                if tg % 2 == 1:
                    rnd = tg // 2
                    sch.add("ve", lambda e, rnd=rnd: e.tensor_copy(
                        b16(o_yb + rnd * 1024, 512), PSB[:, 0:512]),
                        reads=[pr("PSB", 0, 512)],
                        writes=[R(o_yb + rnd * 1024, 1024)], tag="yev")

            for tg in range(16):
                emit_cse(tg)
                emit_T(tg)
                if tg >= 1:
                    emit_CY(tg - 1)
                if tg == 3 and c < NCH - 1:
                    bwprep_te(c + 1)
                    bwprep_ve(c + 1, 1)
                if 3 <= tg <= 10 and c < NCH - 1:
                    scan_slice(c + 1, tg - 3)
                if tg in (5, 7, 9, 11) and c < NCH - 1:
                    q = (tg - 5) // 2
                    sch.add("se", lambda e, q=q, d=o_sinbP[(c + 1) % 2]: e.copy(
                        b16(d + q * 2048, 1024), f32(o_sin + q * 4096, 1024)),
                        reads=[R(o_sin + q * 4096, 4096)],
                        writes=[R(o_sinbP[(c + 1) % 2] + q * 2048, 2048)], tag="sinbq")
                if tg == 8 and c % 4 == 3:
                    emit_wold(0)
                    emit_wold(1)
            emit_CY(15)
            sch.add("se", lambda e, ZC=ZC: e.activation(ZC, ZC, AF.Silu),
                    reads=[R(o_zc, DIN * 2)], writes=[R(o_zc, DIN * 2)], tag="slz")
            YB = b16(o_yb, DIN)
            if DBG:
                sch.add("sy", lambda e, c=c: e.dma_start(
                    out=dbg_y[:, c * 4096:(c + 1) * 4096], in_=YB),
                    reads=[R(o_yb, DIN * 2)], writes=[("dbg", 1, 2)], dma=True, tag="dbgy")
            sch.add("ve", lambda e, ZC=ZC: e.tensor_mul(YB, YB, ZC),
                    reads=[R(o_yb, DIN * 2), R(o_zc, DIN * 2)], writes=[R(o_yb, DIN * 2)], tag="gy")
            if DBG and c == 0:
                sch.add("sy", lambda e: e.dma_start(out=dbg_g, in_=YB),
                        reads=[R(o_yb, DIN * 2)], writes=[("dbg", 6, 7)], dma=True, tag="dbgg")
            sch.add("se", lambda e: e.activation(b16(o_g2, DIN), YB, AF.Square),
                    reads=[R(o_yb, DIN * 2)], writes=[R(o_g2, DIN * 2)], tag="g2")
            for blk in range(32):
                sch.add("te", lambda e, blk=blk: e.matmul(
                    PSC[0:1, 0:128], ONESC, b16(o_g2 + blk * 256, 128),
                    start=(blk == 0), stop=(blk == 31)),
                    reads=[R(o_g2, DIN * 2), R(o_ones1, 64)],
                    writes=[pr("PSC", 0, 128)], tag="g2mm")
            RSR = f32(os_row2, CH, 0, 1)
            sch.add("se", lambda e: e.activation(RSR, PSC[0:1, 0:128], AF.Ln, bias=CEPS[0:1, :], scale=1.0 / DIN),
                    reads=[pr("PSC", 0, 128), RCEPS], writes=[SM], tag="rs1")
            sch.add("se", lambda e: e.activation(RSR, RSR, AF.Exp, scale=-0.5),
                    reads=[SM], writes=[SM], tag="rs2")
            sch.add("te", lambda e: e.matmul(PSC[:, 128:256], ONESRF, RSR, start=True, stop=True),
                    reads=[SM, R(o_onesrf, 512)], writes=[pr("PSC", 128, 256)], tag="rsb")
            RSBT = b16(os_gtb + 256, CH)
            sch.add("se", lambda e: e.copy(RSBT, PSC[:, 128:256]),
                    reads=[pr("PSC", 128, 256)], writes=[R(os_gtb + 256, 256)], tag="rse")
            gi, cc = c // 4, c % 4
            yn_base = o_yn
            YNv = arf[:, yn_base // 2: yn_base // 2 + 32 * 512].rearrange(
                "p (b t) -> p b t", b=32)[:, :, cc * 128:(cc + 1) * 128]
            o_tg = o_g2
            sch.add("ve", lambda e: e.tensor_tensor(
                b16(o_tg, DIN).rearrange("p (b l) -> p b l", b=32),
                YB.rearrange("p (b l) -> p b l", b=32),
                NWC.unsqueeze(2).broadcast_to([P, 32, 128]), ALU.mult),
                reads=[R(o_yb, DIN * 2), R(o_nwc, 128)], writes=[R(o_g2, DIN * 2)], tag="yn1")
            sch.add("ve", lambda e, YNv=YNv: e.tensor_tensor(
                YNv, b16(o_tg, DIN).rearrange("p (b l) -> p b l", b=32),
                RSBT.unsqueeze(1).broadcast_to([P, 32, 128]), ALU.mult),
                reads=[R(o_g2, DIN * 2), R(os_gtb + 256, 256)],
                writes=[R(yn_base, 32768)], tag="yn2")

            if DBG and c == 3:
                sch.add("sy", lambda e: e.dma_start(out=dbg_yn, in_=b16(o_yn, 32 * 512)),
                        reads=[R(o_yn, 32 * 512 * 2)], writes=[("dbg", 7, 8)], dma=True, tag="dbgyn")
            if cc == 3:
                for mb in range(16):
                    slot = mb % 2
                    WOB = b16(o_wob2[slot], 32 * P)
                    if mb >= 2:
                        emit_wold(mb)
                    for kbi in range(32):
                        sch.add("te", lambda e, kbi=kbi, W=WOB, yb=yn_base: e.matmul(
                            PSB[:, 512:1024], W[:, kbi * P:(kbi + 1) * P],
                            arf[:, yb // 2 + kbi * 512: yb // 2 + (kbi + 1) * 512],
                            start=(kbi == 0), stop=(kbi == 31)),
                            reads=[R(o_wob2[slot], 32 * P * 2), R(yn_base, 32768)],
                            writes=[pr("PSB", 512, 1024)], tag="opmm")
                    XRB = f32(o_xr1[slot], 512)
                    sch.add("sy", lambda e, d=XRB, mb=mb, gi=gi: e.dma_start(
                        out=d, in_=xin[mb * P:(mb + 1) * P, HALO + gi * 512:HALO + (gi + 1) * 512]),
                        reads=[("xin", mb, mb + 1)], writes=[R(o_xr1[slot], 2048)],
                        dma=True, tag="xrld")
                    X1O = f32(o_x1o[slot], 512)
                    sch.add("ve", lambda e, d=X1O, s=XRB: e.tensor_add(d, s, PSB[:, 512:1024]),
                            reads=[R(o_xr1[slot], 2048), pr("PSB", 512, 1024)],
                            writes=[R(o_x1o[slot], 2048)], tag="x1add")
                    sch.add("sy", lambda e, s=X1O, mb=mb, gi=gi: e.dma_start(
                        out=x1_s[mb, :, gi * 512:(gi + 1) * 512], in_=s),
                        reads=[R(o_x1o[slot], 2048)], writes=[("x1_s", mb * 4 + gi * 2, mb * 4 + gi * 2 + 2)],
                        dma=True, tag="x1st")
            if c < NCH - 1:
                parn = (c + 1) % 2
                XTn = b16(o_xtP[parn], DIN)
                sch.add("ve", lambda e, XT=XTn: e.tensor_tensor(
                    b16(o_xtd, DIN).rearrange("p (h q) -> p h q", h=NH),
                    XT.rearrange("p (h q) -> p h q", h=NH),
                    DT64B.unsqueeze(2).broadcast_to([P, NH, HD]), ALU.mult),
                    reads=[R(o_xtP[parn], DIN * 2), R(o_dt64, 128)],
                    writes=[R(o_xtd, DIN * 2)], tag="xtd")

        rg.off = o_xscs

        def ffn(o_xb, o_xres, xres_key):
            rg.push()
            o_wg = [rg.alloc(16 * P * 2) for _ in range(2)]
            o_wu = [rg.alloc(16 * P * 2) for _ in range(2)]
            o_wd = [rg.alloc(16 * P * 2) for _ in range(2)]
            o_hg = rg.alloc(16 * TOK * 2)
            o_sg = rg.alloc(TOK * 2)
            for g in range(4):
                for ibi in range(16):
                    ib = g * 16 + ibi
                    slot = ib % 2
                    WGB = b16(o_wg[slot], 16 * P)
                    WUB = b16(o_wu[slot], 16 * P)
                    sch.add("ge", lambda e, d=WGB, ib=ib: e.dma_start(
                            out=d.rearrange("p (kb m) -> p kb m", kb=16), in_=gw[ib].transpose([1, 0, 2])),
                            reads=[("gw", ib, ib + 1)], writes=[R(o_wg[slot], 16 * P * 2)],
                            dma=True, tag="gld")
                    sch.add("ge", lambda e, d=WUB, ib=ib: e.dma_start(
                            out=d.rearrange("p (kb m) -> p kb m", kb=16), in_=uw[ib].transpose([1, 0, 2])),
                            reads=[("uw", ib, ib + 1)], writes=[R(o_wu[slot], 16 * P * 2)],
                            dma=True, tag="uld")
                    for kbi in range(16):
                        xnb = b16(o_xb + kbi * TOK * 2, TOK)
                        for W, PT, pb, wrng in ((WGB, PSA, "PSA", R(o_wg[slot], 16 * P * 2)),
                                                (WUB, PSB, "PSB", R(o_wu[slot], 16 * P * 2))):
                            for th in range(2):
                                sch.add("te", lambda e, th=th, kbi=kbi, W=W, PT=PT, s=xnb: e.matmul(
                                    PT[:, th * 512:(th + 1) * 512], W[:, kbi * P:(kbi + 1) * P],
                                    s[:, th * 512:(th + 1) * 512],
                                    start=(kbi == 0), stop=(kbi == 15)),
                                    reads=[wrng, R(o_xb + kbi * TOK * 2, TOK * 2)],
                                    writes=[pr(pb, th * 512, (th + 1) * 512)], tag="gmm")
                    SG = b16(o_sg, TOK)
                    HGB = b16(o_hg + ibi * TOK * 2, TOK)
                    for th in range(2):
                        sch.add("se", lambda e, th=th, ib=ib: e.activation(
                            SG[0:P, th * 512:(th + 1) * 512], PSA[:, th * 512:(th + 1) * 512],
                            AF.Silu, scale=GSC[:, ib:ib + 1]),
                            reads=[pr("PSA", th * 512, (th + 1) * 512), R(o_gsc, NIB * 4)],
                            writes=[R(o_sg + th * 1024, 1024)], tag="gsl")
                        sch.add("ve", lambda e, th=th, d=HGB, ib=ib: e.scalar_tensor_tensor(
                            d[0:P, th * 512:(th + 1) * 512], PSB[:, th * 512:(th + 1) * 512],
                            USC[:, ib:ib + 1], SG[0:P, th * 512:(th + 1) * 512],
                            ALU.mult, ALU.mult),
                            reads=[pr("PSB", th * 512, (th + 1) * 512), R(o_usc, NIB * 4),
                                   R(o_sg + th * 1024, 1024)],
                            writes=[R(o_hg + ibi * TOK * 2 + th * 1024, 1024)], tag="hmul")
                for hb in range(16):
                    slot = hb % 2
                    WDB = b16(o_wd[slot], 16 * P)
                    sch.add("ge", lambda e, d=WDB, g=g, hb=hb: e.dma_start(
                            out=d.rearrange("p (kb m) -> p kb m", kb=16), in_=dw[g, hb].transpose([1, 0, 2])),
                            reads=[("dw", g * 16 + hb, g * 16 + hb + 1)],
                            writes=[R(o_wd[slot], 16 * P * 2)], dma=True, tag="dld")
                    for j in range(16):
                        HGB = b16(o_hg + j * TOK * 2, TOK)
                        for th in range(2):
                            def dmm(e, th=th, j=j, W=WDB, s=HGB, g=g):
                                dst = PSA[:, 1024:1536] if th == 0 else PSC[:, 0:512]
                                return e.matmul(dst[:, 0:512], W[:, j * P:(j + 1) * P],
                                                s[:, th * 512:(th + 1) * 512],
                                                start=(j == 0), stop=(j == 15))
                            sch.add("te", dmm,
                                    reads=[R(o_wd[slot], 16 * P * 2), R(o_hg + j * TOK * 2, TOK * 2)],
                                    writes=[pr("PSA", 1024, 1536) if th == 0 else pr("PSC", 0, 512)], tag="dmm")
                    for th in range(2):
                        pb = "PSA" if th == 0 else "PSC"
                        XRB = f32(o_xres + hb * TOK * 4 + th * 512 * 4, 512)
                        def dev(e, th=th, d=XRB, hb=hb):
                            src = PSA[:, 1024:1536] if th == 0 else PSC[:, 0:512]
                            return e.scalar_tensor_tensor(d, src[:, 0:512], DSC[:, hb:hb + 1],
                                                          d, ALU.mult, ALU.add)
                        sch.add("ve", dev,
                                reads=[pr(pb, 1024, 1536) if th == 0 else pr(pb, 0, 512),
                                       R(o_dsc, 64), R(o_xres + hb * TOK * 4 + th * 2048, 2048)],
                                writes=[R(o_xres + hb * TOK * 4 + th * 2048, 2048)], tag="dev")
            rg.pop()

        o_xres = rg.alloc(NDB * TOK * 4)
        o_xb2 = rg.alloc(NDB * TOK * 2)
        for b in range(NDB):
            sch.add("sy", lambda e, b=b: e.dma_start(
                out=f32(o_xres + b * TOK * 4, TOK), in_=x1_s[b]),
                reads=[("x1_s", b * 4, b * 4 + 4)], writes=[R(o_xres + b * TOK * 4, TOK * 4)],
                dma=True, tag="x1ld", dkey="x1ld")

        def ln_resident(o_src, o_dst, key):
            rg.push()
            o_sq = rg.alloc(TOK * 2)
            o_rows = rg.alloc(6 * 4096)
            o_mb = rg.alloc(TOK * 4)
            o_rb = rg.alloc(TOK * 4)
            o_tl = rg.alloc(TOK * 4)
            for b in range(NDB):
                xbb = b16(o_dst + b * TOK * 2, TOK)
                sch.add("ve", lambda e, d=xbb, b=b: e.tensor_copy(d, f32(o_src + b * TOK * 4, TOK)),
                        reads=[R(o_src + b * TOK * 4, TOK * 4)],
                        writes=[R(o_dst + b * TOK * 2, TOK * 2)], tag="l2cp")
                sqb = b16(o_sq, TOK)
                sch.add("se", lambda e, s=xbb: e.activation(sqb, s, AF.Square),
                        reads=[R(o_dst + b * TOK * 2, TOK * 2)], writes=[R(o_sq, TOK * 2)], tag="l2sq")
                for th in range(2):
                    sch.add("te", lambda e, th=th, b=b, s=xbb: e.matmul(
                        PSA[0:1, th * 512:(th + 1) * 512], ONESC, s[:, th * 512:(th + 1) * 512],
                        start=(b == 0), stop=(b == NDB - 1)),
                        reads=[R(o_dst + b * TOK * 2, TOK * 2), R(o_ones1, 64)],
                        writes=[pr("PSA", th * 512, (th + 1) * 512)], tag="l2s1")
                    sch.add("te", lambda e, th=th, b=b: e.matmul(
                        PSB[0:1, th * 512:(th + 1) * 512], ONESC, sqb[:, th * 512:(th + 1) * 512],
                        start=(b == 0), stop=(b == NDB - 1)),
                        reads=[R(o_sq, TOK * 2), R(o_ones1, 64)],
                        writes=[pr("PSB", th * 512, (th + 1) * 512)], tag="l2s2")
            MROW = f32(o_rows, TOK, 0, 1)
            VROW = f32(o_rows + 4096, TOK, 0, 1)
            RROW = f32(o_rows + 8192, TOK, 0, 1)
            ABR = b16(o_rows + 12288, TOK, 0, 1)
            BBR = b16(o_rows + 16384, TOK, 0, 1)
            rows_rng = R(o_rows, 6 * 4096)
            sch.add("ve", lambda e: e.tensor_scalar_mul(MROW, PSA[0:1, 0:TOK], 1.0 / DM),
                    reads=[pr("PSA", 0, 1024)], writes=[rows_rng], tag="l2m")
            sch.add("ve", lambda e: e.tensor_scalar_mul(VROW, PSB[0:1, 0:TOK], 1.0 / DM),
                    reads=[pr("PSB", 0, 1024)], writes=[rows_rng], tag="l2v")
            sch.add("ve", lambda e: e.scalar_tensor_tensor(RROW, MROW, -1.0, MROW, ALU.mult, ALU.mult),
                    reads=[rows_rng], writes=[rows_rng], tag="l2m2")
            sch.add("ve", lambda e: e.tensor_add(VROW, VROW, RROW),
                    reads=[rows_rng], writes=[rows_rng], tag="l2v2")
            sch.add("se", lambda e: e.activation(RROW, VROW, AF.Ln, bias=CEPS[0:1, :]),
                    reads=[rows_rng, RCEPS], writes=[rows_rng], tag="l2ln")
            sch.add("se", lambda e: e.activation(RROW, RROW, AF.Exp, scale=-0.5),
                    reads=[rows_rng], writes=[rows_rng], tag="l2ex")
            sch.add("ve", lambda e: e.tensor_copy(ABR, RROW), reads=[rows_rng], writes=[rows_rng], tag="l2a")
            sch.add("ve", lambda e: e.tensor_copy(BBR, MROW), reads=[rows_rng], writes=[rows_rng], tag="l2b")
            for th in range(2):
                sch.add("te", lambda e, th=th: e.matmul(
                    PSA[:, th * 512:(th + 1) * 512], ONESR, BBR[0:1, th * 512:(th + 1) * 512],
                    start=True, stop=True),
                    reads=[rows_rng, R(o_onesr, 256)],
                    writes=[pr("PSA", th * 512, (th + 1) * 512)], tag="l2b1")
                sch.add("te", lambda e, th=th: e.matmul(
                    PSB[:, th * 512:(th + 1) * 512], ONESR, ABR[0:1, th * 512:(th + 1) * 512],
                    start=True, stop=True),
                    reads=[rows_rng, R(o_onesr, 256)],
                    writes=[pr("PSB", th * 512, (th + 1) * 512)], tag="l2b2")
            MB = f32(o_mb, TOK)
            RB = f32(o_rb, TOK)
            sch.add("ve", lambda e: e.tensor_copy(MB, PSA[:, 0:TOK]),
                    reads=[pr("PSA", 0, 1024)], writes=[R(o_mb, TOK * 4)], tag="l2me")
            sch.add("ve", lambda e: e.tensor_copy(RB, PSB[:, 0:TOK]),
                    reads=[pr("PSB", 0, 1024)], writes=[R(o_rb, TOK * 4)], tag="l2re")
            TL = f32(o_tl, TOK)
            for b in range(NDB):
                xbb = b16(o_dst + b * TOK * 2, TOK)
                sch.add("ve", lambda e, b=b: e.tensor_sub(TL, f32(o_src + b * TOK * 4, TOK), MB),
                        reads=[R(o_src + b * TOK * 4, TOK * 4), R(o_mb, TOK * 4)],
                        writes=[R(o_tl, TOK * 4)], tag="l2n1")
                sch.add("ve", lambda e, d=xbb: e.tensor_mul(d, TL, RB),
                        reads=[R(o_tl, TOK * 4), R(o_rb, TOK * 4)],
                        writes=[R(o_dst + b * TOK * 2, TOK * 2)], tag="l2n2")
            rg.pop()

        ln_resident(o_xres, o_xb2, "x1")
        ffn(o_xb2, o_xres, "x1")

        ln_resident(o_xres, o_xb2, "x2")
        for b in range(NDB):
            sch.add("sy", lambda e, b=b: e.dma_start(
                out=x2_s[b], in_=f32(o_xres + b * TOK * 4, TOK)),
                reads=[R(o_xres + b * TOK * 4, TOK * 4)], writes=[("x2_s", b, b + 1)],
                dma=True, tag="x2st", dkey="x2st")
        o_qp = o_xres
        o_kp = o_xres + NDB * TOK * 2
        rg.push()
        o_v = rg.alloc(NDB * TOK * 2)
        o_att = o_v
        o_wq = [rg.alloc(16 * P * 2) for _ in range(2)]
        o_kts = rg.alloc(TOK * 2)
        o_vts = rg.alloc(TOK * 2)
        o_kv = rg.alloc(AH * P * 4)
        o_ks = rg.alloc(AH * 4)
        o_kvb = rg.alloc(AH * P * 2)
        o_ksb = rg.alloc(AH * 2)
        o_zr = rg.alloc(TOK * 4)
        o_zib = rg.alloc(TOK * 4)
        o_tm = rg.alloc(512 * 4)
        o_x2b = [rg.alloc(512 * 4) for _ in range(2)]

        def qkv(wsrc, wkey, bias_off, o_dst, elu):
            for mb in range(16):
                slot = mb % 2
                WQB = b16(o_wq[slot], 16 * P)
                sch.add("ge", lambda e, d=WQB, mb=mb: e.dma_start(
                        out=d.rearrange("p (kb m) -> p kb m", kb=16), in_=wsrc[mb].transpose([1, 0, 2])),
                        reads=[(wkey, mb, mb + 1)], writes=[R(o_wq[slot], 16 * P * 2)],
                        dma=True, tag="qld")
                for kbi in range(16):
                    xnb = b16(o_xb2 + kbi * TOK * 2, TOK)
                    for th in range(2):
                        PT = PSA if th == 0 else PSB
                        pb = "PSA" if th == 0 else "PSB"
                        sch.add("te", lambda e, PT=PT, th=th, kbi=kbi, W=WQB, s=xnb: e.matmul(
                            PT[:, 0:512], W[:, kbi * P:(kbi + 1) * P],
                            s[:, th * 512:(th + 1) * 512],
                            start=(kbi == 0), stop=(kbi == 15)),
                            reads=[R(o_wq[slot], 16 * P * 2), R(o_xb2 + kbi * TOK * 2, TOK * 2)],
                            writes=[pr(pb, 0, 512)], tag="qmm")
                for th in range(2):
                    PT = PSA if th == 0 else PSB
                    pb = "PSA" if th == 0 else "PSB"
                    dst = b16(o_dst + mb * TOK * 2 + th * 512 * 2, 512)
                    bc = f32(bias_off, 16)
                    if elu:
                        TM = f32(o_tm, 512)
                        sch.add("ve", lambda e, PT=PT, mb=mb: e.tensor_scalar(
                            TM, PT[:, 0:512], bc[:, mb:mb + 1], 0.0, ALU.add, ALU.min),
                            reads=[pr(pb, 0, 512), R(bias_off, 64)], writes=[R(o_tm, 2048)], tag="el1")
                        sch.add("se", lambda e: e.activation(TM, TM, AF.Exp),
                                reads=[R(o_tm, 2048)], writes=[R(o_tm, 2048)], tag="el2")
                        sch.add("ve", lambda e, d=dst, PT=PT, mb=mb: e.tensor_scalar(
                            d, PT[:, 0:512], bc[:, mb:mb + 1], 0.0, ALU.add, ALU.max),
                            reads=[pr(pb, 0, 512), R(bias_off, 64)],
                            writes=[R(o_dst + mb * TOK * 2 + th * 1024, 1024)], tag="el3")
                        sch.add("ve", lambda e, d=dst: e.tensor_add(d, d, TM),
                                reads=[R(o_dst + mb * TOK * 2 + th * 1024, 1024), R(o_tm, 2048)],
                                writes=[R(o_dst + mb * TOK * 2 + th * 1024, 1024)], tag="el4")
                    else:
                        sch.add("ve", lambda e, d=dst, PT=PT, mb=mb: e.tensor_scalar_add(
                            d, PT[:, 0:512], bc[:, mb:mb + 1]),
                            reads=[pr(pb, 0, 512), R(bias_off, 64)],
                            writes=[R(o_dst + mb * TOK * 2 + th * 1024, 1024)], tag="vev")

        qkv(kw, "kw", o_kb, o_kp, elu=True)
        qkv(vw, "vw", o_vb, o_v, elu=False)

        for h in range(AH):
            for j in range(8):
                sch.add("te", lambda e, h=h, j=j: e.transpose(
                    PTA[:, j * P:(j + 1) * P], b16(o_kp + h * TOK * 2 + j * P * 2, P), ID128b),
                    reads=[R(o_kp + h * TOK * 2, TOK * 2), R(o_id128b, 256)],
                    writes=[pr("PTA", j * P, (j + 1) * P)], tag="ktt")
                sch.add("te", lambda e, h=h, j=j: e.transpose(
                    PTB[:, j * P:(j + 1) * P], b16(o_v + h * TOK * 2 + j * P * 2, P), ID128b),
                    reads=[R(o_v + h * TOK * 2, TOK * 2), R(o_id128b, 256)],
                    writes=[pr("PTB", j * P, (j + 1) * P)], tag="vtt")
            KTS = b16(o_kts, TOK)
            VTS = b16(o_vts, TOK)
            sch.add("ve", lambda e: e.tensor_copy(KTS, PTA[:, 0:1024]),
                    reads=[pr("PTA", 0, 1024)], writes=[R(o_kts, TOK * 2)], tag="kte")
            sch.add("ve", lambda e: e.tensor_copy(VTS, PTB[:, 0:1024]),
                    reads=[pr("PTB", 0, 1024)], writes=[R(o_vts, TOK * 2)], tag="vte")
            for j in range(8):
                sch.add("te", lambda e, j=j: e.matmul(
                    PSC[:, 0:128], KTS[:, j * P:(j + 1) * P], VTS[:, j * P:(j + 1) * P],
                    start=(j == 0), stop=(j == 7)),
                    reads=[R(o_kts, TOK * 2), R(o_vts, TOK * 2)],
                    writes=[pr("PSC", 0, 128)], tag="kvmm")
            sch.add("ve", lambda e, h=h: e.tensor_copy(f32(o_kv + h * P * 4, P), PSC[:, 0:128]),
                    reads=[pr("PSC", 0, 128)], writes=[R(o_kv + h * P * 4, P * 4)], tag="kve")
            sch.add("ve", lambda e, h=h: e.reduce_sum(
                f32(o_ks + h * 4, 1), b16(o_kp + h * TOK * 2, TOK), axis=AX.X),
                reads=[R(o_kp + h * TOK * 2, TOK * 2)], writes=[R(o_ks + h * 4, 4)], tag="ksum")
        sch.add("sy", lambda e: e.dma_start(out=ar_in[:, 0:2048], in_=f32(o_kv, AH * P)),
                reads=[R(o_kv, AH * P * 4)], writes=[("ar_in", 0, 1)], dma=True, tag="arw1")
        sch.add("sy", lambda e: e.dma_start(out=ar_in[:, 2048:2064], in_=f32(o_ks, AH)),
                reads=[R(o_ks, AH * 4)], writes=[("ar_in", 1, 2)], dma=True, tag="arw2")
        sch.add("ge", lambda e: e.collective_compute(
            "AllReduce", ALU.add, RG, [ar_in.opt()], [ar_out.opt()]),
            reads=[("ar_in", 0, 2)], writes=[("ar_out", 0, 2)], tag="ar")
        qkv(qw, "qw", o_qb, o_qp, elu=True)
        sch.add("sy", lambda e: e.dma_start(out=f32(o_kv, AH * P), in_=ar_out[:, 0:2048]),
                reads=[("ar_out", 0, 2)], writes=[R(o_kv, AH * P * 4)], dma=True, tag="arr1")
        sch.add("sy", lambda e: e.dma_start(out=f32(o_ks, AH), in_=ar_out[:, 2048:2064]),
                reads=[("ar_out", 0, 2)], writes=[R(o_ks, AH * 4)], dma=True, tag="arr2")
        sch.add("se", lambda e: e.copy(b16(o_kvb, AH * P), f32(o_kv, AH * P)),
                reads=[R(o_kv, AH * P * 4)], writes=[R(o_kvb, AH * P * 2)], tag="kvb")
        sch.add("ve", lambda e: e.tensor_copy(b16(o_ksb, AH), f32(o_ks, AH)),
                reads=[R(o_ks, AH * 4)], writes=[R(o_ksb, AH * 2)], tag="ksb")

        for h in range(AH):
            for th in range(2):
                sch.add("te", lambda e, h=h, th=th: e.matmul(
                    PSA[0:1, th * 512:(th + 1) * 512], b16(o_ksb + h * 2, 1),
                    b16(o_qp + h * TOK * 2 + th * 1024, 512), start=True, stop=True),
                    reads=[R(o_ksb, AH * 2), R(o_qp + h * TOK * 2, TOK * 2)],
                    writes=[pr("PSA", th * 512, (th + 1) * 512)], tag="zmm")
            ZR = f32(o_zr, TOK, 0, 1)
            sch.add("ve", lambda e: e.tensor_scalar_add(ZR, PSA[0:1, 0:TOK], 1e-6),
                    reads=[pr("PSA", 0, 1024)], writes=[R(o_zr, TOK * 4)], tag="zeps")
            sch.add("se", lambda e: e.activation(ZR, ZR, AF.Ln),
                    reads=[R(o_zr, TOK * 4)], writes=[R(o_zr, TOK * 4)], tag="zln")
            sch.add("se", lambda e: e.activation(ZR, ZR, AF.Exp, scale=-1.0),
                    reads=[R(o_zr, TOK * 4)], writes=[R(o_zr, TOK * 4)], tag="zrec")
            ZIBS = f32(o_zib, TOK)
            for th in range(2):
                sch.add("te", lambda e, th=th: e.matmul(
                    PSB[:, th * 512:(th + 1) * 512], ONESRF,
                    ZR[0:1, th * 512:(th + 1) * 512], start=True, stop=True),
                    reads=[R(o_zr, TOK * 4), R(o_onesrf, 512)],
                    writes=[pr("PSB", th * 512, (th + 1) * 512)], tag="zbc")
                sch.add("te", lambda e, h=h, th=th: e.matmul(
                    PSA[:, th * 512:(th + 1) * 512], b16(o_kvb + h * P * 2, P),
                    b16(o_qp + h * TOK * 2 + th * 1024, 512), start=True, stop=True),
                    reads=[R(o_kvb, AH * P * 2), R(o_qp + h * TOK * 2, TOK * 2)],
                    writes=[pr("PSA", th * 512, (th + 1) * 512)], tag="atmm")
            sch.add("ve", lambda e: e.tensor_copy(ZIBS, PSB[:, 0:1024]),
                    reads=[pr("PSB", 0, 1024)], writes=[R(o_zib, TOK * 4)], tag="zev")
            sch.add("ve", lambda e, h=h: e.tensor_mul(
                b16(o_att + h * TOK * 2, TOK), PSA[:, 0:1024], ZIBS),
                reads=[pr("PSA", 0, 1024), R(o_zib, TOK * 4)],
                writes=[R(o_att + h * TOK * 2, TOK * 2)], tag="atev")

        for mb in range(16):
            slot = mb % 2
            WQB = b16(o_wq[slot], 16 * P)
            sch.add("ge", lambda e, d=WQB, mb=mb: e.dma_start(
                    out=d.rearrange("p (kb m) -> p kb m", kb=16), in_=ow[mb].transpose([1, 0, 2])),
                    reads=[("ow", mb, mb + 1)], writes=[R(o_wq[slot], 16 * P * 2)],
                    dma=True, tag="old")
            for kbi in range(16):
                for th in range(2):
                    sch.add("te", lambda e, th=th, kbi=kbi, W=WQB: e.matmul(
                        PSB[:, th * 512:(th + 1) * 512], W[:, kbi * P:(kbi + 1) * P],
                        b16(o_att + kbi * TOK * 2 + th * 1024, 512),
                        start=(kbi == 0), stop=(kbi == 15)),
                        reads=[R(o_wq[slot], 16 * P * 2), R(o_att + kbi * TOK * 2, TOK * 2)],
                        writes=[pr("PSB", th * 512, (th + 1) * 512)], tag="omm")
            for th in range(2):
                pb = "PSB"
                X2B = f32(o_x2b[slot], 512)
                sch.add("sy", lambda e, d=X2B, mb=mb, th=th: e.dma_start(
                    out=d, in_=x2_s[mb, :, th * 512:(th + 1) * 512]),
                    reads=[("x2_s", mb, mb + 1)], writes=[R(o_x2b[slot], 2048)],
                    dma=True, tag="x2ld")
                dst = f32(o_xres + mb * TOK * 4 + th * 512 * 4, 512)
                sch.add("ve", lambda e, d=dst, s=X2B, mb=mb, th=th: e.scalar_tensor_tensor(
                    d, PSB[:, th * 512:(th + 1) * 512], OBc[:, mb:mb + 1], s, ALU.add, ALU.add),
                    reads=[pr(pb, th * 512, (th + 1) * 512), R(o_obi, 64), R(o_x2b[slot], 2048)],
                    writes=[R(o_xres + mb * TOK * 4 + th * 2048, 2048)], tag="x3")
        rg.pop()

        ln_resident(o_xres, o_xb2, "x3")
        ffn(o_xb2, o_xres, "x3")
        for b in range(NDB):
            sch.add("sy", lambda e, b=b: e.dma_start(
                out=out[b * P:(b + 1) * P, :], in_=f32(o_xres + b * TOK * 4, TOK)),
                reads=[R(o_xres + b * TOK * 4, TOK * 4)], writes=[("out", b, b + 1)],
                dma=True, tag="outst", dkey="outst")

        n_waits = sch.emit(nc, block)
        print(f"[build] ops={len(sch.ops)} waits={n_waits} sbuf_peak={rg.peak}")
    return nc


_NC_CACHE = {}


def kernel(**inputs):
    f = np.float32
    x = np.asarray(inputs["L0_attn_a_hidden_states"], f)
    Win = np.asarray(inputs["L0_attn_a_W_in"], f)
    cw = np.asarray(inputs["L0_attn_a_conv_w"], f)
    cb = np.asarray(inputs["L0_attn_a_conv_b"], f)
    A = np.asarray(inputs["L0_attn_a_A"], f)
    D = np.asarray(inputs["L0_attn_a_D"], f)
    dtb = np.asarray(inputs["L0_attn_a_dt_bias"], f)
    nw = np.asarray(inputs["L0_attn_a_norm_w"], f)
    Wout = np.asarray(inputs["L0_attn_a_W_out"], f)
    qwv = np.asarray(inputs["L0_attn_b_q_w"], f)
    qbv = np.asarray(inputs["L0_attn_b_q_b"], f)
    kwv = np.asarray(inputs["L0_attn_b_k_w"], f)
    kbv = np.asarray(inputs["L0_attn_b_k_b"], f)
    vwv = np.asarray(inputs["L0_attn_b_v_w"], f)
    vbv = np.asarray(inputs["L0_attn_b_v_b"], f)
    owv = np.asarray(inputs["L0_attn_b_o_w"], f)
    obv = np.asarray(inputs["L0_attn_b_o_b"], f)
    gwq = np.asarray(inputs["L0_ffn_gate_w"])
    gsc = np.asarray(inputs["L0_ffn_gate_scale"], f)
    uwq = np.asarray(inputs["L0_ffn_up_w"])
    usc = np.asarray(inputs["L0_ffn_up_scale"], f)
    dwq = np.asarray(inputs["L0_ffn_down_w"])
    dsc = np.asarray(inputs["L0_ffn_down_scale"], f)

    C = np.ascontiguousarray

    Winp = np.zeros((DM, DIPP), f)
    Winp[:, :DIP] = Win
    win_t = C(Winp.reshape(16, 128, 67, 128).transpose(2, 0, 1, 3))
    wout_t = C(Wout.reshape(32, 128, 16, 128).transpose(2, 0, 1, 3))

    def qk_tiles(w):
        return C(w.T.reshape(16, 128, 16, 128).transpose(2, 0, 1, 3))

    qw_t, kw_t, vw_t, ow_t = (qk_tiles(w) for w in (qwv, kwv, vwv, owv))
    gw_t = C((gwq.astype(np.uint8) ^ 128).view(np.int8).T.reshape(16, 128, 64, 128).transpose(2, 0, 1, 3))
    uw_t = C((uwq.astype(np.uint8) ^ 128).view(np.int8).T.reshape(16, 128, 64, 128).transpose(2, 0, 1, 3))
    dw_t = C((dwq.astype(np.uint8) ^ 128).view(np.int8).T.reshape(4, 16, 128, 16, 128).transpose(0, 3, 1, 2, 4))

    convw_t = C(cw.reshape(NCB, 128, DCONV).transpose(1, 0, 2).reshape(128, NCB * DCONV))
    convb_t = C(cb.reshape(NCB, 128).T)
    hp = np.arange(128)
    dcol_t = C(D[2 * np.arange(32)[None, :] + (hp[:, None] // 64)])
    nwc_t = C(nw.reshape(32, 128).T)
    gsc_t = C(gsc.reshape(64, 128).T)
    usc_t = C(usc.reshape(64, 128).T)
    dsc_t = C(dsc.reshape(16, 128).T)
    qb_t = C(qbv.reshape(16, 128).T); kb_t = C(kbv.reshape(16, 128).T)
    vb_t = C(vbv.reshape(16, 128).T); ob_t = C(obv.reshape(16, 128).T)

    shared = dict(
        win=win_t, wout=wout_t, qw=qw_t, kw=kw_t, vw=vw_t, ow=ow_t,
        gw=gw_t, uw=uw_t, dw=dw_t,
        convw=convw_t, convb=convb_t,
        avec=C(A.reshape(NH, 1)), dtbv=C(dtb.reshape(NH, 1)),
        dcol=dcol_t, drow=C(D.reshape(1, 64)), nwc=nwc_t, gsc=gsc_t, usc=usc_t, dsc=dsc_t,
        qb=qb_t, kb=kb_t, vb=vb_t, ob=ob_t,
    )

    in_maps = []
    for core in range(8):
        b, q = core // 4, core % 4
        t0 = q * TOK
        xs = np.zeros((COLS, DM), f)
        lo = max(t0 - HALO, 0)
        xs[HALO - (t0 - lo):] = x[b, lo:t0 + TOK]
        m = dict(shared)
        m["xin"] = C(xs.T)
        mm = np.zeros((4, 4), f)
        cm = np.zeros((4, 1), f)
        for i in range(4):
            cm[i, 0] = 1.0 if i < q else 0.0
            for k in range(4):
                mm[k, i] = 1.0 if (i < k < q) else 0.0
        m["mmask"] = mm
        m["cmask"] = cm
        in_maps.append(m)

    if "nc" not in _NC_CACHE:
        _NC_CACHE["nc"] = build_program()
    nc = _NC_CACHE["nc"]
    res = run_bass_kernel_spmd(nc, in_maps, core_ids=list(range(8)))

    outp = np.empty((2, 4096, DM), f)
    for core in range(8):
        b, q = core // 4, core % 4
        outp[b, q * TOK:(q + 1) * TOK] = res.results[core]["out"].T
    if DBG:
        kernel.dbg = res.results
    return outp


# version 28 (final)
# speedup vs baseline: 1.0842x; 1.0842x over previous
import numpy as np
import ml_dtypes
import concourse.bass as bass
import concourse.mybir as mybir
from concourse.bass_utils import run_bass_kernel_spmd

F32 = mybir.dt.float32
BF16 = mybir.dt.bfloat16
I32 = mybir.dt.int32
U8 = mybir.dt.uint8
I8 = mybir.dt.int8
AF = mybir.ActivationFunctionType
ALU = mybir.AluOpType
AX = mybir.AxisListType

P = 128
TOK = 1024
HALO = 4
COLS = TOK + HALO
DM = 2048
NDB = DM // P
DIN = 4096
NH = 64
HD = 64
DST = 128
DCONV = 4
CH = 128
NCH = TOK // CH
DIP = 8512
DIPP = 67 * 128
CONVD = 4352
NCB = 34
FI = 8192
NIB = 64
AH = 16
SELF_SYNC = True
DBG = False

COMPUTE = ("te", "ve", "se", "ge")
STREAMS = ("te", "ve", "se", "ge", "sy")


class Sched:

    def __init__(self):
        self.ops = []
        self.wr = {}
        self.rd = {}
        self.PAGE = 4096

    def _pages(self, lo, hi):
        return range(lo // self.PAGE, (max(hi - 1, lo)) // self.PAGE + 1)

    PSUM_BANK = {"PSA": 512, "PSB": 512, "PSC": 512, "PTA": 1024, "PTB": 1024}

    def _bankify(self, ranges):
        out = []
        for (b, lo, hi) in ranges:
            bk = self.PSUM_BANK.get(b)
            if bk:
                lo = (lo // bk) * bk
                hi = ((hi + bk - 1) // bk) * bk
            out.append((b, lo, hi))
        return out

    def add(self, stream, fn, reads=(), writes=(), dma=False, tag="", dkey=None):
        reads = self._bankify(reads)
        writes = self._bankify(writes)
        opi = len(self.ops)
        deps = set()
        for (b, lo, hi) in reads:
            m = self.wr.get(b)
            if m:
                for pg in self._pages(lo, hi):
                    for (l2, h2, oi) in m.get(pg, ()):
                        if l2 < hi and lo < h2:
                            deps.add(oi)
        for (b, lo, hi) in writes:
            m = self.wr.get(b)
            if m:
                for pg in self._pages(lo, hi):
                    for (l2, h2, oi) in m.get(pg, ()):
                        if l2 < hi and lo < h2:
                            deps.add(oi)
            m = self.rd.get(b)
            if m:
                for pg in self._pages(lo, hi):
                    for (l2, h2, oi) in m.get(pg, ()):
                        if l2 < hi and lo < h2:
                            deps.add(oi)
        for (b, lo, hi) in writes:
            wm = self.wr.setdefault(b, {})
            rm = self.rd.setdefault(b, {})
            for pg in self._pages(lo, hi):
                lst = wm.setdefault(pg, [])
                lst[:] = [e for e in lst if not (lo <= e[0] and e[1] <= hi)]
                lst.append((lo, hi, opi))
                rl = rm.get(pg)
                if rl:
                    rl[:] = [e for e in rl if not (lo <= e[0] and e[1] <= hi)]
        for (b, lo, hi) in reads:
            rm = self.rd.setdefault(b, {})
            for pg in self._pages(lo, hi):
                rm.setdefault(pg, []).append((lo, hi, opi))
        if dma and dkey is None:
            for (b, lo, hi) in list(writes) + list(reads):
                if b == "sbuf" or isinstance(b, str) and b.startswith("PS") or b in ("PTA", "PTB"):
                    dkey = ("a", b, lo)
                    break
            if dkey is None:
                dkey = ("a", "none", opi)
        elif dma:
            dkey = ("b", dkey)
        self.ops.append(dict(stream=stream, fn=fn, deps=deps, dma=dma,
                             tag=tag, inc=False, cnt=0, dkey=dkey))
        return opi

    def emit(self, nc, block):
        ops = self.ops
        for op in ops:
            kept = set()
            for d in op["deps"]:
                dop = ops[d]
                if dop["dma"]:
                    kept.add(d)
                elif dop["stream"] != op["stream"]:
                    kept.add(d)
                elif SELF_SYNC and dop["stream"] != "te":
                    kept.add(d)
            op["deps"] = kept
            for d in kept:
                ops[d]["inc"] = True
        dk_cnt = {}
        dk_positions = {}
        cs_cnt = {s: 0 for s in STREAMS}
        for opi, op in enumerate(ops):
            if op["dma"]:
                k = op["dkey"]
                dk_cnt[k] = dk_cnt.get(k, 0) + 1
                op["cnt"] = dk_cnt[k] * 16
                dk_positions.setdefault(k, []).append(opi)
            elif op["inc"]:
                cs_cnt[op["stream"]] += 1
                op["cnt"] = cs_cnt[op["stream"]]
        sems = {}
        for s in STREAMS:
            sems[("c", s)] = nc.alloc_semaphore(f"sem_c_{s}")
        for ki, k in enumerate(dk_cnt):
            sems[k] = nc.alloc_semaphore(f"sem_d{ki}")
        n_waits = [0]

        import bisect

        def body_for(name):
            def body(e):
                water = {}
                for opi, op in enumerate(ops):
                    if op["stream"] != name:
                        continue
                    needed = {}
                    for d in op["deps"]:
                        dop = ops[d]
                        if dop["dma"]:
                            key = dop["dkey"]
                            if key[0] == "b":
                                v = 16 * bisect.bisect_right(dk_positions[key], opi)
                            else:
                                v = dop["cnt"]
                        else:
                            key = ("c", dop["stream"])
                            v = dop["cnt"]
                        if v > needed.get(key, 0):
                            needed[key] = v
                    for key, v in sorted(needed.items(), key=str):
                        if v > water.get(key, 0):
                            e.wait_ge(sems[key], v)
                            water[key] = v
                            n_waits[0] += 1
                    inst = op["fn"](e)
                    if op["dma"]:
                        inst.then_inc(sems[op["dkey"]], 16)
                    elif op["inc"]:
                        inst.then_inc(sems[("c", op["stream"])], 1)
                if name == "sy":
                    for k, c in dk_cnt.items():
                        e.wait_ge(sems[k], 16 * c)
            return body

        block.sync(body_for("sy"))
        block.vector(body_for("ve"))
        block.scalar(body_for("se"))
        block.tensor(body_for("te"))
        block.gpsimd(body_for("ge"))
        return n_waits[0]


class Regions:

    def __init__(self, nbytes):
        self.nbytes = nbytes
        self.off = 0
        self.peak = 0
        self.marks = []

    def alloc(self, n, align=512):
        o = (self.off + align - 1) // align * align
        self.off = o + n
        self.peak = max(self.peak, self.off)
        assert self.off <= self.nbytes, f"SBUF arena overflow: {self.off} > {self.nbytes}"
        return o

    def push(self):
        self.marks.append(self.off)

    def pop(self):
        self.off = self.marks.pop()


def build_program():
    nc = bass.Bass("TRN2", target_bir_lowering=False)
    sch = Sched()

    def din(name, shape, dtype):
        return nc.dram_tensor(name, shape, dtype, kind="ExternalInput").ap()

    xin = din("xin", [DM, COLS], F32)
    win = din("win", [67, 16, P, P], F32)
    convw = din("convw", [P, NCB * DCONV], F32)
    convb = din("convb", [P, NCB], F32)
    avec = din("avec", [NH, 1], F32)
    dtbv = din("dtbv", [NH, 1], F32)
    dcol = din("dcol", [P, 32], F32)
    drow = din("drow", [1, NH], F32)
    nwc = din("nwc", [P, 32], F32)
    wout = din("wout", [16, 32, P, P], F32)
    qw = din("qw", [16, 16, P, P], F32)
    kw = din("kw", [16, 16, P, P], F32)
    vw = din("vw", [16, 16, P, P], F32)
    ow = din("ow", [16, 16, P, P], F32)
    qb = din("qb", [P, 16], F32)
    kb = din("kb", [P, 16], F32)
    vb = din("vb", [P, 16], F32)
    ob = din("ob", [P, 16], F32)
    gw = din("gw", [NIB, 16, P, P], I8)
    uw = din("uw", [NIB, 16, P, P], I8)
    dw = din("dw", [4, 16, 16, P, P], I8)
    gsc = din("gsc", [P, NIB], F32)
    usc = din("usc", [P, NIB], F32)
    dsc = din("dsc", [P, 16], F32)
    mmask = din("mmask", [4, 4], F32)
    cmask = din("cmask", [4, 1], F32)

    out = nc.dram_tensor("out", [DM, TOK], F32, kind="ExternalOutput").ap()

    kind_s = "ExternalOutput" if DBG else "Internal"
    xbc_s = nc.dram_tensor("xbc_s", [NCB, P, TOK], BF16, kind=kind_s).ap()
    z_s = nc.dram_tensor("z_s", [32, P, TOK], BF16, kind=kind_s).ap()
    x1_s = nc.dram_tensor("x1_s", [16, P, TOK], F32, kind=kind_s).ap()
    xt_s = nc.dram_tensor("xt_s", [NCH, P, DIN], BF16, kind="Internal").ap()
    x2_s = nc.dram_tensor("x2_s", [16, P, TOK], F32, kind=kind_s).ap()
    ag_in = nc.dram_tensor("ag_in", [P, 4224], BF16, kind="Internal").ap()
    ag_out = nc.dram_tensor("ag_out", [4 * P, 4224], BF16, kind="Internal").ap()
    co_d = nc.dram_tensor("co_d", [4, NH], F32, kind="Internal").ap()
    ar_in = nc.dram_tensor("ar_in", [P, 2064], F32, kind="Internal").ap()
    ar_out = nc.dram_tensor("ar_out", [P, 2064], F32, kind="Internal").ap()
    if DBG:
        dbg_dt = nc.dram_tensor("dbg_dt", [NH, TOK], F32, kind="ExternalOutput").ap()
        dbg_y = nc.dram_tensor("dbg_y", [P, 4096 * NCH], BF16, kind="ExternalOutput").ap()
        dbg_xt = nc.dram_tensor("dbg_xt", [P, 4096], BF16, kind="ExternalOutput").ap()
        dbg_m = nc.dram_tensor("dbg_m", [P, 8192], BF16, kind="ExternalOutput").ap()
        dbg_sin = nc.dram_tensor("dbg_sin", [P, 4096 * NCH], BF16, kind="ExternalOutput").ap()
        dbg_slc = nc.dram_tensor("dbg_slc", [P, 4096 * 2], F32, kind="ExternalOutput").ap()
        dbg_g = nc.dram_tensor("dbg_g", [P, 4096], BF16, kind="ExternalOutput").ap()
        dbg_yn = nc.dram_tensor("dbg_yn", [P, 32 * 512], BF16, kind="ExternalOutput").ap()
        dbg_csc = nc.dram_tensor("dbg_csc", [P, 8192], BF16, kind="ExternalOutput").ap()
        dbg_exc = nc.dram_tensor("dbg_exc", [NH, TOK], BF16, kind="ExternalOutput").ap()
        dbg_xc8 = nc.dram_tensor("dbg_xc8", [NH, TOK], F32, kind="ExternalOutput").ap()

    RG = [[0, 1, 2, 3], [4, 5, 6, 7]]

    SB = "sbuf"

    ARENA_BYTES = 212480
    rg = Regions(ARENA_BYTES)

    with (
        nc.sbuf_tensor("AR", [P, ARENA_BYTES // 4], F32) as AR,
        nc.psum_tensor("PSA", [P, 1536], F32) as PSA,
        nc.psum_tensor("PSB", [P, 1024], F32) as PSB,
        nc.psum_tensor("PSC", [P, 512], F32) as PSC,
        nc.psum_tensor("PTA", [P, 1024], BF16) as PTA,
        nc.psum_tensor("PTB", [P, 1024], BF16) as PTB,
        nc.Block() as block,
    ):
        arf = AR[:].bitcast(BF16)

        def f32(off, n, p0=0, pn=P):
            assert off % 4 == 0
            return AR[p0:p0 + pn, off // 4: off // 4 + n]

        def b16(off, n, p0=0, pn=P):
            assert off % 2 == 0
            return arf[p0:p0 + pn, off // 2: off // 2 + n]

        def R(off, nbytes):
            return (SB, off, off + nbytes)

        o_id128b = rg.alloc(256)
        o_idf = rg.alloc(512)
        o_id64b = rg.alloc(128)
        o_mask = rg.alloc(1024)
        o_ones1 = rg.alloc(64)
        o_onesr = rg.alloc(256)
        o_onesrf = rg.alloc(512)
        o_onesf = rg.alloc(512)
        o_m128 = rg.alloc(256)
        o_convw = rg.alloc(NCB * DCONV * 4)
        o_convb = rg.alloc(NCB * 4)
        o_avec = rg.alloc(4)
        o_dtb = rg.alloc(4)
        o_dcol = rg.alloc(128)
        o_nwc = rg.alloc(128)
        o_qb = rg.alloc(64); o_kb = rg.alloc(64); o_vb = rg.alloc(64); o_obi = rg.alloc(64)
        o_gsc = rg.alloc(256); o_usc = rg.alloc(256); o_dsc = rg.alloc(64)
        o_dt = rg.alloc(TOK * 4)
        o_mmask = rg.alloc(16)
        o_cmask = rg.alloc(4)
        o_c1 = rg.alloc(4)
        o_ceps = rg.alloc(4)
        o_zcol = rg.alloc(64)
        o_drow = rg.alloc(256)
        o_dt64 = rg.alloc(128)

        ID128b = b16(o_id128b, 128)
        IDF = f32(o_idf, 128)
        ID64b = b16(o_id64b, 64, 0, 64)
        MASKR = b16(o_mask, 512)
        ONESC = b16(o_ones1, 1)
        ONESR = b16(o_onesr, 128, 0, 1)
        ONESRF = f32(o_onesrf, 128, 0, 1)
        ONESF = f32(o_onesf, 128, 0, 64)
        M128R = b16(o_m128, 128, 0, 1)
        CVW = f32(o_convw, NCB * DCONV)
        CVB = f32(o_convb, NCB)
        AC = f32(o_avec, 1, 0, NH)
        DTB = f32(o_dtb, 1, 0, NH)
        DCOL = f32(o_dcol, 32)
        NWC = f32(o_nwc, 32)
        QBc = f32(o_qb, 16); KBc = f32(o_kb, 16); VBc = f32(o_vb, 16); OBc = f32(o_obi, 16)
        GSC = f32(o_gsc, NIB); USC = f32(o_usc, NIB); DSC = f32(o_dsc, 16)
        DTS = f32(o_dt, TOK, 0, NH)
        MMK = f32(o_mmask, 4, 0, 4)
        CMK = f32(o_cmask, 1, 0, 4)
        C1 = f32(o_c1, 1)
        CEPS = f32(o_ceps, 1)
        ZCOL = b16(o_zcol, 1)
        DRW = f32(o_drow, NH, 0, 1)
        DT64B = b16(o_dt64, NH)
        RC1 = R(o_c1, 4)
        RCEPS = R(o_ceps, 4)

        PS = {"PSA": PSA, "PSB": PSB, "PSC": PSC, "PTA": PTA, "PTB": PTB}

        def pr(name, lo, hi):
            return (name, lo, hi)

        def load_const(dst_off, src_ap, nelem, dtype_bytes=4, p0=0, pn=P, name="c"):
            dst = f32(dst_off, nelem, p0, pn) if dtype_bytes == 4 else b16(dst_off, nelem, p0, pn)
            sch.add("se", lambda e, d=dst, s=src_ap: e.dma_start(out=d, in_=s),
                    reads=[], writes=[R(dst_off, nelem * dtype_bytes)], dma=True, tag=name,
                    dkey="const")

        load_const(o_convw, convw, NCB * DCONV)
        load_const(o_convb, convb, NCB)
        load_const(o_avec, avec, 1, p0=0, pn=NH)
        load_const(o_dtb, dtbv, 1, p0=0, pn=NH)
        load_const(o_dcol, dcol, 32)
        load_const(o_drow, drow, NH, p0=0, pn=1)
        load_const(o_nwc, nwc, 32)
        load_const(o_qb, qb, 16); load_const(o_kb, kb, 16)
        load_const(o_vb, vb, 16); load_const(o_obi, ob, 16)
        load_const(o_gsc, gsc, NIB); load_const(o_usc, usc, NIB)
        load_const(o_dsc, dsc, 16)
        load_const(o_mmask, mmask, 4, p0=0, pn=4)
        load_const(o_cmask, cmask, 1, p0=0, pn=4)

        def ge_init(fn, wr_rng, rd_rng=(), tag="init"):
            sch.add("ge", fn, reads=list(rd_rng), writes=[wr_rng], tag=tag)

        ge_init(lambda e: e.memset(ID128b, 1.0), R(o_id128b, 256))
        ge_init(lambda e: e.affine_select(out=ID128b, in_=ID128b, compare_op=ALU.is_equal,
                                          fill=0.0, base=0, pattern=[[-1, 128]], channel_multiplier=1),
                R(o_id128b, 256), [R(o_id128b, 256)])
        ge_init(lambda e: e.memset(IDF, 1.0), R(o_idf, 512))
        ge_init(lambda e: e.affine_select(out=IDF, in_=IDF, compare_op=ALU.is_equal,
                                          fill=0.0, base=0, pattern=[[-1, 128]], channel_multiplier=1),
                R(o_idf, 512), [R(o_idf, 512)])
        ge_init(lambda e: e.memset(ID64b, 1.0), R(o_id64b, 128))
        ge_init(lambda e: e.affine_select(out=ID64b, in_=ID64b, compare_op=ALU.is_equal,
                                          fill=0.0, base=0, pattern=[[-1, 64]], channel_multiplier=1),
                R(o_id64b, 128), [R(o_id64b, 128)])
        ge_init(lambda e: e.memset(MASKR, 0.0), R(o_mask, 1024))
        ge_init(lambda e: e.affine_select(
            out=MASKR.rearrange("p (r l) -> p r l", r=4), in_=MASKR.rearrange("p (r l) -> p r l", r=4),
            compare_op=ALU.is_ge, fill=-1e30, base=0, pattern=[[0, 4], [1, 128]],
            channel_multiplier=-1), R(o_mask, 1024), [R(o_mask, 1024)])
        ge_init(lambda e: e.memset(ONESC, 1.0), R(o_ones1, 64))
        ge_init(lambda e: e.memset(ONESR, 1.0), R(o_onesr, 256))
        ge_init(lambda e: e.memset(ONESRF, 1.0), R(o_onesrf, 512))
        ge_init(lambda e: e.memset(ONESF, 1.0), R(o_onesf, 512))
        ge_init(lambda e: e.memset(M128R, -128.0), R(o_m128, 256))
        ge_init(lambda e: e.memset(C1, 1.0), RC1)
        ge_init(lambda e: e.memset(ZCOL, 0.0), R(o_zcol, 64))
        sch.add("te", lambda e: e.matmul(PSC[:, 0:NH], ONESRF, DRW, start=True, stop=True),
                reads=[R(o_drow, 256), R(o_onesrf, 512)], writes=[pr("PSC", 0, NH)], tag="dt64m")
        sch.add("ve", lambda e: e.tensor_copy(DT64B, PSC[:, 0:NH]),
                reads=[pr("PSC", 0, NH)], writes=[R(o_dt64, 128)], tag="dt64e")
        ge_init(lambda e: e.memset(CEPS, 1e-5), RCEPS)

        def layernorm(xsrc_dram, o_xb, cols, xdram_key, dram_colspan=None):
            rg.push()
            o_tmp = rg.alloc(cols * 4)
            o_tmp2 = rg.alloc(cols * 4)
            o_sq = rg.alloc(cols * 2)
            o_rows = rg.alloc(6 * ((cols * 4 + 511) // 512) * 512)
            row_sz = (cols * 4 + 511) // 512 * 512
            o_mb = rg.alloc(cols * 4)
            o_rb = rg.alloc(cols * 4)
            o_tl = rg.alloc(cols * 4)

            ntile = [(t, min(512, cols - t)) for t in range(0, cols, 512)]
            for b in range(NDB):
                o_x = o_tmp if b % 2 == 0 else o_tmp2
                xt = f32(o_x, cols)
                sch.add("sy", lambda e, d=xt, s=xsrc_dram(b): e.dma_start(out=d, in_=s),
                        writes=[R(o_x, cols * 4)], reads=[(xdram_key, b, b + 1)], dma=True, tag="lnld")
                xbb = b16(o_xb + b * cols * 2, cols)
                sch.add("ve", lambda e, d=xbb, s=xt: e.tensor_copy(d, s),
                        reads=[R(o_x, cols * 4)], writes=[R(o_xb + b * cols * 2, cols * 2)], tag="lncp")
                sqb = b16(o_sq, cols)
                sch.add("se", lambda e, d=sqb, s=xbb: e.activation(d, s, AF.Square),
                        reads=[R(o_xb + b * cols * 2, cols * 2)], writes=[R(o_sq, cols * 2)], tag="lnsq")
                for (t0, tn) in ntile:
                    sch.add("te", lambda e, t0=t0, tn=tn, b=b, s=xbb: e.matmul(
                        PSA[0:1, t0:t0 + tn], ONESC, s[:, t0:t0 + tn],
                        start=(b == 0), stop=(b == NDB - 1)),
                        reads=[R(o_xb + b * cols * 2, cols * 2), R(o_ones1, 64)],
                        writes=[pr("PSA", t0, t0 + tn)], tag="lnsum")
                    sch.add("te", lambda e, t0=t0, tn=tn, b=b, s=sqb: e.matmul(
                        PSB[0:1, t0:t0 + tn] if t0 < 1024 else PSC[0:1, t0 - 1024:t0 - 1024 + tn],
                        ONESC, s[:, t0:t0 + tn],
                        start=(b == 0), stop=(b == NDB - 1)),
                        reads=[R(o_sq, cols * 2), R(o_ones1, 64)],
                        writes=[pr("PSB", t0, t0 + tn) if t0 < 1024 else pr("PSC", t0 - 1024, t0 - 1024 + tn)],
                        tag="lnsq2")
            MROW = f32(o_rows, cols, 0, 1)
            VROW = f32(o_rows + row_sz, cols, 0, 1)
            RROW = f32(o_rows + 2 * row_sz, cols, 0, 1)
            rows_rng = R(o_rows, 6 * row_sz)
            sch.add("ve", lambda e: e.tensor_scalar_mul(MROW, PSA[0:1, 0:cols], 1.0 / DM),
                    reads=[pr("PSA", 0, cols)], writes=[rows_rng], tag="lnmean")
            if cols <= 1024:
                sch.add("ve", lambda e: e.tensor_scalar_mul(VROW, PSB[0:1, 0:cols], 1.0 / DM),
                        reads=[pr("PSB", 0, cols)], writes=[rows_rng], tag="lnvar")
            else:
                sch.add("ve", lambda e: e.tensor_scalar_mul(VROW[0:1, 0:1024], PSB[0:1, 0:1024], 1.0 / DM),
                        reads=[pr("PSB", 0, 1024)], writes=[rows_rng], tag="lnvar")
                sch.add("ve", lambda e: e.tensor_scalar_mul(VROW[0:1, 1024:cols], PSC[0:1, 0:cols - 1024], 1.0 / DM),
                        reads=[pr("PSC", 0, cols - 1024)], writes=[rows_rng], tag="lnvar2")
            sch.add("ve", lambda e: e.scalar_tensor_tensor(RROW, MROW, -1.0, MROW, ALU.mult, ALU.mult),
                    reads=[rows_rng], writes=[rows_rng], tag="lnm2")
            sch.add("ve", lambda e: e.tensor_add(VROW, VROW, RROW),
                    reads=[rows_rng], writes=[rows_rng], tag="lnv2")
            sch.add("se", lambda e: e.activation(RROW, VROW, AF.Ln, bias=CEPS[0:1, :]),
                    reads=[rows_rng, RCEPS], writes=[rows_rng], tag="lnln")
            sch.add("se", lambda e: e.activation(RROW, RROW, AF.Exp, scale=-0.5),
                    reads=[rows_rng], writes=[rows_rng], tag="lnexp")
            for (t0, tn) in ntile:
                sch.add("te", lambda e, t0=t0, tn=tn: e.matmul(
                    PSA[:, t0:t0 + tn], ONESRF, MROW[0:1, t0:t0 + tn], start=True, stop=True),
                    reads=[rows_rng, R(o_onesrf, 512)], writes=[pr("PSA", t0, t0 + tn)], tag="lnbc1")
                sch.add("te", lambda e, t0=t0, tn=tn: e.matmul(
                    PSB[:, t0:t0 + tn] if t0 < 1024 else PSC[:, t0 - 1024:t0 - 1024 + tn],
                    ONESRF, RROW[0:1, t0:t0 + tn], start=True, stop=True),
                    reads=[rows_rng, R(o_onesrf, 512)],
                    writes=[pr("PSB", t0, t0 + tn) if t0 < 1024 else pr("PSC", t0 - 1024, t0 - 1024 + tn)],
                    tag="lnbc2")
            MB = f32(o_mb, cols)
            RB = f32(o_rb, cols)
            sch.add("ve", lambda e: e.tensor_copy(MB, PSA[:, 0:cols]),
                    reads=[pr("PSA", 0, cols)], writes=[R(o_mb, cols * 4)], tag="lnmbe")
            if cols <= 1024:
                sch.add("ve", lambda e: e.tensor_copy(RB, PSB[:, 0:cols]),
                        reads=[pr("PSB", 0, cols)], writes=[R(o_rb, cols * 4)], tag="lnrbe")
            else:
                sch.add("ve", lambda e: e.tensor_copy(RB[:, 0:1024], PSB[:, 0:1024]),
                        reads=[pr("PSB", 0, 1024)], writes=[R(o_rb, cols * 4)], tag="lnrbe")
                sch.add("ve", lambda e: e.tensor_copy(RB[:, 1024:cols], PSC[:, 0:cols - 1024]),
                        reads=[pr("PSC", 0, cols - 1024)], writes=[R(o_rb, cols * 4)], tag="lnrbe2")
            TL = f32(o_tl, cols)
            for b in range(NDB):
                o_x = o_tmp if b % 2 == 0 else o_tmp2
                xt = f32(o_x, cols)
                sch.add("sy", lambda e, d=xt, s=xsrc_dram(b): e.dma_start(out=d, in_=s),
                        writes=[R(o_x, cols * 4)], reads=[(xdram_key, b, b + 1)], dma=True, tag="lnld2")
                xbb = b16(o_xb + b * cols * 2, cols)
                sch.add("ve", lambda e, s=xt: e.tensor_sub(TL, s, MB),
                        reads=[R(o_x, cols * 4), R(o_mb, cols * 4)],
                        writes=[R(o_tl, cols * 4)], tag="lnn1")
                sch.add("ve", lambda e, d=xbb: e.tensor_mul(d, TL, RB),
                        reads=[R(o_tl, cols * 4), R(o_rb, cols * 4)],
                        writes=[R(o_xb + b * cols * 2, cols * 2)], tag="lnn2")
            rg.pop()

        rg.push()
        o_xn = rg.alloc(NDB * COLS * 2)
        layernorm(lambda b: xin[b * P:(b + 1) * P, :], o_xn, COLS, "xin")

        rg.push()
        o_wib = [rg.alloc(16 * P * 2) for _ in range(2)]
        o_xcb = [rg.alloc(COLS * 2) for _ in range(2)]
        o_cvo = [rg.alloc(TOK * 2) for _ in range(2)]
        o_dg = [rg.alloc(DCONV * P * 2) for _ in range(2)]
        o_zev = [rg.alloc(TOK * 2) for _ in range(2)]
        o_dtr = rg.alloc(TOK * 4)
        o_dtt = rg.alloc(TOK * 4)
        o_xts = [rg.alloc(1024 * 2) for _ in range(2)]

        for mb in range(67):
            slot = mb % 2
            WIB = b16(o_wib[slot], 16 * P)
            sch.add("ge", lambda e, d=WIB, mb=mb: e.dma_start(
                    out=d.rearrange("p (kb m) -> p kb m", kb=16), in_=win[mb].transpose([1, 0, 2])),
                    writes=[R(o_wib[slot], 16 * P * 2)], reads=[("win", mb, mb + 1)],
                    dma=True, tag="wld")
            is_z = mb < 32
            is_xbc = 32 <= mb < 66
            is_dt = mb == 66
            if is_xbc:
                tiles = [(0, 512), (512, 512), (1024, 4)]
            else:
                tiles = [(4, 512), (516, 512)]
            for kbi in range(16):
                xnb = b16(o_xn + kbi * COLS * 2, COLS)
                for ti, (t0, tn) in enumerate(tiles):
                    po = [0, 512, 1024][ti]
                    sch.add("te", lambda e, po=po, tn=tn, t0=t0, kbi=kbi, W=WIB, s=xnb: e.matmul(
                        PSA[:, po:po + tn], W[:, kbi * P:(kbi + 1) * P], s[:, t0:t0 + tn],
                        start=(kbi == 0), stop=(kbi == 15)),
                        reads=[R(o_wib[slot], 16 * P * 2), R(o_xn + kbi * COLS * 2, COLS * 2)],
                        writes=[pr("PSA", po, po + tn)], tag="ipmm")
            if is_z:
                ZEV = b16(o_zev[slot], TOK)
                sch.add("ve", lambda e, d=ZEV: e.tensor_copy(d, PSA[:, 0:TOK]),
                        reads=[pr("PSA", 0, 1024)], writes=[R(o_zev[slot], TOK * 2)], tag="zev")
                sch.add("sy", lambda e, s=ZEV, mb=mb: e.dma_start(out=z_s[mb], in_=s),
                        reads=[R(o_zev[slot], TOK * 2)], writes=[("z_s", mb, mb + 1)],
                        dma=True, tag="zst")
            elif is_dt:
                DTR = f32(o_dtr, TOK, 0, NH)
                sch.add("ve", lambda e: e.tensor_copy(DTR, PSA[0:NH, 0:TOK]),
                        reads=[pr("PSA", 0, 1024)], writes=[R(o_dtr, TOK * 4)], tag="dtev")
                DTT = f32(o_dtt, TOK, 0, NH)
                sch.add("ve", lambda e: e.tensor_scalar_add(DTT, DTR, DTB),
                        reads=[R(o_dtr, TOK * 4), R(o_dtb, 4)], writes=[R(o_dtt, TOK * 4)], tag="sp1")
                sch.add("ve", lambda e: e.scalar_tensor_tensor(DTT, DTT, -1.0, DTT, ALU.mult, ALU.min),
                        reads=[R(o_dtt, TOK * 4)], writes=[R(o_dtt, TOK * 4)], tag="sp1b")
                sch.add("se", lambda e: e.activation(DTT, DTT, AF.Exp),
                        reads=[R(o_dtt, TOK * 4)], writes=[R(o_dtt, TOK * 4)], tag="sp2")
                sch.add("se", lambda e: e.activation(DTT, DTT, AF.Ln, bias=f32(o_c1, 1, 0, NH)),
                        reads=[R(o_dtt, TOK * 4), RC1], writes=[R(o_dtt, TOK * 4)], tag="sp3")
                sch.add("ve", lambda e: e.tensor_scalar(DTS, DTR, DTB, 0.0, ALU.add, ALU.max),
                        reads=[R(o_dtr, TOK * 4), R(o_dtb, 4)], writes=[R(o_dt, TOK * 4)], tag="sp4")
                sch.add("ve", lambda e: e.tensor_add(DTS, DTS, DTT),
                        reads=[R(o_dt, TOK * 4), R(o_dtt, TOK * 4)], writes=[R(o_dt, TOK * 4)], tag="sp5")
                if DBG:
                    sch.add("sy", lambda e: e.dma_start(out=dbg_dt, in_=DTS),
                            reads=[R(o_dt, TOK * 4)], writes=[("dbg", 0, 1)], dma=True, tag="dbg")
            else:
                cb = mb - 32
                XCB = b16(o_xcb[slot], COLS)
                sch.add("ve", lambda e, d=XCB: e.tensor_copy(d, PSA[:, 0:COLS]),
                        reads=[pr("PSA", 0, COLS)], writes=[R(o_xcb[slot], COLS * 2)], tag="xcev")
                DG = b16(o_dg[slot], DCONV * P)
                for k in range(DCONV):
                    dgk = b16(o_dg[slot] + k * P * 2, P)
                    sch.add("ge", lambda e, d=dgk, cb=cb, k=k: e.tensor_copy(
                        d, CVW[:, cb * DCONV + k:cb * DCONV + k + 1].broadcast_to([P, P])),
                        reads=[R(o_convw, NCB * DCONV * 4)],
                        writes=[R(o_dg[slot] + k * P * 2, P * 2)], tag="dgc")
                    sch.add("ge", lambda e, d=dgk: e.affine_select(
                        out=d, in_=d, compare_op=ALU.is_equal, fill=0.0, base=0,
                        pattern=[[-1, P]], channel_multiplier=1),
                        reads=[R(o_dg[slot] + k * P * 2, P * 2)],
                        writes=[R(o_dg[slot] + k * P * 2, P * 2)], tag="dgs")
                for k in range(DCONV):
                    dgk = b16(o_dg[slot] + k * P * 2, P)
                    for a in (0, 512):
                        sch.add("te", lambda e, a=a, k=k, d=dgk, s=XCB: e.matmul(
                            PSB[:, a:a + 512], d, s[:, a + 1 + k:a + 513 + k],
                            start=(k == 0), stop=(k == DCONV - 1)),
                            reads=[R(o_dg[slot] + k * P * 2, P * 2), R(o_xcb[slot], COLS * 2)],
                            writes=[pr("PSB", a, a + 512)], tag="cvmm")
                CVO = b16(o_cvo[slot], TOK)
                sch.add("se", lambda e, d=CVO, cb=cb: e.activation(
                    d, PSB[:, 0:TOK], AF.Silu, bias=CVB[:, cb:cb + 1]),
                    reads=[pr("PSB", 0, 1024), R(o_convb, NCB * 4)],
                    writes=[R(o_cvo[slot], TOK * 2)], tag="cvsl")
                sch.add("sy", lambda e, s=CVO, cb=cb: e.dma_start(out=xbc_s[cb], in_=s),
                        reads=[R(o_cvo[slot], TOK * 2)], writes=[("xbc_s", cb, cb + 1)],
                        dma=True, tag="cvst")
                if cb < 32:
                    for j in range(NCH):
                        sch.add("te", lambda e, j=j, s=CVO: e.transpose(
                            PTA[:, j * P:(j + 1) * P], s[:, j * P:(j + 1) * P], ID128b),
                            reads=[R(o_cvo[slot], TOK * 2), R(o_id128b, 256)],
                            writes=[pr("PTA", j * P, (j + 1) * P)], tag="bxt")
                    XTS = b16(o_xts[slot], 1024)
                    sch.add("ve", lambda e, d=XTS: e.tensor_copy(d, PTA[:, 0:1024]),
                            reads=[pr("PTA", 0, 1024)], writes=[R(o_xts[slot], 2048)], tag="bxe")
                    sch.add("sy", lambda e, s=XTS, cb=cb: e.dma_start(
                        out=xt_s[0:NCH, :, cb * P:(cb + 1) * P].transpose([1, 0, 2]),
                        in_=s.rearrange("p (c t) -> p c t", c=NCH)),
                        reads=[R(o_xts[slot], 2048)], writes=[("xt_s", cb, cb + 1)],
                        dma=True, tag="bxst")
        rg.pop()
        rg.pop()

        o_xscs = rg.alloc(2 * CH * 2)
        o_xscs2 = rg.alloc(2 * CH * 2)
        o_xt = rg.alloc(DIN * 2)
        o_xt2 = rg.alloc(DIN * 2)
        o_xtd = rg.alloc(DIN * 2)
        o_bw = rg.alloc(NH * CH * 2)
        o_m0 = rg.alloc(3 * 512 * 2)
        o_csc = rg.alloc(3 * 512 * 2)
        o_sin = rg.alloc(DIN * 4)
        o_slp = rg.alloc(DIN * 2)
        o_slc = rg.alloc(DIN * 4)
        o_sinb = rg.alloc(DIN * 2)
        o_wob2 = [rg.alloc(32 * P * 2) for _ in range(2)]
        o_pre = rg.alloc(24 * 1024)
        op_xc8 = o_pre
        op_vfm8 = o_pre + 4096
        op_wdc8 = o_pre + 8192
        op_xcb8 = o_pre + 12288
        op_xcl8 = o_pre + 14336
        op_exc8 = o_pre + 16384
        op_vfh8 = o_pre + 18432
        op_vfl8 = o_pre + 20480
        op_gct8 = o_pre + 22528
        o_sinb2 = o_pre
        o_xtP = [o_xt, o_xt2]
        o_xscsP = [o_xscs, o_xscs2]
        o_sinbP = [o_sinb, o_sinb2]
        o_pct = o_sinb
        o_suf = o_sinb + 2048
        PRE = R(o_pre, 24 * 1024)
        o_small = rg.alloc(4 * 1024)
        o_dat = rg.alloc(256)
        os_bt = o_small
        os_wt = o_small + 256
        os_gtb = o_small + 512
        os_row2 = o_small + 1024
        os_dat4 = o_small + 1536
        os_coef = o_small + 2048
        os_dacol = o_small + 2560

        SM = R(o_small, 4 * 1024)

        def ssd_preamble(sweep):
            XC8 = f32(op_xc8, TOK, 0, NH)
            XC8G = f32(op_vfm8, TOK, 0, NH)
            DA8 = f32(op_wdc8, TOK, 0, NH)
            WDC8 = f32(op_wdc8, TOK, 0, NH)
            sch.add("ve", lambda e: e.tensor_scalar_mul(DA8, DTS, AC),
                    reads=[R(o_dt, TOK * 4), R(o_avec, 4)], writes=[R(op_wdc8, 4096)], tag="da")
            sch.add("ve", lambda e: e.tensor_tensor_scan(
                XC8G, DA8, ZCOL[0:NH, 0:1].broadcast_to([NH, TOK]), 0.0,
                ALU.add, ALU.add),
                reads=[R(op_wdc8, 4096), R(o_zcol, 64)],
                writes=[R(op_vfm8, 4096)], tag="xc")
            sch.add("ve", lambda e: e.tensor_copy(XC8[:, 0:CH], XC8G[:, 0:CH]),
                    reads=[R(op_vfm8, 4096)], writes=[R(op_xc8, 4096)], tag="xcs0")
            for c in range(1, NCH):
                sch.add("ve", lambda e, c=c: e.tensor_scalar(
                    XC8[:, c * CH:(c + 1) * CH], XC8G[:, c * CH:(c + 1) * CH],
                    XC8G[:, c * CH - 1:c * CH], 0.0, ALU.subtract, ALU.add),
                    reads=[R(op_vfm8, 4096)], writes=[R(op_xc8, 4096)], tag="xcs")
            DACOL = f32(os_dacol, 8, 0, NH)
            sch.add("ve", lambda e: e.tensor_copy(
                DACOL, XC8.rearrange("p (c t) -> p c t", c=NCH)[:, :, 127]),
                reads=[R(op_xc8, 4096)], writes=[SM], tag="dacol")
            if sweep == 1:
                DCOLS = f32(os_row2, 1, 0, NH)
                sch.add("ve", lambda e: e.reduce_sum(DCOLS, DACOL, axis=AX.X),
                        reads=[SM], writes=[SM], tag="datred")
                sch.add("te", lambda e: e.transpose(PSC[0:1, 0:NH], DCOLS, IDF[0:NH, 0:NH]),
                        reads=[SM, R(o_idf, 512)], writes=[pr("PSC", 0, NH)], tag="datt")
                sch.add("ve", lambda e: e.tensor_copy(f32(o_dat, NH, 0, 1), PSC[0:1, 0:NH]),
                        reads=[pr("PSC", 0, NH)], writes=[R(o_dat, 256)], tag="datev")
            if sweep == 2:
                for c in range(NCH):
                    sch.add("te", lambda e, c=c: e.transpose(
                        PSC[0:1, 0:NH], DACOL[:, c:c + 1], IDF[0:NH, 0:NH]),
                        reads=[SM, R(o_idf, 512)], writes=[pr("PSC", 0, NH)], tag="gdt")
                    GR = f32(os_row2 + 512, NH, 0, 1)
                    sch.add("se", lambda e: e.activation(GR, PSC[0:1, 0:NH], AF.Exp),
                            reads=[pr("PSC", 0, NH)], writes=[SM], tag="gexp")
                    sch.add("te", lambda e: e.matmul(PSC[:, 64:64 + NH], ONESRF, GR,
                                                     start=True, stop=True),
                            reads=[SM, R(o_onesrf, 512)], writes=[pr("PSC", 64, 64 + NH)], tag="gbc")
                    sch.add("ve", lambda e, c=c: e.tensor_copy(
                        f32(op_gct8 + c * 256, NH), PSC[:, 64:64 + NH]),
                        reads=[pr("PSC", 64, 64 + NH)], writes=[R(op_gct8 + c * 256, 256)], tag="gce")
            else:
                SUF = f32(o_suf, 16, 0, NH)
                sch.add("ve", lambda e: e.memset(SUF, 0.0),
                        writes=[R(o_suf, 64)], tag="sufz")
                sch.add("ve", lambda e: e.tensor_copy(SUF[:, 0:8], DACOL),
                        reads=[SM], writes=[R(o_suf, 64)], tag="sufc")
                for sh in (1, 2, 4):
                    sch.add("ve", lambda e, sh=sh: e.tensor_add(
                        SUF[:, 0:8], SUF[:, 0:8], SUF[:, sh:sh + 8]),
                        reads=[R(o_suf, 64)], writes=[R(o_suf, 64)], tag="sufa")
                for c in range(NCH):
                    sch.add("te", lambda e, c=c: e.transpose(
                        PSC[0:1, 0:NH], SUF[:, c + 1:c + 2], IDF[0:NH, 0:NH]),
                        reads=[R(o_suf, 64), R(o_idf, 512)], writes=[pr("PSC", 0, NH)], tag="pdt")
                    GR = f32(os_row2 + 512, NH, 0, 1)
                    sch.add("se", lambda e: e.activation(GR, PSC[0:1, 0:NH], AF.Exp),
                            reads=[pr("PSC", 0, NH)], writes=[SM], tag="pexp")
                    sch.add("te", lambda e: e.matmul(PSC[:, 64:64 + NH], ONESRF, GR,
                                                     start=True, stop=True),
                            reads=[SM, R(o_onesrf, 512)], writes=[pr("PSC", 64, 64 + NH)], tag="pbc")
                    sch.add("ve", lambda e, c=c: e.tensor_copy(
                        f32(o_pct + c * 256, NH), PSC[:, 64:64 + NH]),
                        reads=[pr("PSC", 64, 64 + NH)], writes=[R(o_pct + c * 256, 256)], tag="pce")
            for c in range(NCH):
                sch.add("se", lambda e, c=c: e.activation(
                    WDC8[:, c * CH:(c + 1) * CH], XC8[:, c * CH:(c + 1) * CH],
                    AF.Exp, bias=XC8[:, c * CH + 127:c * CH + 128], scale=-1.0),
                    reads=[R(op_xc8, 4096)], writes=[R(op_wdc8, 4096)], tag="wdc")
            sch.add("ve", lambda e: e.tensor_mul(WDC8, WDC8, DTS),
                    reads=[R(op_wdc8, 4096), R(o_dt, TOK * 4)],
                    writes=[R(op_wdc8, 4096)], tag="wdt")
            if sweep == 2:
                XCB8 = b16(op_xcb8, TOK, 0, NH)
                XCL8 = b16(op_xcl8, TOK, 0, NH)
                EXC8 = b16(op_exc8, TOK, 0, NH)
                VFM8 = f32(op_vfm8, TOK, 0, NH)
                VFH8 = b16(op_vfh8, TOK, 0, NH)
                VFL8 = b16(op_vfl8, TOK, 0, NH)
                sch.add("ve", lambda e: e.tensor_copy(XCB8, XC8),
                        reads=[R(op_xc8, 4096)], writes=[R(op_xcb8, 2048)], tag="xcb")
                sch.add("ve", lambda e: e.tensor_sub(XCL8, XC8, XCB8),
                        reads=[R(op_xc8, 4096), R(op_xcb8, 2048)],
                        writes=[R(op_xcl8, 2048)], tag="xcl")
                sch.add("se", lambda e: e.activation(EXC8, XC8, AF.Exp),
                        reads=[R(op_xc8, 4096)], writes=[R(op_exc8, 2048)], tag="exc")
                sch.add("se", lambda e: e.activation(VFM8, DTS, AF.Ln),
                        reads=[R(o_dt, TOK * 4)], writes=[R(op_vfm8, 4096)], tag="lnd")
                sch.add("ve", lambda e: e.tensor_sub(VFM8, VFM8, XC8),
                        reads=[R(op_vfm8, 4096), R(op_xc8, 4096)],
                        writes=[R(op_vfm8, 4096)], tag="vfm")
                sch.add("ve", lambda e: e.tensor_copy(VFH8, VFM8),
                        reads=[R(op_vfm8, 4096)], writes=[R(op_vfh8, 2048)], tag="vfh")
                sch.add("ve", lambda e: e.tensor_sub(VFL8, VFM8, VFH8),
                        reads=[R(op_vfm8, 4096), R(op_vfh8, 2048)],
                        writes=[R(op_vfl8, 2048)], tag="vfl")

        def chunk_load(c, sweep):
            par = c % 2
            XTp = b16(o_xtP[par], DIN)
            sch.add("sy", lambda e, c=c, d=XTp: e.dma_start(out=d, in_=xt_s[c]),
                    reads=[("xt_s", 0, 32)], writes=[R(o_xtP[par], DIN * 2)],
                    dma=True, tag="chld")
            nbc = 2 if sweep == 2 else 1
            sch.add("sy", lambda e, c=c, nbc=nbc, off=o_xscsP[par]: e.dma_start(
                out=arf[:, off // 2:off // 2 + nbc * CH].rearrange(
                    "p (b t) -> p b t", b=nbc),
                in_=xbc_s[32:32 + nbc, :, c * CH:(c + 1) * CH].transpose([1, 0, 2])),
                reads=[("xbc_s", 32, 34)], writes=[R(o_xscsP[par], 512)],
                dma=True, tag="bcld")

        def bwprep_te(c):
            par = c % 2
            XSC = b16(o_xscsP[par], 256)
            sch.add("te", lambda e, XSC=XSC: e.transpose(PTB[:, 0:P], XSC[:, 0:CH], ID128b),
                    reads=[R(o_xscsP[par], 512), R(o_id128b, 256)],
                    writes=[pr("PTB", 0, P)], tag="btt")
            sch.add("te", lambda e, c=c: e.transpose(
                PSC[:, 128:192], f32(op_wdc8, TOK, 0, NH)[:, c * CH:(c + 1) * CH],
                IDF[0:64, 0:64]),
                reads=[R(op_wdc8, 4096), R(o_idf, 512)], writes=[pr("PSC", 128, 192)], tag="wtt")

        def bwprep_ve(c, sweep):
            par = c % 2
            XT = b16(o_xtP[par], DIN)
            BT = b16(os_bt, P)
            sch.add("ve", lambda e: e.tensor_copy(BT, PTB[:, 0:P]),
                    reads=[pr("PTB", 0, P)], writes=[R(os_bt, 256)], tag="bte")
            WT = f32(os_wt, 64)
            sch.add("ve", lambda e: e.tensor_copy(WT, PSC[:, 128:192]),
                    reads=[pr("PSC", 128, 192)], writes=[R(os_wt, 256)], tag="wte")
            BW = b16(o_bw, NH * CH)
            for eng, h0, h1 in (("ve", 0, 46), ("ge", 46, 64)):
                sch.add(eng, lambda e, h0=h0, h1=h1: e.tensor_tensor(
                    BW[0:P, h0 * P:h1 * P].rearrange("p (h n) -> p h n", h=h1 - h0),
                    BT.unsqueeze(1).broadcast_to([P, h1 - h0, P]),
                    WT[0:P, h0:h1].unsqueeze(2).broadcast_to([P, h1 - h0, P]), ALU.mult),
                    reads=[R(os_bt, 256), R(os_wt, 256)],
                    writes=[R(o_bw + h0 * P * 2, (h1 - h0) * P * 2)], tag="bw")
            if sweep == 2:
                sch.add("ve", lambda e, XT=XT: e.tensor_tensor(
                    b16(o_xtd, DIN).rearrange("p (h q) -> p h q", h=NH),
                    XT.rearrange("p (h q) -> p h q", h=NH),
                    DT64B.unsqueeze(2).broadcast_to([P, NH, HD]), ALU.mult),
                    reads=[R(o_xtP[par], DIN * 2), R(o_dt64, 128)],
                    writes=[R(o_xtd, DIN * 2)], tag="xtd")

        def scan_slice(c, hg):
            off = hg * 2048
            SRUNs = f32(o_sin + off, 512)
            GCT = f32(op_gct8 + (c - 1) * 256, NH)
            sch.add("ge", lambda e, s=SRUNs, g=GCT, hg=hg: e.tensor_tensor(
                s.rearrange("p (h q) -> p h q", h=8), s.rearrange("p (h q) -> p h q", h=8),
                g[:, hg * 8:(hg + 1) * 8].unsqueeze(2).broadcast_to([P, 8, HD]), ALU.mult),
                reads=[R(o_sin + off, 2048), R(op_gct8 + (c - 1) * 256, 256)],
                writes=[R(o_sin + off, 2048)], tag="sc1")
            sch.add("ve", lambda e, s=SRUNs, off=off: e.tensor_add(
                s, s, f32(o_slc + off, 512)),
                reads=[R(o_sin + off, 2048), R(o_slc + off, 2048)],
                writes=[R(o_sin + off, 2048)], tag="sc2")

        def slocmm(c, mid=None, scan_c=None):
            par = c % 2
            XT = b16(o_xtP[par], DIN)
            BW = b16(o_bw, NH * CH)
            for hg in range(8):
                if hg == 4 and mid is not None:
                    mid()
                if scan_c is not None:
                    scan_slice(scan_c, hg)
                po = (hg % 3) * 512
                for hh in range(8):
                    h = hg * 8 + hh
                    sch.add("te", lambda e, po=po, hh=hh, h=h, BW=BW, XT=XT: e.matmul(
                        PSA[:, po + hh * 64:po + (hh + 1) * 64],
                        BW[:, h * P:(h + 1) * P], XT[:, h * HD:(h + 1) * HD],
                        start=True, stop=True),
                        reads=[R(o_bw, NH * CH * 2), R(o_xtP[par], DIN * 2)],
                        writes=[pr("PSA", po + hh * 64, po + (hh + 1) * 64)], tag="slmm")
                sch.add("se", lambda e, hg=hg, po=po: e.copy(
                    f32(o_slc + hg * 512 * 4, 512), PSA[:, po:po + 512]),
                    reads=[pr("PSA", po, po + 512)],
                    writes=[R(o_slc + hg * 2048, 2048)], tag="slev")

        def scan_update(c):
            SRUN = f32(o_sin, DIN)
            GCT = f32(op_gct8 + (c - 1) * 256, NH)
            SLP = f32(o_slc, DIN)
            sch.add("ge", lambda e: e.tensor_tensor(
                SRUN.rearrange("p (h q) -> p h q", h=NH), SRUN.rearrange("p (h q) -> p h q", h=NH),
                GCT.unsqueeze(2).broadcast_to([P, NH, HD]), ALU.mult),
                reads=[R(o_sin, DIN * 4), R(op_gct8 + (c - 1) * 256, 256)],
                writes=[R(o_sin, DIN * 4)], tag="sc1")
            sch.add("ve", lambda e: e.tensor_add(SRUN, SRUN, SLP),
                    reads=[R(o_sin, DIN * 4), R(o_slc, DIN * 4)], writes=[R(o_sin, DIN * 4)], tag="sc2")

        ssd_preamble(1)
        sch.add("ve", lambda e: e.memset(f32(o_sin, DIN), 0.0),
                writes=[R(o_sin, DIN * 4)], tag="s1z")
        OSWDB = os_row2
        O_BWP = [(o_bw, o_bw + 32 * P * 2), (o_slc, o_sin)]

        def s1_prep(c):
            par = c % 2
            blo, bhi = O_BWP[par]
            XSC = b16(o_xscsP[par], 256)
            WDB = b16(OSWDB, CH)
            sch.add("ve", lambda e, c=c, d=WDB: e.tensor_copy(
                d[0:NH, :], f32(op_wdc8, TOK, 0, NH)[:, c * CH:(c + 1) * CH]),
                reads=[R(op_wdc8, 4096)], writes=[R(OSWDB, 256)], tag="wdb")
            sch.add("te", lambda e, XSC=XSC: e.transpose(PTB[:, 0:P], XSC[:, 0:CH], ID128b),
                    reads=[R(o_xscsP[par], 512), R(o_id128b, 256)],
                    writes=[pr("PTB", 0, P)], tag="btt")
            sch.add("te", lambda e, s=WDB: e.transpose(
                PTB[:, 128:192], s[0:NH, :], ID128b[0:NH, 0:NH]),
                reads=[R(OSWDB, 256), R(o_id128b, 256)], writes=[pr("PTB", 128, 192)], tag="wtt1")
            BT = b16(os_bt, P)
            sch.add("ve", lambda e: e.tensor_copy(BT, PTB[:, 0:P]),
                    reads=[pr("PTB", 0, P)], writes=[R(os_bt, 256)], tag="bte")
            WT = f32(os_wt, 64)
            sch.add("ve", lambda e, c=c: e.tensor_tensor(
                WT, PTB[:, 128:192], f32(o_pct + c * 256, NH), ALU.mult),
                reads=[pr("PTB", 128, 192), R(o_pct + c * 256, 256)],
                writes=[R(os_wt, 256)], tag="wtp")
            for eng, h0, h1 in (("ve", 0, 32), ("ve", 32, 46), ("ge", 46, 64)):
                base = blo if h0 < 32 else bhi + (h0 - 32) * P * 2
                BWh = b16(base, (h1 - h0) * P)
                sch.add(eng, lambda e, h0=h0, h1=h1, BT=BT, WT=WT, BWh=BWh: e.tensor_tensor(
                    BWh.rearrange("p (h n) -> p h n", h=h1 - h0),
                    BT.unsqueeze(1).broadcast_to([P, h1 - h0, P]),
                    WT[0:P, h0:h1].unsqueeze(2).broadcast_to([P, h1 - h0, P]), ALU.mult),
                    reads=[R(os_bt, 256), R(os_wt, 256)],
                    writes=[R(base, (h1 - h0) * P * 2)], tag="bw")

        PERS = [("PSA", 0), ("PSA", 512), ("PSA", 1024), ("PSB", 0), ("PSB", 512)]
        PERS_T = [PSA, PSA, PSA, PSB, PSB]
        chunk_load(0, 1)
        s1_prep(0)
        for c in range(NCH):
            par = c % 2
            XT = b16(o_xtP[par], DIN)
            blo, bhi = O_BWP[par]
            if c < NCH - 1:
                chunk_load(c + 1, 1)
            for hg in range(5):
                pbn, po = PERS[hg]
                PT = PERS_T[hg]
                for hh in range(8):
                    h = hg * 8 + hh
                    base = blo if h < 32 else bhi
                    BWh = b16(base, 32 * P)
                    hr = h % 32
                    sch.add("te", lambda e, PT=PT, po=po, hh=hh, h=h, hr=hr, BWh=BWh, XT=XT, c=c: e.matmul(
                        PT[:, po + hh * 64:po + (hh + 1) * 64],
                        BWh[:, hr * P:(hr + 1) * P], XT[:, h * HD:(h + 1) * HD],
                        start=(c == 0), stop=(c == NCH - 1)),
                        reads=[R(base, 32 * P * 2), R(o_xtP[par], DIN * 2)],
                        writes=[pr(pbn, po + hh * 64, po + (hh + 1) * 64)], tag="spmm")
                if hg == 2 and c < NCH - 1:
                    s1_prep(c + 1)
            for hg in (5, 6, 7):
                for hh in range(8):
                    h = hg * 8 + hh
                    hr = h % 32
                    BWh = b16(bhi, 32 * P)
                    sch.add("te", lambda e, hh=hh, h=h, hr=hr, BWh=BWh, XT=XT: e.matmul(
                        PSC[:, hh * 64:(hh + 1) * 64],
                        BWh[:, hr * P:(hr + 1) * P], XT[:, h * HD:(h + 1) * HD],
                        start=True, stop=True),
                        reads=[R(bhi, 32 * P * 2), R(o_xtP[par], DIN * 2)],
                        writes=[pr("PSC", hh * 64, (hh + 1) * 64)], tag="slmm")
                sch.add("se", lambda e, hg=hg: e.copy(
                    f32(o_slc + hg * 2048, 512), PSC[:, 0:512]),
                    reads=[pr("PSC", 0, 512)],
                    writes=[R(o_slc + hg * 2048, 2048)], tag="slev")
                sch.add("ve", lambda e, hg=hg: e.tensor_add(
                    f32(o_sin + hg * 2048, 512), f32(o_sin + hg * 2048, 512),
                    f32(o_slc + hg * 2048, 512)),
                    reads=[R(o_sin + hg * 2048, 2048), R(o_slc + hg * 2048, 2048)],
                    writes=[R(o_sin + hg * 2048, 2048)], tag="s1a")
        AGP = b16(o_bw, 4224)
        for hg in range(8):
            if hg < 5:
                pbn, po = PERS[hg]
                PT = PERS_T[hg]
                sch.add("se", lambda e, PT=PT, po=po, hg=hg: e.copy(
                    b16(o_bw + hg * 1024, 512), PT[:, po:po + 512]),
                    reads=[pr(pbn, po, po + 512)],
                    writes=[R(o_bw + hg * 1024, 1024)], tag="agev")
            else:
                sch.add("ve", lambda e, hg=hg: e.tensor_copy(
                    b16(o_bw + hg * 1024, 512), f32(o_sin + hg * 2048, 512)),
                    reads=[R(o_sin + hg * 2048, 2048)],
                    writes=[R(o_bw + hg * 1024, 1024)], tag="agp1")
            if hg == 3:
                sch.add("sy", lambda e: e.dma_start(out=ag_in[:, 0:2048], in_=AGP[:, 0:2048]),
                        reads=[R(o_bw, 4096)], writes=[("ag_in", 0, 1)], dma=True, tag="agw")
        sch.add("ve", lambda e: e.tensor_copy(
            b16(o_bw + DIN * 2, 128, 0, 1), f32(o_dat, NH, 0, 1).bitcast(BF16)),
            reads=[R(o_dat, 256)], writes=[R(o_bw + DIN * 2, 256)], tag="agp2")
        sch.add("sy", lambda e: e.dma_start(out=ag_in[:, 2048:4224], in_=AGP[:, 2048:4224]),
                reads=[R(o_bw + 4096, 4352)], writes=[("ag_in", 1, 2)], dma=True, tag="agw")
        cc_op = sch.add("ge", lambda e: e.collective_compute(
            "AllGather", ALU.bypass, RG, [ag_in.opt()], [ag_out.opt()]),
            reads=[("ag_in", 0, 2)], writes=[("ag_out", 0, 4)], tag="ag")
        sch.ops[cc_op]["dma"] = False

        o_zc = rg.alloc(DIN * 2)
        o_yb = o_slp

        def s0_combine():
            sch.add("sy", lambda e: e.dma_start(
                out=f32(os_dat4, NH, 0, 4),
                in_=ag_out.rearrange("(i p) c -> i p c", p=P)[:, 0, DIN:DIN + 128].bitcast(F32)),
                reads=[("ag_out", 0, 4)], writes=[SM], dma=True, tag="datld")
            sch.add("te", lambda e: e.matmul(PSC[0:4, 0:NH], MMK, f32(os_dat4, NH, 0, 4),
                                             start=True, stop=True),
                    reads=[SM, R(o_mmask, 16)], writes=[pr("PSC", 0, NH)], tag="exmm")
            sch.add("se", lambda e: e.activation(f32(os_coef, NH, 0, 4), PSC[0:4, 0:NH], AF.Exp),
                    reads=[pr("PSC", 0, NH)], writes=[SM], tag="exe")
            sch.add("ve", lambda e: e.tensor_scalar_mul(f32(os_coef, NH, 0, 4), f32(os_coef, NH, 0, 4), CMK),
                    reads=[SM, R(o_cmask, 4)], writes=[SM], tag="exm")
            sch.add("sy", lambda e: e.dma_start(out=co_d, in_=f32(os_coef, NH, 0, 4)),
                    reads=[SM], writes=[("co_d", 0, 1)], dma=True, tag="cost")
            sch.add("ve", lambda e: e.memset(f32(o_sin, DIN), 0.0),
                    reads=[], writes=[R(o_sin, DIN * 4)], tag="s0z")
            o_cb = o_wob2[0]
            o_t0 = o_wob2[0] + 1024
            for i in range(4):
                o_sfb = o_wob2[1] if i % 2 == 0 else o_zc
                SFB = b16(o_sfb, DIN)
                sch.add("sy", lambda e, i=i, d=SFB: e.dma_start(
                    out=d, in_=ag_out[i * P:(i + 1) * P, 0:DIN]),
                    reads=[("ag_out", i, i + 1)], writes=[R(o_sfb, DIN * 2)], dma=True, tag="sfld")
                CB = f32(o_cb, NH)
                sch.add("sy", lambda e, i=i, d=CB: e.dma_start(
                    out=d, in_=co_d[i:i + 1, :].broadcast_to([P, NH])),
                    reads=[("co_d", 0, 1)], writes=[R(o_cb, NH * 4)], dma=True, tag="cbld")
                for q in range(4):
                    sch.add("ve", lambda e, s=SFB, cb=CB, q=q, o_t0=o_t0: e.tensor_tensor(
                        f32(o_t0, 1024).rearrange("p (h q) -> p h q", h=16),
                        s[0:P, q * 1024:(q + 1) * 1024].rearrange("p (h q) -> p h q", h=16),
                        cb[0:P, q * 16:(q + 1) * 16].unsqueeze(2).broadcast_to([P, 16, HD]),
                        ALU.mult),
                        reads=[R(o_sfb, DIN * 2), R(o_cb, NH * 4)],
                        writes=[R(o_t0, 4096)], tag="s0m")
                    sch.add("ve", lambda e, q=q, o_t0=o_t0: e.tensor_add(
                        f32(o_sin + q * 4096, 1024), f32(o_sin + q * 4096, 1024), f32(o_t0, 1024)),
                        reads=[R(o_sin, DIN * 4), R(o_t0, 4096)], writes=[R(o_sin, DIN * 4)], tag="s0a")
        o_g2 = o_zc
        o_yn = rg.alloc(32 * 512 * 2)
        o_xr1 = [rg.alloc(512 * 4) for _ in range(2)]
        o_x1o = [rg.alloc(512 * 4) for _ in range(2)]

        def emit_wold(mb):
            slot = mb % 2
            WOB = b16(o_wob2[slot], 32 * P)
            sch.add("ge", lambda e, d=WOB, mb=mb: e.dma_start(
                    out=d.rearrange("p (kb m) -> p kb m", kb=32), in_=wout[mb].transpose([1, 0, 2])),
                    reads=[("wout", mb, mb + 1)], writes=[R(o_wob2[slot], 32 * P * 2)],
                    dma=True, tag="wold")

        def emit_sinb(c):
            SINB = b16(o_sinbP[c % 2], DIN)
            sch.add("se", lambda e, d=SINB: e.copy(d, f32(o_sin, DIN)),
                    reads=[R(o_sin, DIN * 4)], writes=[R(o_sinbP[c % 2], DIN * 2)], tag="sinb")

        ssd_preamble(2)
        if DBG:
            sch.add("sy", lambda e: e.dma_start(out=dbg_exc, in_=b16(op_exc8, TOK, 0, NH)),
                    reads=[PRE], writes=[("dbg", 9, 10)], dma=True, tag="dbgexc")
        chunk_load(0, 2)
        bwprep_te(0)
        bwprep_ve(0, 2)
        for c in range(NCH):
            par = c % 2
            if c < NCH - 1:
                chunk_load(c + 1, 2)
            slocmm(c)
            XSC = b16(o_xscsP[par], 256)
            sch.add("te", lambda e, XSC=XSC: e.matmul(
                PSC[:, 0:128], XSC[:, 0:CH], XSC[:, CH:2 * CH],
                start=True, stop=True),
                reads=[R(o_xscsP[par], 512)], writes=[pr("PSC", 0, 128)], tag="gtmm")
            GTB = b16(os_gtb, P)
            sch.add("ve", lambda e: e.tensor_copy(GTB, PSC[:, 0:128]),
                    reads=[pr("PSC", 0, 128)], writes=[SM], tag="gtev")
            if DBG and c < 2:
                sch.add("sy", lambda e, c=c: e.dma_start(
                    out=dbg_slc[:, c * 4096:(c + 1) * 4096], in_=f32(o_slc, DIN)),
                    reads=[R(o_slc, DIN * 4)], writes=[("dbg", 3, 4)], dma=True, tag="dbgslc")
            ZC = b16(o_zc, DIN)
            zdst = arf[:, o_zc // 2: o_zc // 2 + DIN].rearrange("p (b t) -> p b t", b=32)
            def emit_zld(c=c, d=zdst):
                sch.add("sy", lambda e, c=c, d=d: e.dma_start(
                    out=d, in_=z_s[0:32, :, c * CH:(c + 1) * CH].transpose([1, 0, 2])),
                    reads=[("z_s", 0, 32)], writes=[R(o_zc, DIN * 2)], dma=True, tag="zld")
            XCB16 = b16(op_xcb8, TOK, 0, NH)[0:NH, c * CH:(c + 1) * CH]
            XCL = b16(op_xcl8, TOK, 0, NH)[0:NH, c * CH:(c + 1) * CH]
            EXCB = b16(op_exc8, TOK, 0, NH)[0:NH, c * CH:(c + 1) * CH]
            VFH = b16(op_vfh8, TOK, 0, NH)[0:NH, c * CH:(c + 1) * CH]
            VFL = b16(op_vfl8, TOK, 0, NH)[0:NH, c * CH:(c + 1) * CH]
            if c == 0:
                s0_combine()
                emit_zld()
                emit_sinb(0)
            else:
                emit_zld()
            SINB = b16(o_sinbP[par], DIN)
            if DBG:
                sch.add("sy", lambda e, c=c, s=SINB: e.dma_start(
                    out=dbg_sin[:, c * 4096:(c + 1) * 4096], in_=s),
                    reads=[R(o_sinbP[par], DIN * 2)], writes=[("dbg", 4, 5)], dma=True, tag="dbgsin")
            XT = b16(o_xtP[par], DIN)

            CSE_SLOT = [(PSC, 0, pr("PSC", 0, 512)), (PSB, 512, pr("PSB", 512, 1024))]

            def emit_cse(tg):
                PT, cpo, cpb = CSE_SLOT[tg % 2]
                for r4 in range(4):
                    h = tg * 4 + r4
                    sch.add("te", lambda e, PT=PT, cpo=cpo, r4=r4, h=h, xs=EXCB: e.matmul(
                        PT[:, cpo + r4 * 128:cpo + (r4 + 1) * 128],
                        ID64b[:, h:h + 1].broadcast_to([NH, P]), xs,
                        start=True, stop=True),
                        reads=[R(op_exc8, 2048), R(o_id64b, 128)],
                        writes=[cpb], tag="cse")

            def emit_T(tg):
                po = (tg % 3) * 512
                sch.add("te", lambda e, po=po: e.matmul(
                    PSA[:, po:po + 512], ID128b, MASKR, start=True, stop=False),
                    reads=[R(o_id128b, 256), R(o_mask, 1024)],
                    writes=[pr("PSA", po, po + 512)], tag="tbm")
                for hh in range(4):
                    h = tg * 4 + hh
                    for xi, xsrc in enumerate((XCB16, XCL)):
                        ro = op_xcb8 if xi == 0 else op_xcl8
                        sch.add("te", lambda e, po=po, hh=hh, h=h, xs=xsrc: e.matmul(
                            PSA[:, po + hh * 128:po + (hh + 1) * 128],
                            ID64b[:, h:h + 1].broadcast_to([NH, P]), xs,
                            start=False, stop=False),
                            reads=[R(ro, 2048), R(o_id64b, 128)],
                            writes=[pr("PSA", po + hh * 128, po + (hh + 1) * 128)], tag="tbe")
                for vi, vsrc in enumerate((VFH, VFL)):
                    ro = op_vfh8 if vi == 0 else op_vfl8
                    sch.add("te", lambda e, po=po, tg=tg, vs=vsrc, vi=vi: e.matmul(
                        PSA[:, po:po + 512], vs,
                        ID64b[:, tg * 4:(tg + 1) * 4].unsqueeze(2).broadcast_to([NH, 4, P]),
                        start=False, stop=(vi == 1)),
                        reads=[R(ro, 2048), R(o_id64b, 128)], writes=[pr("PSA", po, po + 512)], tag="tbv")
                sch.add("se", lambda e, po=po, tg=tg: e.activation(
                    b16(o_m0 + (tg % 3) * 1024, 512), PSA[:, po:po + 512], AF.Exp),
                    reads=[pr("PSA", po, po + 512)],
                    writes=[R(o_m0 + (tg % 3) * 1024, 1024)], tag="tbx")

            def emit_CY(tg):
                po = (tg % 3) * 512
                mo = o_m0 + (tg % 3) * 1024
                co = o_csc + (tg % 3) * 1024
                M0S = b16(mo, 512)
                CSCS = b16(co, 512)
                sch.add("ve", lambda e, M0S=M0S: e.tensor_tensor(
                    M0S.rearrange("p (r l) -> p r l", r=4),
                    M0S.rearrange("p (r l) -> p r l", r=4),
                    GTB.unsqueeze(1).broadcast_to([P, 4, P]), ALU.mult),
                    reads=[R(mo, 1024), R(os_gtb, 256)],
                    writes=[R(mo, 1024)], tag="mg")
                cpt, cpo, cpb = CSE_SLOT[tg % 2]
                sch.add("ve", lambda e, PT=cpt, cpo=cpo, XSC=XSC, CSCS=CSCS: e.tensor_tensor(
                    CSCS.rearrange("p (r l) -> p r l", r=4),
                    PT[:, cpo:cpo + 512].rearrange("p (r l) -> p r l", r=4),
                    XSC[:, CH:2 * CH].unsqueeze(1).broadcast_to([P, 4, P]), ALU.mult),
                    reads=[cpb], writes=[R(co, 1024)], tag="csc")
                for bp in (tg * 2, tg * 2 + 1):
                    yo = (bp % 4) * 128
                    sch.add("te", lambda e, yo=yo, bp=bp: e.matmul(
                        PSB[:, yo:yo + 128], b16(o_xtd, DIN)[:, bp * 128:(bp + 1) * 128],
                        ID128b, start=True, stop=False),
                        reads=[R(o_xtd, DIN * 2), R(o_id128b, 256)],
                        writes=[pr("PSB", yo, yo + 128)], tag="yds")
                    h0b, h1b = 2 * bp - tg * 4, 2 * bp + 1 - tg * 4
                    sch.add("te", lambda e, yo=yo, hh=h0b, XT=XT, M0S=M0S, bp=bp: e.matmul(
                        PSB[0:64, yo:yo + 128], XT[:, bp * 128:bp * 128 + HD],
                        M0S[:, hh * P:(hh + 1) * P], start=False, stop=False,
                        tile_position=(0, 0)),
                        reads=[R(o_xtP[par], DIN * 2), R(mo, 1024)],
                        writes=[pr("PSB", yo, yo + 128)], tag="ydg")
                    sch.add("te", lambda e, yo=yo, hh=h0b, SINB=SINB, CSCS=CSCS, bp=bp: e.matmul(
                        PSB[0:64, yo:yo + 128], SINB[:, bp * 128:bp * 128 + HD],
                        CSCS[:, hh * P:(hh + 1) * P], start=False, stop=False,
                        tile_position=(0, 0)),
                        reads=[R(o_sinbP[par], DIN * 2), R(co, 1024)],
                        writes=[pr("PSB", yo, yo + 128)], tag="yof")
                    sch.add("te", lambda e, yo=yo, hh=h1b, XT=XT, M0S=M0S, bp=bp: e.matmul(
                        PSB[64:128, yo:yo + 128], XT[:, bp * 128 + HD:(bp + 1) * 128],
                        M0S[:, hh * P:(hh + 1) * P], start=False, stop=False,
                        tile_position=(0, 64)),
                        reads=[R(o_xtP[par], DIN * 2), R(mo, 1024)],
                        writes=[pr("PSB", yo, yo + 128)], tag="ydg2")
                    sch.add("te", lambda e, yo=yo, hh=h1b, SINB=SINB, CSCS=CSCS, bp=bp: e.matmul(
                        PSB[64:128, yo:yo + 128], SINB[:, bp * 128 + HD:(bp + 1) * 128],
                        CSCS[:, hh * P:(hh + 1) * P], start=False, stop=True,
                        tile_position=(0, 64)),
                        reads=[R(o_sinbP[par], DIN * 2), R(co, 1024)],
                        writes=[pr("PSB", yo, yo + 128)], tag="yof2")
                if tg % 2 == 1:
                    rnd = tg // 2
                    sch.add("ve", lambda e, rnd=rnd: e.tensor_copy(
                        b16(o_yb + rnd * 1024, 512), PSB[:, 0:512]),
                        reads=[pr("PSB", 0, 512)],
                        writes=[R(o_yb + rnd * 1024, 1024)], tag="yev")

            for tg in range(16):
                emit_cse(tg)
                emit_T(tg)
                if tg >= 1:
                    emit_CY(tg - 1)
                if tg == 3 and c < NCH - 1:
                    bwprep_te(c + 1)
                    bwprep_ve(c + 1, 1)
                if 3 <= tg <= 10 and c < NCH - 1:
                    scan_slice(c + 1, tg - 3)
                if tg in (5, 7, 9, 11) and c < NCH - 1:
                    q = (tg - 5) // 2
                    sch.add("se", lambda e, q=q, d=o_sinbP[(c + 1) % 2]: e.copy(
                        b16(d + q * 2048, 1024), f32(o_sin + q * 4096, 1024)),
                        reads=[R(o_sin + q * 4096, 4096)],
                        writes=[R(o_sinbP[(c + 1) % 2] + q * 2048, 2048)], tag="sinbq")
                if tg == 8 and c % 4 == 3:
                    emit_wold(0)
                    emit_wold(1)
            emit_CY(15)
            sch.add("se", lambda e, ZC=ZC: e.activation(ZC, ZC, AF.Silu),
                    reads=[R(o_zc, DIN * 2)], writes=[R(o_zc, DIN * 2)], tag="slz")
            YB = b16(o_yb, DIN)
            if DBG:
                sch.add("sy", lambda e, c=c: e.dma_start(
                    out=dbg_y[:, c * 4096:(c + 1) * 4096], in_=YB),
                    reads=[R(o_yb, DIN * 2)], writes=[("dbg", 1, 2)], dma=True, tag="dbgy")
            sch.add("ve", lambda e, ZC=ZC: e.tensor_mul(YB, YB, ZC),
                    reads=[R(o_yb, DIN * 2), R(o_zc, DIN * 2)], writes=[R(o_yb, DIN * 2)], tag="gy")
            if DBG and c == 0:
                sch.add("sy", lambda e: e.dma_start(out=dbg_g, in_=YB),
                        reads=[R(o_yb, DIN * 2)], writes=[("dbg", 6, 7)], dma=True, tag="dbgg")
            sch.add("se", lambda e: e.activation(b16(o_g2, DIN), YB, AF.Square),
                    reads=[R(o_yb, DIN * 2)], writes=[R(o_g2, DIN * 2)], tag="g2")
            for blk in range(32):
                sch.add("te", lambda e, blk=blk: e.matmul(
                    PSC[0:1, 0:128], ONESC, b16(o_g2 + blk * 256, 128),
                    start=(blk == 0), stop=(blk == 31)),
                    reads=[R(o_g2, DIN * 2), R(o_ones1, 64)],
                    writes=[pr("PSC", 0, 128)], tag="g2mm")
            RSR = f32(os_row2, CH, 0, 1)
            sch.add("se", lambda e: e.activation(RSR, PSC[0:1, 0:128], AF.Ln, bias=CEPS[0:1, :], scale=1.0 / DIN),
                    reads=[pr("PSC", 0, 128), RCEPS], writes=[SM], tag="rs1")
            sch.add("se", lambda e: e.activation(RSR, RSR, AF.Exp, scale=-0.5),
                    reads=[SM], writes=[SM], tag="rs2")
            sch.add("te", lambda e: e.matmul(PSC[:, 128:256], ONESRF, RSR, start=True, stop=True),
                    reads=[SM, R(o_onesrf, 512)], writes=[pr("PSC", 128, 256)], tag="rsb")
            RSBT = b16(os_gtb + 256, CH)
            sch.add("se", lambda e: e.copy(RSBT, PSC[:, 128:256]),
                    reads=[pr("PSC", 128, 256)], writes=[R(os_gtb + 256, 256)], tag="rse")
            gi, cc = c // 4, c % 4
            yn_base = o_yn
            YNv = arf[:, yn_base // 2: yn_base // 2 + 32 * 512].rearrange(
                "p (b t) -> p b t", b=32)[:, :, cc * 128:(cc + 1) * 128]
            o_tg = o_g2
            sch.add("ve", lambda e: e.tensor_tensor(
                b16(o_tg, DIN).rearrange("p (b l) -> p b l", b=32),
                YB.rearrange("p (b l) -> p b l", b=32),
                NWC.unsqueeze(2).broadcast_to([P, 32, 128]), ALU.mult),
                reads=[R(o_yb, DIN * 2), R(o_nwc, 128)], writes=[R(o_g2, DIN * 2)], tag="yn1")
            sch.add("ve", lambda e, YNv=YNv: e.tensor_tensor(
                YNv, b16(o_tg, DIN).rearrange("p (b l) -> p b l", b=32),
                RSBT.unsqueeze(1).broadcast_to([P, 32, 128]), ALU.mult),
                reads=[R(o_g2, DIN * 2), R(os_gtb + 256, 256)],
                writes=[R(yn_base, 32768)], tag="yn2")

            if DBG and c == 3:
                sch.add("sy", lambda e: e.dma_start(out=dbg_yn, in_=b16(o_yn, 32 * 512)),
                        reads=[R(o_yn, 32 * 512 * 2)], writes=[("dbg", 7, 8)], dma=True, tag="dbgyn")
            if cc == 3:
                for mb in range(16):
                    slot = mb % 2
                    WOB = b16(o_wob2[slot], 32 * P)
                    if mb >= 2:
                        emit_wold(mb)
                    for kbi in range(32):
                        sch.add("te", lambda e, kbi=kbi, W=WOB, yb=yn_base: e.matmul(
                            PSB[:, 512:1024], W[:, kbi * P:(kbi + 1) * P],
                            arf[:, yb // 2 + kbi * 512: yb // 2 + (kbi + 1) * 512],
                            start=(kbi == 0), stop=(kbi == 31)),
                            reads=[R(o_wob2[slot], 32 * P * 2), R(yn_base, 32768)],
                            writes=[pr("PSB", 512, 1024)], tag="opmm")
                    XRB = f32(o_xr1[slot], 512)
                    sch.add("sy", lambda e, d=XRB, mb=mb, gi=gi: e.dma_start(
                        out=d, in_=xin[mb * P:(mb + 1) * P, HALO + gi * 512:HALO + (gi + 1) * 512]),
                        reads=[("xin", mb, mb + 1)], writes=[R(o_xr1[slot], 2048)],
                        dma=True, tag="xrld")
                    X1O = f32(o_x1o[slot], 512)
                    sch.add("ve", lambda e, d=X1O, s=XRB: e.tensor_add(d, s, PSB[:, 512:1024]),
                            reads=[R(o_xr1[slot], 2048), pr("PSB", 512, 1024)],
                            writes=[R(o_x1o[slot], 2048)], tag="x1add")
                    sch.add("sy", lambda e, s=X1O, mb=mb, gi=gi: e.dma_start(
                        out=x1_s[mb, :, gi * 512:(gi + 1) * 512], in_=s),
                        reads=[R(o_x1o[slot], 2048)], writes=[("x1_s", mb * 4 + gi * 2, mb * 4 + gi * 2 + 2)],
                        dma=True, tag="x1st")
            if c < NCH - 1:
                parn = (c + 1) % 2
                XTn = b16(o_xtP[parn], DIN)
                sch.add("ve", lambda e, XT=XTn: e.tensor_tensor(
                    b16(o_xtd, DIN).rearrange("p (h q) -> p h q", h=NH),
                    XT.rearrange("p (h q) -> p h q", h=NH),
                    DT64B.unsqueeze(2).broadcast_to([P, NH, HD]), ALU.mult),
                    reads=[R(o_xtP[parn], DIN * 2), R(o_dt64, 128)],
                    writes=[R(o_xtd, DIN * 2)], tag="xtd")

        rg.off = o_xscs

        def ffn(o_xb, o_xres, xres_key):
            rg.push()
            o_wg = [rg.alloc(16 * P * 2) for _ in range(2)]
            o_wu = [rg.alloc(16 * P * 2) for _ in range(2)]
            o_wd = [rg.alloc(16 * P * 2) for _ in range(2)]
            o_hg = rg.alloc(16 * TOK * 2)
            o_sg = rg.alloc(TOK * 2)
            for g in range(4):
                for ibi in range(16):
                    ib = g * 16 + ibi
                    slot = ib % 2
                    WGB = b16(o_wg[slot], 16 * P)
                    WUB = b16(o_wu[slot], 16 * P)
                    sch.add("ge", lambda e, d=WGB, ib=ib: e.dma_start(
                            out=d.rearrange("p (kb m) -> p kb m", kb=16), in_=gw[ib].transpose([1, 0, 2])),
                            reads=[("gw", ib, ib + 1)], writes=[R(o_wg[slot], 16 * P * 2)],
                            dma=True, tag="gld")
                    sch.add("ge", lambda e, d=WUB, ib=ib: e.dma_start(
                            out=d.rearrange("p (kb m) -> p kb m", kb=16), in_=uw[ib].transpose([1, 0, 2])),
                            reads=[("uw", ib, ib + 1)], writes=[R(o_wu[slot], 16 * P * 2)],
                            dma=True, tag="uld")
                    for kbi in range(16):
                        xnb = b16(o_xb + kbi * TOK * 2, TOK)
                        for W, PT, pb, wrng in ((WGB, PSA, "PSA", R(o_wg[slot], 16 * P * 2)),
                                                (WUB, PSB, "PSB", R(o_wu[slot], 16 * P * 2))):
                            for th in range(2):
                                sch.add("te", lambda e, th=th, kbi=kbi, W=W, PT=PT, s=xnb: e.matmul(
                                    PT[:, th * 512:(th + 1) * 512], W[:, kbi * P:(kbi + 1) * P],
                                    s[:, th * 512:(th + 1) * 512],
                                    start=(kbi == 0), stop=(kbi == 15)),
                                    reads=[wrng, R(o_xb + kbi * TOK * 2, TOK * 2)],
                                    writes=[pr(pb, th * 512, (th + 1) * 512)], tag="gmm")
                    SG = b16(o_sg, TOK)
                    HGB = b16(o_hg + ibi * TOK * 2, TOK)
                    for th in range(2):
                        sch.add("se", lambda e, th=th, ib=ib: e.activation(
                            SG[0:P, th * 512:(th + 1) * 512], PSA[:, th * 512:(th + 1) * 512],
                            AF.Silu, scale=GSC[:, ib:ib + 1]),
                            reads=[pr("PSA", th * 512, (th + 1) * 512), R(o_gsc, NIB * 4)],
                            writes=[R(o_sg + th * 1024, 1024)], tag="gsl")
                        sch.add("ve", lambda e, th=th, d=HGB, ib=ib: e.scalar_tensor_tensor(
                            d[0:P, th * 512:(th + 1) * 512], PSB[:, th * 512:(th + 1) * 512],
                            USC[:, ib:ib + 1], SG[0:P, th * 512:(th + 1) * 512],
                            ALU.mult, ALU.mult),
                            reads=[pr("PSB", th * 512, (th + 1) * 512), R(o_usc, NIB * 4),
                                   R(o_sg + th * 1024, 1024)],
                            writes=[R(o_hg + ibi * TOK * 2 + th * 1024, 1024)], tag="hmul")
                for hb in range(16):
                    slot = hb % 2
                    WDB = b16(o_wd[slot], 16 * P)
                    sch.add("ge", lambda e, d=WDB, g=g, hb=hb: e.dma_start(
                            out=d.rearrange("p (kb m) -> p kb m", kb=16), in_=dw[g, hb].transpose([1, 0, 2])),
                            reads=[("dw", g * 16 + hb, g * 16 + hb + 1)],
                            writes=[R(o_wd[slot], 16 * P * 2)], dma=True, tag="dld")
                    for j in range(16):
                        HGB = b16(o_hg + j * TOK * 2, TOK)
                        for th in range(2):
                            def dmm(e, th=th, j=j, W=WDB, s=HGB, g=g):
                                dst = PSA[:, 1024:1536] if th == 0 else PSC[:, 0:512]
                                return e.matmul(dst[:, 0:512], W[:, j * P:(j + 1) * P],
                                                s[:, th * 512:(th + 1) * 512],
                                                start=(j == 0), stop=(j == 15))
                            sch.add("te", dmm,
                                    reads=[R(o_wd[slot], 16 * P * 2), R(o_hg + j * TOK * 2, TOK * 2)],
                                    writes=[pr("PSA", 1024, 1536) if th == 0 else pr("PSC", 0, 512)], tag="dmm")
                    for th in range(2):
                        pb = "PSA" if th == 0 else "PSC"
                        XRB = f32(o_xres + hb * TOK * 4 + th * 512 * 4, 512)
                        def dev(e, th=th, d=XRB, hb=hb):
                            src = PSA[:, 1024:1536] if th == 0 else PSC[:, 0:512]
                            return e.scalar_tensor_tensor(d, src[:, 0:512], DSC[:, hb:hb + 1],
                                                          d, ALU.mult, ALU.add)
                        sch.add("ve", dev,
                                reads=[pr(pb, 1024, 1536) if th == 0 else pr(pb, 0, 512),
                                       R(o_dsc, 64), R(o_xres + hb * TOK * 4 + th * 2048, 2048)],
                                writes=[R(o_xres + hb * TOK * 4 + th * 2048, 2048)], tag="dev")
            rg.pop()

        o_xres = rg.alloc(NDB * TOK * 4)
        o_xb2 = rg.alloc(NDB * TOK * 2)
        for b in range(NDB):
            sch.add("sy", lambda e, b=b: e.dma_start(
                out=f32(o_xres + b * TOK * 4, TOK), in_=x1_s[b]),
                reads=[("x1_s", b * 4, b * 4 + 4)], writes=[R(o_xres + b * TOK * 4, TOK * 4)],
                dma=True, tag="x1ld", dkey="x1ld")

        def ln_resident(o_src, o_dst, key):
            rg.push()
            o_sq = rg.alloc(TOK * 2)
            o_rows = rg.alloc(6 * 4096)
            o_mb = rg.alloc(TOK * 4)
            o_rb = rg.alloc(TOK * 4)
            o_tl = rg.alloc(TOK * 4)
            for b in range(NDB):
                xbb = b16(o_dst + b * TOK * 2, TOK)
                sch.add("ve", lambda e, d=xbb, b=b: e.tensor_copy(d, f32(o_src + b * TOK * 4, TOK)),
                        reads=[R(o_src + b * TOK * 4, TOK * 4)],
                        writes=[R(o_dst + b * TOK * 2, TOK * 2)], tag="l2cp")
                sqb = b16(o_sq, TOK)
                sch.add("se", lambda e, s=xbb: e.activation(sqb, s, AF.Square),
                        reads=[R(o_dst + b * TOK * 2, TOK * 2)], writes=[R(o_sq, TOK * 2)], tag="l2sq")
                for th in range(2):
                    sch.add("te", lambda e, th=th, b=b, s=xbb: e.matmul(
                        PSA[0:1, th * 512:(th + 1) * 512], ONESC, s[:, th * 512:(th + 1) * 512],
                        start=(b == 0), stop=(b == NDB - 1)),
                        reads=[R(o_dst + b * TOK * 2, TOK * 2), R(o_ones1, 64)],
                        writes=[pr("PSA", th * 512, (th + 1) * 512)], tag="l2s1")
                    sch.add("te", lambda e, th=th, b=b: e.matmul(
                        PSB[0:1, th * 512:(th + 1) * 512], ONESC, sqb[:, th * 512:(th + 1) * 512],
                        start=(b == 0), stop=(b == NDB - 1)),
                        reads=[R(o_sq, TOK * 2), R(o_ones1, 64)],
                        writes=[pr("PSB", th * 512, (th + 1) * 512)], tag="l2s2")
            MROW = f32(o_rows, TOK, 0, 1)
            VROW = f32(o_rows + 4096, TOK, 0, 1)
            RROW = f32(o_rows + 8192, TOK, 0, 1)
            ABR = b16(o_rows + 12288, TOK, 0, 1)
            BBR = b16(o_rows + 16384, TOK, 0, 1)
            rows_rng = R(o_rows, 6 * 4096)
            sch.add("ve", lambda e: e.tensor_scalar_mul(MROW, PSA[0:1, 0:TOK], 1.0 / DM),
                    reads=[pr("PSA", 0, 1024)], writes=[rows_rng], tag="l2m")
            sch.add("ve", lambda e: e.tensor_scalar_mul(VROW, PSB[0:1, 0:TOK], 1.0 / DM),
                    reads=[pr("PSB", 0, 1024)], writes=[rows_rng], tag="l2v")
            sch.add("ve", lambda e: e.scalar_tensor_tensor(RROW, MROW, -1.0, MROW, ALU.mult, ALU.mult),
                    reads=[rows_rng], writes=[rows_rng], tag="l2m2")
            sch.add("ve", lambda e: e.tensor_add(VROW, VROW, RROW),
                    reads=[rows_rng], writes=[rows_rng], tag="l2v2")
            sch.add("se", lambda e: e.activation(RROW, VROW, AF.Ln, bias=CEPS[0:1, :]),
                    reads=[rows_rng, RCEPS], writes=[rows_rng], tag="l2ln")
            sch.add("se", lambda e: e.activation(RROW, RROW, AF.Exp, scale=-0.5),
                    reads=[rows_rng], writes=[rows_rng], tag="l2ex")
            sch.add("ve", lambda e: e.tensor_copy(ABR, RROW), reads=[rows_rng], writes=[rows_rng], tag="l2a")
            sch.add("ve", lambda e: e.tensor_copy(BBR, MROW), reads=[rows_rng], writes=[rows_rng], tag="l2b")
            for th in range(2):
                sch.add("te", lambda e, th=th: e.matmul(
                    PSA[:, th * 512:(th + 1) * 512], ONESR, BBR[0:1, th * 512:(th + 1) * 512],
                    start=True, stop=True),
                    reads=[rows_rng, R(o_onesr, 256)],
                    writes=[pr("PSA", th * 512, (th + 1) * 512)], tag="l2b1")
                sch.add("te", lambda e, th=th: e.matmul(
                    PSB[:, th * 512:(th + 1) * 512], ONESR, ABR[0:1, th * 512:(th + 1) * 512],
                    start=True, stop=True),
                    reads=[rows_rng, R(o_onesr, 256)],
                    writes=[pr("PSB", th * 512, (th + 1) * 512)], tag="l2b2")
            MB = f32(o_mb, TOK)
            RB = f32(o_rb, TOK)
            sch.add("ve", lambda e: e.tensor_copy(MB, PSA[:, 0:TOK]),
                    reads=[pr("PSA", 0, 1024)], writes=[R(o_mb, TOK * 4)], tag="l2me")
            sch.add("ve", lambda e: e.tensor_copy(RB, PSB[:, 0:TOK]),
                    reads=[pr("PSB", 0, 1024)], writes=[R(o_rb, TOK * 4)], tag="l2re")
            TL = f32(o_tl, TOK)
            for b in range(NDB):
                xbb = b16(o_dst + b * TOK * 2, TOK)
                sch.add("ve", lambda e, b=b: e.tensor_sub(TL, f32(o_src + b * TOK * 4, TOK), MB),
                        reads=[R(o_src + b * TOK * 4, TOK * 4), R(o_mb, TOK * 4)],
                        writes=[R(o_tl, TOK * 4)], tag="l2n1")
                sch.add("ve", lambda e, d=xbb: e.tensor_mul(d, TL, RB),
                        reads=[R(o_tl, TOK * 4), R(o_rb, TOK * 4)],
                        writes=[R(o_dst + b * TOK * 2, TOK * 2)], tag="l2n2")
            rg.pop()

        ln_resident(o_xres, o_xb2, "x1")
        ffn(o_xb2, o_xres, "x1")

        ln_resident(o_xres, o_xb2, "x2")
        for b in range(NDB):
            sch.add("sy", lambda e, b=b: e.dma_start(
                out=x2_s[b], in_=f32(o_xres + b * TOK * 4, TOK)),
                reads=[R(o_xres + b * TOK * 4, TOK * 4)], writes=[("x2_s", b, b + 1)],
                dma=True, tag="x2st", dkey="x2st")
        o_qp = o_xres
        o_kp = o_xres + NDB * TOK * 2
        rg.push()
        o_v = rg.alloc(NDB * TOK * 2)
        o_att = o_v
        o_wq = [rg.alloc(16 * P * 2) for _ in range(2)]
        o_kts = rg.alloc(TOK * 2)
        o_vts = rg.alloc(TOK * 2)
        o_kv = rg.alloc(AH * P * 4)
        o_ks = rg.alloc(AH * 4)
        o_kvb = rg.alloc(AH * P * 2)
        o_ksb = rg.alloc(AH * 2)
        o_zr = rg.alloc(TOK * 4)
        o_zib = rg.alloc(TOK * 4)
        o_tm = rg.alloc(512 * 4)
        o_x2b = [rg.alloc(512 * 4) for _ in range(2)]

        def qkv(wsrc, wkey, bias_off, o_dst, elu):
            for mb in range(16):
                slot = mb % 2
                WQB = b16(o_wq[slot], 16 * P)
                sch.add("ge", lambda e, d=WQB, mb=mb: e.dma_start(
                        out=d.rearrange("p (kb m) -> p kb m", kb=16), in_=wsrc[mb].transpose([1, 0, 2])),
                        reads=[(wkey, mb, mb + 1)], writes=[R(o_wq[slot], 16 * P * 2)],
                        dma=True, tag="qld")
                for kbi in range(16):
                    xnb = b16(o_xb2 + kbi * TOK * 2, TOK)
                    for th in range(2):
                        PT = PSA if th == 0 else PSB
                        pb = "PSA" if th == 0 else "PSB"
                        sch.add("te", lambda e, PT=PT, th=th, kbi=kbi, W=WQB, s=xnb: e.matmul(
                            PT[:, 0:512], W[:, kbi * P:(kbi + 1) * P],
                            s[:, th * 512:(th + 1) * 512],
                            start=(kbi == 0), stop=(kbi == 15)),
                            reads=[R(o_wq[slot], 16 * P * 2), R(o_xb2 + kbi * TOK * 2, TOK * 2)],
                            writes=[pr(pb, 0, 512)], tag="qmm")
                for th in range(2):
                    PT = PSA if th == 0 else PSB
                    pb = "PSA" if th == 0 else "PSB"
                    dst = b16(o_dst + mb * TOK * 2 + th * 512 * 2, 512)
                    bc = f32(bias_off, 16)
                    if elu:
                        TM = f32(o_tm, 512)
                        sch.add("ve", lambda e, PT=PT, mb=mb: e.tensor_scalar(
                            TM, PT[:, 0:512], bc[:, mb:mb + 1], 0.0, ALU.add, ALU.min),
                            reads=[pr(pb, 0, 512), R(bias_off, 64)], writes=[R(o_tm, 2048)], tag="el1")
                        sch.add("se", lambda e: e.activation(TM, TM, AF.Exp),
                                reads=[R(o_tm, 2048)], writes=[R(o_tm, 2048)], tag="el2")
                        sch.add("ve", lambda e, d=dst, PT=PT, mb=mb: e.tensor_scalar(
                            d, PT[:, 0:512], bc[:, mb:mb + 1], 0.0, ALU.add, ALU.max),
                            reads=[pr(pb, 0, 512), R(bias_off, 64)],
                            writes=[R(o_dst + mb * TOK * 2 + th * 1024, 1024)], tag="el3")
                        sch.add("ve", lambda e, d=dst: e.tensor_add(d, d, TM),
                                reads=[R(o_dst + mb * TOK * 2 + th * 1024, 1024), R(o_tm, 2048)],
                                writes=[R(o_dst + mb * TOK * 2 + th * 1024, 1024)], tag="el4")
                    else:
                        sch.add("ve", lambda e, d=dst, PT=PT, mb=mb: e.tensor_scalar_add(
                            d, PT[:, 0:512], bc[:, mb:mb + 1]),
                            reads=[pr(pb, 0, 512), R(bias_off, 64)],
                            writes=[R(o_dst + mb * TOK * 2 + th * 1024, 1024)], tag="vev")

        qkv(kw, "kw", o_kb, o_kp, elu=True)
        qkv(vw, "vw", o_vb, o_v, elu=False)

        for h in range(AH):
            for j in range(8):
                sch.add("te", lambda e, h=h, j=j: e.transpose(
                    PTA[:, j * P:(j + 1) * P], b16(o_kp + h * TOK * 2 + j * P * 2, P), ID128b),
                    reads=[R(o_kp + h * TOK * 2, TOK * 2), R(o_id128b, 256)],
                    writes=[pr("PTA", j * P, (j + 1) * P)], tag="ktt")
                sch.add("te", lambda e, h=h, j=j: e.transpose(
                    PTB[:, j * P:(j + 1) * P], b16(o_v + h * TOK * 2 + j * P * 2, P), ID128b),
                    reads=[R(o_v + h * TOK * 2, TOK * 2), R(o_id128b, 256)],
                    writes=[pr("PTB", j * P, (j + 1) * P)], tag="vtt")
            KTS = b16(o_kts, TOK)
            VTS = b16(o_vts, TOK)
            sch.add("ve", lambda e: e.tensor_copy(KTS, PTA[:, 0:1024]),
                    reads=[pr("PTA", 0, 1024)], writes=[R(o_kts, TOK * 2)], tag="kte")
            sch.add("ve", lambda e: e.tensor_copy(VTS, PTB[:, 0:1024]),
                    reads=[pr("PTB", 0, 1024)], writes=[R(o_vts, TOK * 2)], tag="vte")
            for j in range(8):
                sch.add("te", lambda e, j=j: e.matmul(
                    PSC[:, 0:128], KTS[:, j * P:(j + 1) * P], VTS[:, j * P:(j + 1) * P],
                    start=(j == 0), stop=(j == 7)),
                    reads=[R(o_kts, TOK * 2), R(o_vts, TOK * 2)],
                    writes=[pr("PSC", 0, 128)], tag="kvmm")
            sch.add("ve", lambda e, h=h: e.tensor_copy(f32(o_kv + h * P * 4, P), PSC[:, 0:128]),
                    reads=[pr("PSC", 0, 128)], writes=[R(o_kv + h * P * 4, P * 4)], tag="kve")
            sch.add("ve", lambda e, h=h: e.reduce_sum(
                f32(o_ks + h * 4, 1), b16(o_kp + h * TOK * 2, TOK), axis=AX.X),
                reads=[R(o_kp + h * TOK * 2, TOK * 2)], writes=[R(o_ks + h * 4, 4)], tag="ksum")
        sch.add("sy", lambda e: e.dma_start(out=ar_in[:, 0:2048], in_=f32(o_kv, AH * P)),
                reads=[R(o_kv, AH * P * 4)], writes=[("ar_in", 0, 1)], dma=True, tag="arw1")
        sch.add("sy", lambda e: e.dma_start(out=ar_in[:, 2048:2064], in_=f32(o_ks, AH)),
                reads=[R(o_ks, AH * 4)], writes=[("ar_in", 1, 2)], dma=True, tag="arw2")
        sch.add("ge", lambda e: e.collective_compute(
            "AllReduce", ALU.add, RG, [ar_in.opt()], [ar_out.opt()]),
            reads=[("ar_in", 0, 2)], writes=[("ar_out", 0, 2)], tag="ar")
        qkv(qw, "qw", o_qb, o_qp, elu=True)
        sch.add("sy", lambda e: e.dma_start(out=f32(o_kv, AH * P), in_=ar_out[:, 0:2048]),
                reads=[("ar_out", 0, 2)], writes=[R(o_kv, AH * P * 4)], dma=True, tag="arr1")
        sch.add("sy", lambda e: e.dma_start(out=f32(o_ks, AH), in_=ar_out[:, 2048:2064]),
                reads=[("ar_out", 0, 2)], writes=[R(o_ks, AH * 4)], dma=True, tag="arr2")
        sch.add("se", lambda e: e.copy(b16(o_kvb, AH * P), f32(o_kv, AH * P)),
                reads=[R(o_kv, AH * P * 4)], writes=[R(o_kvb, AH * P * 2)], tag="kvb")
        sch.add("ve", lambda e: e.tensor_copy(b16(o_ksb, AH), f32(o_ks, AH)),
                reads=[R(o_ks, AH * 4)], writes=[R(o_ksb, AH * 2)], tag="ksb")

        for h in range(AH):
            for th in range(2):
                sch.add("te", lambda e, h=h, th=th: e.matmul(
                    PSA[0:1, th * 512:(th + 1) * 512], b16(o_ksb + h * 2, 1),
                    b16(o_qp + h * TOK * 2 + th * 1024, 512), start=True, stop=True),
                    reads=[R(o_ksb, AH * 2), R(o_qp + h * TOK * 2, TOK * 2)],
                    writes=[pr("PSA", th * 512, (th + 1) * 512)], tag="zmm")
            ZR = f32(o_zr, TOK, 0, 1)
            sch.add("ve", lambda e: e.tensor_scalar_add(ZR, PSA[0:1, 0:TOK], 1e-6),
                    reads=[pr("PSA", 0, 1024)], writes=[R(o_zr, TOK * 4)], tag="zeps")
            sch.add("se", lambda e: e.activation(ZR, ZR, AF.Ln),
                    reads=[R(o_zr, TOK * 4)], writes=[R(o_zr, TOK * 4)], tag="zln")
            sch.add("se", lambda e: e.activation(ZR, ZR, AF.Exp, scale=-1.0),
                    reads=[R(o_zr, TOK * 4)], writes=[R(o_zr, TOK * 4)], tag="zrec")
            ZIBS = f32(o_zib, TOK)
            for th in range(2):
                sch.add("te", lambda e, th=th: e.matmul(
                    PSB[:, th * 512:(th + 1) * 512], ONESRF,
                    ZR[0:1, th * 512:(th + 1) * 512], start=True, stop=True),
                    reads=[R(o_zr, TOK * 4), R(o_onesrf, 512)],
                    writes=[pr("PSB", th * 512, (th + 1) * 512)], tag="zbc")
                sch.add("te", lambda e, h=h, th=th: e.matmul(
                    PSA[:, th * 512:(th + 1) * 512], b16(o_kvb + h * P * 2, P),
                    b16(o_qp + h * TOK * 2 + th * 1024, 512), start=True, stop=True),
                    reads=[R(o_kvb, AH * P * 2), R(o_qp + h * TOK * 2, TOK * 2)],
                    writes=[pr("PSA", th * 512, (th + 1) * 512)], tag="atmm")
            sch.add("se", lambda e: e.copy(ZIBS, PSB[:, 0:1024]),
                    reads=[pr("PSB", 0, 1024)], writes=[R(o_zib, TOK * 4)], tag="zev")
            sch.add("ve", lambda e, h=h: e.tensor_mul(
                b16(o_att + h * TOK * 2, TOK), PSA[:, 0:1024], ZIBS),
                reads=[pr("PSA", 0, 1024), R(o_zib, TOK * 4)],
                writes=[R(o_att + h * TOK * 2, TOK * 2)], tag="atev")

        for mb in range(16):
            slot = mb % 2
            WQB = b16(o_wq[slot], 16 * P)
            sch.add("ge", lambda e, d=WQB, mb=mb: e.dma_start(
                    out=d.rearrange("p (kb m) -> p kb m", kb=16), in_=ow[mb].transpose([1, 0, 2])),
                    reads=[("ow", mb, mb + 1)], writes=[R(o_wq[slot], 16 * P * 2)],
                    dma=True, tag="old")
            for kbi in range(16):
                for th in range(2):
                    sch.add("te", lambda e, th=th, kbi=kbi, W=WQB: e.matmul(
                        PSB[:, th * 512:(th + 1) * 512], W[:, kbi * P:(kbi + 1) * P],
                        b16(o_att + kbi * TOK * 2 + th * 1024, 512),
                        start=(kbi == 0), stop=(kbi == 15)),
                        reads=[R(o_wq[slot], 16 * P * 2), R(o_att + kbi * TOK * 2, TOK * 2)],
                        writes=[pr("PSB", th * 512, (th + 1) * 512)], tag="omm")
            for th in range(2):
                pb = "PSB"
                X2B = f32(o_x2b[slot], 512)
                sch.add("sy", lambda e, d=X2B, mb=mb, th=th: e.dma_start(
                    out=d, in_=x2_s[mb, :, th * 512:(th + 1) * 512]),
                    reads=[("x2_s", mb, mb + 1)], writes=[R(o_x2b[slot], 2048)],
                    dma=True, tag="x2ld")
                dst = f32(o_xres + mb * TOK * 4 + th * 512 * 4, 512)
                sch.add("ve", lambda e, d=dst, s=X2B, mb=mb, th=th: e.scalar_tensor_tensor(
                    d, PSB[:, th * 512:(th + 1) * 512], OBc[:, mb:mb + 1], s, ALU.add, ALU.add),
                    reads=[pr(pb, th * 512, (th + 1) * 512), R(o_obi, 64), R(o_x2b[slot], 2048)],
                    writes=[R(o_xres + mb * TOK * 4 + th * 2048, 2048)], tag="x3")
        rg.pop()

        ln_resident(o_xres, o_xb2, "x3")
        ffn(o_xb2, o_xres, "x3")
        for b in range(NDB):
            sch.add("sy", lambda e, b=b: e.dma_start(
                out=out[b * P:(b + 1) * P, :], in_=f32(o_xres + b * TOK * 4, TOK)),
                reads=[R(o_xres + b * TOK * 4, TOK * 4)], writes=[("out", b, b + 1)],
                dma=True, tag="outst", dkey="outst")

        n_waits = sch.emit(nc, block)
        print(f"[build] ops={len(sch.ops)} waits={n_waits} sbuf_peak={rg.peak}")
    return nc


_NC_CACHE = {}


def kernel(**inputs):
    f = np.float32
    x = np.asarray(inputs["L0_attn_a_hidden_states"], f)
    Win = np.asarray(inputs["L0_attn_a_W_in"], f)
    cw = np.asarray(inputs["L0_attn_a_conv_w"], f)
    cb = np.asarray(inputs["L0_attn_a_conv_b"], f)
    A = np.asarray(inputs["L0_attn_a_A"], f)
    D = np.asarray(inputs["L0_attn_a_D"], f)
    dtb = np.asarray(inputs["L0_attn_a_dt_bias"], f)
    nw = np.asarray(inputs["L0_attn_a_norm_w"], f)
    Wout = np.asarray(inputs["L0_attn_a_W_out"], f)
    qwv = np.asarray(inputs["L0_attn_b_q_w"], f)
    qbv = np.asarray(inputs["L0_attn_b_q_b"], f)
    kwv = np.asarray(inputs["L0_attn_b_k_w"], f)
    kbv = np.asarray(inputs["L0_attn_b_k_b"], f)
    vwv = np.asarray(inputs["L0_attn_b_v_w"], f)
    vbv = np.asarray(inputs["L0_attn_b_v_b"], f)
    owv = np.asarray(inputs["L0_attn_b_o_w"], f)
    obv = np.asarray(inputs["L0_attn_b_o_b"], f)
    gwq = np.asarray(inputs["L0_ffn_gate_w"])
    gsc = np.asarray(inputs["L0_ffn_gate_scale"], f)
    uwq = np.asarray(inputs["L0_ffn_up_w"])
    usc = np.asarray(inputs["L0_ffn_up_scale"], f)
    dwq = np.asarray(inputs["L0_ffn_down_w"])
    dsc = np.asarray(inputs["L0_ffn_down_scale"], f)

    C = np.ascontiguousarray

    Winp = np.zeros((DM, DIPP), f)
    Winp[:, :DIP] = Win
    win_t = C(Winp.reshape(16, 128, 67, 128).transpose(2, 0, 1, 3))
    wout_t = C(Wout.reshape(32, 128, 16, 128).transpose(2, 0, 1, 3))

    def qk_tiles(w):
        return C(w.T.reshape(16, 128, 16, 128).transpose(2, 0, 1, 3))

    qw_t, kw_t, vw_t, ow_t = (qk_tiles(w) for w in (qwv, kwv, vwv, owv))
    gw_t = C((gwq.astype(np.uint8) ^ 128).view(np.int8).T.reshape(16, 128, 64, 128).transpose(2, 0, 1, 3))
    uw_t = C((uwq.astype(np.uint8) ^ 128).view(np.int8).T.reshape(16, 128, 64, 128).transpose(2, 0, 1, 3))
    dw_t = C((dwq.astype(np.uint8) ^ 128).view(np.int8).T.reshape(4, 16, 128, 16, 128).transpose(0, 3, 1, 2, 4))

    convw_t = C(cw.reshape(NCB, 128, DCONV).transpose(1, 0, 2).reshape(128, NCB * DCONV))
    convb_t = C(cb.reshape(NCB, 128).T)
    hp = np.arange(128)
    dcol_t = C(D[2 * np.arange(32)[None, :] + (hp[:, None] // 64)])
    nwc_t = C(nw.reshape(32, 128).T)
    gsc_t = C(gsc.reshape(64, 128).T)
    usc_t = C(usc.reshape(64, 128).T)
    dsc_t = C(dsc.reshape(16, 128).T)
    qb_t = C(qbv.reshape(16, 128).T); kb_t = C(kbv.reshape(16, 128).T)
    vb_t = C(vbv.reshape(16, 128).T); ob_t = C(obv.reshape(16, 128).T)

    shared = dict(
        win=win_t, wout=wout_t, qw=qw_t, kw=kw_t, vw=vw_t, ow=ow_t,
        gw=gw_t, uw=uw_t, dw=dw_t,
        convw=convw_t, convb=convb_t,
        avec=C(A.reshape(NH, 1)), dtbv=C(dtb.reshape(NH, 1)),
        dcol=dcol_t, drow=C(D.reshape(1, 64)), nwc=nwc_t, gsc=gsc_t, usc=usc_t, dsc=dsc_t,
        qb=qb_t, kb=kb_t, vb=vb_t, ob=ob_t,
    )

    in_maps = []
    for core in range(8):
        b, q = core // 4, core % 4
        t0 = q * TOK
        xs = np.zeros((COLS, DM), f)
        lo = max(t0 - HALO, 0)
        xs[HALO - (t0 - lo):] = x[b, lo:t0 + TOK]
        m = dict(shared)
        m["xin"] = C(xs.T)
        mm = np.zeros((4, 4), f)
        cm = np.zeros((4, 1), f)
        for i in range(4):
            cm[i, 0] = 1.0 if i < q else 0.0
            for k in range(4):
                mm[k, i] = 1.0 if (i < k < q) else 0.0
        m["mmask"] = mm
        m["cmask"] = cm
        in_maps.append(m)

    if "nc" not in _NC_CACHE:
        _NC_CACHE["nc"] = build_program()
    nc = _NC_CACHE["nc"]
    res = run_bass_kernel_spmd(nc, in_maps, core_ids=list(range(8)))

    outp = np.empty((2, 4096, DM), f)
    for core in range(8):
        b, q = core // 4, core % 4
        outp[b, q * TOK:(q + 1) * TOK] = res.results[core]["out"].T
    if DBG:
        kernel.dbg = res.results
    return outp
